# version 07
# baseline (speedup 1.0000x reference)
import numpy as np

import concourse.bass as bass
import concourse.mybir as mybir
from concourse.bass_utils import run_bass_kernel_spmd

F32 = mybir.dt.float32
AF = mybir.ActivationFunctionType
ALU = mybir.AluOpType
AX = mybir.AxisListType

C = 1024
T = 2048
QL = 512
KVL = 512
NH = 16
HS = 64
NG = 4
DC = C // 128
LC = KVL // 128
MC = QL // 128
NQT = T // 256
NQ5 = T // 512
NTK = T // 128
LN2 = float(np.log(2.0))
NEG = -1.0e30
KMIN, KMAX = -14, 5

DEBUG_TAPS = []


class Prog:

    ENGS = ("t", "v", "s", "g", "y")

    def __init__(self):
        self.ops = []
        self.cnt = {e: 0 for e in self.ENGS}
        self.dcnt = {"ld": 0, "st": 0}
        self._lastwait = {e: {} for e in self.ENGS}

    def _wait(self, eng, key, val):
        if val <= 0:
            return
        lw = self._lastwait[eng]
        if lw.get(key, -1) >= val:
            return
        lw[key] = val
        self.ops.append((eng, lambda h, S, k=key, v=val: h.wait_ge(S[k], v)))

    def _chk(self, key, val):
        lim = self.cnt[key] if key in self.cnt else self.dcnt[key] * 16
        assert val <= lim, f"wait on future count {key}={val} (now {lim})"

    def op(self, eng, build, inc=False, wait=()):
        for (k, v) in wait:
            self._chk(k, v)
            self._wait(eng, k, v)
        if inc:
            self.cnt[eng] += 1

            def fn(h, S, b=build, e=eng):
                b(h, S).then_inc(S[e], 1)
            self.ops.append((eng, fn))
            return (eng, self.cnt[eng])
        self.ops.append((eng, lambda h, S, b=build: b(h, S)))
        return None

    def drain(self, eng):
        self.ops.append((eng, lambda h, S: h.drain()))

    def load(self, out_ap, in_ap, wait=()):
        for (k, v) in wait:
            self._chk(k, v)
            self._wait("y", k, v)
        self.dcnt["ld"] += 1
        self.ops.append(("y", lambda h, S, o=out_ap, i=in_ap:
                         h.dma_start(out=o, in_=i).then_inc(S["ld"], 16)))
        return ("ld", self.dcnt["ld"] * 16)

    def store(self, out_ap, in_ap, wait=()):
        for (k, v) in wait:
            self._chk(k, v)
            self._wait("g", k, v)
        self.dcnt["st"] += 1
        self.ops.append(("g", lambda h, S, o=out_ap, i=in_ap:
                         h.dma_start(out=o, in_=i).then_inc(S["st"], 16)))
        return ("st", self.dcnt["st"] * 16)


def build(nc: bass.Bass, taps=()):
    d_xf = nc.dram_tensor("xf", [C, T], F32, kind="ExternalInput").ap()
    d_wdqT = nc.dram_tensor("wdqT", [C, QL], F32, kind="ExternalInput").ap()
    d_wuqT = nc.dram_tensor("wuqT", [QL, C], F32, kind="ExternalInput").ap()
    d_wdkvT = nc.dram_tensor("wdkvT", [C, KVL], F32, kind="ExternalInput").ap()
    d_wuq = nc.dram_tensor("wuq", [C, QL], F32, kind="ExternalInput").ap()
    d_wuk = nc.dram_tensor("wuk", [C, KVL], F32, kind="ExternalInput").ap()
    d_wuv = nc.dram_tensor("wuv", [C, KVL], F32, kind="ExternalInput").ap()
    d_wdq = nc.dram_tensor("wdq", [QL, C], F32, kind="ExternalInput").ap()
    d_woT = nc.dram_tensor("woT", [C, C], F32, kind="ExternalInput").ap()
    d_wffn = nc.dram_tensor("wffn", [C, C], F32, kind="ExternalInput").ap()
    d_biasr = nc.dram_tensor("biasr", [128, DC], F32, kind="ExternalInput").ap()
    d_dw = nc.dram_tensor("dw", [C, 32], F32, kind="ExternalInput").ap()
    d_da = nc.dram_tensor("da", [1, 1], F32, kind="ExternalInput").ap()
    d_out = nc.dram_tensor("outf", [C, T], F32, kind="ExternalOutput").ap()
    d_y = nc.dram_tensor("ydram", [C, T], F32, kind="Internal").ap()

    tap_aps = {}
    for name in taps:
        pass

    arA = nc.alloc_sbuf_tensor("arA", [128, 16448], F32)
    arB = nc.alloc_sbuf_tensor("arB", [128, 16384], F32)
    arC = nc.alloc_sbuf_tensor("arC", [128, 8192], F32)
    arD = nc.alloc_sbuf_tensor("arD", [128, 8192], F32)

    def views(ar, start, n, w):
        return [ar[:, start + i * w: start + (i + 1) * w] for i in range(n)]

    vx = views(arA, 0, DC, T)
    vckv = views(arB, 0, LC, T)
    vqmid = views(arB, LC * T, MC, T)
    vkT = views(arC, 0, LC, C)
    vve = views(arC, LC * C, LC, C)
    woT_v = views(arB, 0, DC, C)
    vB = views(arB, DC * C, MC, KVL)
    wuq_v = views(arD, 0, DC, QL)
    wuk_v = views(arD, DC * QL, DC, KVL)
    wdq_v = views(arD, 0, MC, C)
    wuv_v = views(arD, MC * C, DC, KVL)
    wdkvT_v = views(arD, 0, DC, KVL)
    wdqT_v = views(arD, DC * KVL, DC, QL)

    vK = views(arA, 0, 2, T)
    vV = views(arA, 2 * T, NTK, 260)
    vV3 = arA[:, 2 * T: 2 * T + NTK * 260].rearrange("p (t w) -> p t w", w=260)
    vqg = views(arA, 2 * T + NTK * 260, 2, T)
    vwuqT = views(arA, 4 * T + NTK * 260, MC, C)
    vvaug = views(arD, 0, LC, 260)
    s_pt = arD[:, 1040:2064]
    s_yraw = arD[:, 2064:2576]
    s_yst = arD[:, 2576:3088]
    s_rrow = arD[:, 3088:3344]
    s_mka = arD[:, 3344:3600]
    s_mkb = arD[:, 3600:3856]

    vu = views(arB, 0, DC, T)
    vxq2 = views(arA, 0, DC, 1024)
    vfab = [views(arA, DC * 1024, DC, 512),
            views(arA, DC * 1024 + DC * 512, DC, 512)]
    vwqT = views(arC, 0, DC, C)
    vxs = views(arD, 0, DC, 512)
    vys = views(arD, DC * 512, DC, 512)
    qw_src = views(arD, 0, 4, 1024)
    qw_h2f = views(arD, 4096, DC, 512)
    CKB = 4096
    ck_sl = [arD[:, CKB + i * 128: CKB + (i + 1) * 128] for i in range(21)]
    tmp1 = arD[:, 4096:5120]
    tmp2 = arD[:, 5120:6144]
    tmp3 = arD[:, 6144:7168]
    tmp4 = arD[:, 7168:8192]
    vf = views(arD, 0, DC, 512)
    vo = views(arD, DC * 512, DC, 512)

    s_ones = nc.alloc_sbuf_tensor("s_ones", [128, 1], F32)
    s_onesr = nc.alloc_sbuf_tensor("s_onesr", [1, 128], F32)
    s_id2 = nc.alloc_sbuf_tensor("s_id2", [128, 256], F32)
    s_e64 = nc.alloc_sbuf_tensor("s_e64", [128, 64], F32)
    s_lnt = nc.alloc_sbuf_tensor("s_lnt", [128, 512], F32)
    s_lnq = nc.alloc_sbuf_tensor("s_lnq", [128, 512], F32)
    s_lns = nc.alloc_sbuf_tensor("s_lns", [128, 512], F32)
    s_rows = nc.alloc_sbuf_tensor("s_rows", [1, 1024], F32)
    s_eps = nc.alloc_sbuf_tensor("s_eps", [128, 1], F32)
    s_am = nc.alloc_sbuf_tensor("s_am", [128, 128], F32)
    s_sc = nc.alloc_sbuf_tensor("s_sc", [128, 128], F32)
    s_rsc = nc.alloc_sbuf_tensor("s_rsc", [128, 128], F32)
    s_bias = nc.alloc_sbuf_tensor("s_bias", [128, DC], F32)
    s_tdw = nc.alloc_sbuf_tensor("s_tdw", [128, DC * 32], F32)
    s_rdw = nc.alloc_sbuf_tensor("s_rdw", [128, DC * 32], F32)
    s_da = nc.alloc_sbuf_tensor("s_da", [1, 2], F32)
    s_tda = nc.alloc_sbuf_tensor("s_tda", [128, 2], F32)

    PQ = [nc.alloc_psum_tensor(f"PQ{i}", [128, 512], F32) for i in range(4)]
    pY = [nc.alloc_psum_tensor("pY0", [128, 512], F32),
          nc.alloc_psum_tensor("pY1", [128, 512], F32)]
    pG = [PQ[0], PQ[1]]
    pB = nc.alloc_psum_tensor("pB", [64, 256], F32)
    pH = nc.alloc_psum_tensor("pH", [128, 512], F32)

    p = Prog()
    free = {}

    def fw(res):
        v = free.get(res)
        return list(v) if v else []

    def mark(res, mk):
        if mk is None:
            return
        free[res] = list(mk) if isinstance(mk, list) else [mk]

    def mm(out, lhsT, rhs, start, stop, inc=False, wait=()):
        return p.op("t", lambda h, S, o=out, l=lhsT, r=rhs, a=start, b=stop:
                    h.matmul(o, l, r, start=a, stop=b), inc=inc, wait=wait)

    def vcp(out, in_, inc=False, wait=()):
        return p.op("v", lambda h, S, o=out, i=in_: h.tensor_copy(o, i),
                    inc=inc, wait=wait)

    def vtt(out, a, b, alu, inc=False, wait=()):
        return p.op("v", lambda h, S, o=out, x=a, yy=b, u=alu:
                    h.tensor_tensor(out=o, in0=x, in1=yy, op=u), inc=inc, wait=wait)

    def vts(out, in_, s1, s2, op0, op1=None, inc=False, wait=()):
        if s2 is None:
            return p.op("v", lambda h, S, o=out, i=in_, a=s1, u0=op0:
                        h.tensor_scalar(out=o, in0=i, scalar1=a, scalar2=None,
                                        op0=u0), inc=inc, wait=wait)
        return p.op("v", lambda h, S, o=out, i=in_, a=s1, b=s2, u0=op0, u1=op1:
                    h.tensor_scalar(out=o, in0=i, scalar1=a, scalar2=b,
                                    op0=u0, op1=u1), inc=inc, wait=wait)

    def vred_absmax(out, in_, inc=False, wait=()):
        return p.op("v", lambda h, S, o=out, i=in_:
                    h.tensor_reduce(o, i, axis=AX.X, op=ALU.max,
                                    apply_absolute_value=True), inc=inc, wait=wait)

    def vrecip(out, in_, inc=False, wait=()):
        return p.op("v", lambda h, S, o=out, i=in_: h.reciprocal(o, i),
                    inc=inc, wait=wait)

    def vmemset(ap, val, inc=False, wait=()):
        return p.op("v", lambda h, S, a=ap, x=val: h.memset(a, x),
                    inc=inc, wait=wait)

    def sact(out, in_, func, bias=0.0, scale=1.0, inc=False, wait=()):
        return p.op("s", lambda h, S, o=out, i=in_, f=func, b=bias, sc=scale:
                    h.activation(out=o, in_=i, func=f, bias=b, scale=sc),
                    inc=inc, wait=wait)

    def gp(build, inc=False, wait=()):
        return p.op("g", build, inc=inc, wait=wait)

    def gtt(out, a, b, alu, inc=False, wait=()):
        return p.op("g", lambda h, S, o=out, x=a, yy=b, u=alu:
                    h.tensor_tensor(out=o, in0=x, in1=yy, op=u), inc=inc, wait=wait)

    def scp(out, in_, inc=False, wait=()):
        return p.op("s", lambda h, S, o=out, i=in_: h.copy(o, i),
                    inc=inc, wait=wait)

    def tap(name, src_ap, rows, cols, wait=()):
        if name not in taps:
            return
        dt = nc.dram_tensor(f"tap_{name}", [rows, cols], F32,
                            kind="ExternalOutput").ap()
        tap_aps[name] = dt
        p.store(dt, src_ap, wait=wait)

    gp(lambda h, S: h.memset(s_ones[:], 1.0))
    gp(lambda h, S: h.memset(s_onesr[:], 1.0))
    gp(lambda h, S: h.memset(s_eps[:], 1e-5))
    gp(lambda h, S: h.memset(s_id2[:], 0.0))
    gp(lambda h, S: h.affine_select(
        out=s_id2[:].rearrange("p (a b) -> p a b", b=128),
        in_=s_id2[:].rearrange("p (a b) -> p a b", b=128),
        compare_op=ALU.not_equal, fill=1.0, base=0,
        pattern=[[0, 2], [-1, 128]], channel_multiplier=1))
    gp(lambda h, S: h.memset(s_e64[:], 1.0))
    g_setup = gp(lambda h, S: h.affine_select(
        out=s_e64[:], in_=s_e64[:], compare_op=ALU.is_equal, fill=0.0,
        base=-64, pattern=[[0, 64]], channel_multiplier=1), inc=True)

    ld_x = None
    for c_ in range(DC):
        ld_x = p.load(vx[c_], d_xf[c_ * 128:(c_ + 1) * 128, :])
    ld_wuq = p.load(arD[:, 0:DC * QL].rearrange("p (c m) -> p c m", m=QL), d_wuq.rearrange("(c p) m -> p c m", p=128))
    ld_wuk = p.load(arD[:, DC * QL:2 * DC * QL].rearrange("p (c m) -> p c m", m=KVL),
                    d_wuk.rearrange("(c p) m -> p c m", p=128))
    ld_woT = p.load(arB[:, 0:DC * C].rearrange("p (c m) -> p c m", m=C), d_woT.rearrange("(c p) m -> p c m", p=128))
    p.load(s_bias[:], d_biasr)
    p.load(s_tdw[:].rearrange("p (c m) -> p c m", m=32), d_dw.rearrange("(c p) m -> p c m", p=128))
    ld_small = p.load(s_da[0:1, 0:1], d_da)

    R_SUM, R_SQ = slice(0, 512), slice(512, 1024)

    def ln_tile(src, dst, add=None, src_wait=(), tapname=None):
        gates = []
        vtt(s_lnq[:], src[0], src[0], ALU.mult, wait=list(src_wait))
        for c_ in range(1, DC):
            vtt(s_lnt[:], src[c_], src[c_], ALU.mult)
            vtt(s_lnq[:], s_lnq[:], s_lnt[:], ALU.add)
        vq = p.op("v", lambda h, S: h.tensor_scalar(
            out=s_lnq[:], in0=s_lnq[:], scalar1=1.0, scalar2=None,
            op0=ALU.mult), inc=True)
        vtt(s_lns[:], src[0], src[1], ALU.add)
        for c_ in range(2, DC):
            vt = vtt(s_lns[:], s_lns[:], src[c_], ALU.add, inc=(c_ == DC - 1))
        if tapname is not None and f"{tapname}_lnt" in taps:
            gates.append(p.store(
                nc.dram_tensor(f"tap_{tapname}_lnt", [128, 512], F32,
                               kind="ExternalOutput").ap(), s_lns[:], wait=[vt]))
            gates.append(p.store(
                nc.dram_tensor(f"tap_{tapname}_lnq", [128, 512], F32,
                               kind="ExternalOutput").ap(), s_lnq[:], wait=[vq]))
        m1 = mm(pH[0:1, :], s_ones[:], s_lns[:], True, True, inc=True,
                wait=[vt, g_setup] + fw("pH"))
        m2 = mm(pY[0][0:1, :], s_ones[:], s_lnq[:], True, True, inc=True,
                wait=[vq] + fw("pYL"))
        c1 = vcp(s_rows[0:1, R_SUM], pH[0:1, :], inc=True, wait=[m1])
        c2 = vcp(s_rows[0:1, R_SQ], pY[0][0:1, :], inc=True, wait=[m2])
        p.drain("v")
        vm = vts(s_rows[0:1, R_SUM], s_rows[0:1, R_SUM], 1.0 / C, None, ALU.mult,
                 inc=True)
        p.drain("v")
        vtt(s_lns[0:1, 0:512], s_rows[0:1, R_SUM], s_rows[0:1, R_SUM], ALU.mult)
        vts(s_rows[0:1, R_SQ], s_rows[0:1, R_SQ], 1.0 / C, None, ALU.mult)
        p.drain("v")
        vv = vtt(s_rows[0:1, R_SQ], s_rows[0:1, R_SQ], s_lns[0:1, 0:512],
                 ALU.subtract, inc=True)
        sq = sact(s_rows[0:1, R_SQ], s_rows[0:1, R_SQ], AF.Sqrt,
                  bias=s_eps[0:1, :], inc=True, wait=[vv])
        rr = vrecip(s_rows[0:1, R_SQ], s_rows[0:1, R_SQ], inc=True, wait=[sq])
        if tapname is not None and f"{tapname}_rows" in taps:
            gates.append(p.store(
                nc.dram_tensor(f"tap_{tapname}_rows", [1, 1024], F32,
                               kind="ExternalOutput").ap(), s_rows[0:1, :],
                wait=[rr]))
        b1 = mm(pH[:, :], s_onesr[0:1, 0:128], s_rows[0:1, R_SQ], True, True,
                inc=True, wait=[rr, c1])
        b2 = mm(pY[0][:, :], s_onesr[0:1, 0:128], s_rows[0:1, R_SUM], True, True,
                inc=True, wait=[vm, c2])
        e1 = vcp(s_lnt[:], pH[:, :], inc=True, wait=[b1])
        e2 = vcp(s_lnq[:], pY[0][:, :], inc=True, wait=[b2])
        mark("pH", e1)
        mark("pYL", e2)
        vlast = None
        for c_ in range(DC):
            vtt(dst[c_], src[c_], s_lnq[:], ALU.subtract,
                wait=[e1, e2] if c_ == 0 else [])
            if add is None:
                vlast = vtt(dst[c_], dst[c_], s_lnt[:], ALU.mult,
                            inc=(c_ == DC - 1))
            else:
                vtt(dst[c_], dst[c_], s_lnt[:], ALU.mult)
                vlast = vtt(dst[c_], dst[c_], add[c_], ALU.add,
                            inc=(c_ == DC - 1))
        marks = [vlast]
        if gates:
            marks += gates
        return marks

    def ln_full(srcv, dstv, addv=None, src_wait=(), tapname=None):
        last = None
        for q5 in range(NQ5):
            sl = slice(q5 * 512, (q5 + 1) * 512)
            last = ln_tile([s[:, sl] for s in srcv], [d[:, sl] for d in dstv],
                           add=None if addv is None else [a[:, sl] for a in addv],
                           src_wait=src_wait if q5 == 0 else last,
                           tapname=(tapname if q5 == 0 else
                                    (f"{tapname}s3" if (tapname and q5 == 3) else None)))
        return last

    ln_h1 = ln_full(vx, vx, src_wait=[ld_x], tapname="h1")

    pg_i = [0]

    def proj(out_views, out_w, lhs_tiles, rhs_tiles, nk, first_wait,
             m_range, n_range, lhs_sel, rhs_sel, out_sel):
        done = None
        for mi in m_range:
            for nj in n_range:
                for kk in range(nk):
                    w = []
                    if kk == 0:
                        w = list(fw(f"PQ{pg_i[0]}"))
                        if mi == m_range[0] and nj == n_range[0]:
                            w += list(first_wait)
                    mmn = mm(pG[pg_i[0]][:, 0:out_w], lhs_sel(mi, kk),
                             rhs_sel(nj, kk), kk == 0, kk == nk - 1,
                             inc=(kk == nk - 1), wait=w)
                done = vcp(out_sel(mi, nj), pG[pg_i[0]][:, 0:out_w], inc=True,
                           wait=[mmn])
                mark(f"PQ{pg_i[0]}", done)
                pg_i[0] ^= 1
        return done

    b_done = proj(None, 512, None, None, DC, [ld_wuq, ld_wuk],
                  list(range(MC)), [0],
                  lambda mi, kk: wuq_v[kk][:, mi * 128:(mi + 1) * 128],
                  lambda nj, kk: wuk_v[kk],
                  lambda mi, nj: vB[mi])
    ld_wdq = p.load(arD[:, 0:MC * C].rearrange("p (c m) -> p c m", m=C), d_wdq.rearrange("(c p) m -> p c m", p=128),
                    wait=[b_done])
    kT_done = proj(None, 512, None, None, MC, [ld_wdq],
                   list(range(LC)), list(range(2)),
                   lambda mi, kk: vB[kk][:, mi * 128:(mi + 1) * 128],
                   lambda nj, kk: wdq_v[kk][:, nj * 512:(nj + 1) * 512],
                   lambda mi, nj: vkT[mi][:, nj * 512:(nj + 1) * 512])
    ld_wuv = p.load(arD[:, MC * C:MC * C + DC * KVL].rearrange("p (c m) -> p c m", m=KVL),
                    d_wuv.rearrange("(c p) m -> p c m", p=128), wait=[kT_done])
    ve_done = proj(None, 512, None, None, DC, [ld_wuv, ld_woT],
                   list(range(LC)), list(range(2)),
                   lambda mi, kk: wuv_v[kk][:, mi * 128:(mi + 1) * 128],
                   lambda nj, kk: woT_v[kk][:, nj * 512:(nj + 1) * 512],
                   lambda mi, nj: vve[mi][:, nj * 512:(nj + 1) * 512])
    ld_wdkvT = p.load(arD[:, 0:DC * KVL].rearrange("p (c m) -> p c m", m=KVL),
                      d_wdkvT.rearrange("(c p) m -> p c m", p=128),
                      wait=[kT_done])
    ckv_done = proj(None, 512, None, None, DC, [ld_wdkvT] + ln_h1 + [ve_done],
                    list(range(LC)), list(range(4)),
                    lambda mi, kk: wdkvT_v[kk][:, mi * 128:(mi + 1) * 128],
                    lambda nj, kk: vx[kk][:, nj * 512:(nj + 1) * 512],
                    lambda mi, nj: vckv[mi][:, nj * 512:(nj + 1) * 512])
    ld_wdqT = p.load(arD[:, DC * KVL:2 * DC * KVL].rearrange("p (c m) -> p c m", m=QL),
                     d_wdqT.rearrange("(c p) m -> p c m", p=128), wait=[ve_done])
    qmid_done = proj(None, 512, None, None, DC, [ld_wdqT],
                     list(range(MC)), list(range(4)),
                     lambda mi, kk: wdqT_v[kk][:, mi * 128:(mi + 1) * 128],
                     lambda nj, kk: vx[kk][:, nj * 512:(nj + 1) * 512],
                     lambda mi, nj: vqmid[mi][:, nj * 512:(nj + 1) * 512])
    ld_wuqT = p.load(arA[:, 4 * T + NTK * 260: 4 * T + NTK * 260 + MC * C].rearrange("p (c m) -> p c m", m=C),
                     d_wuqT.rearrange("(c p) m -> p c m", p=128),
                     wait=[ckv_done, qmid_done])

    tap("h1", vx[0], 128, T, wait=ln_h1)
    tap("ckv", vckv[0], 128, T, wait=[ckv_done])
    tap("keffT", vkT[0], 128, C, wait=[kT_done])
    tap("veff", vve[0], 128, C, wait=[ve_done])
    tap("qmid", vqmid[0], 128, T, wait=[qmid_done])

    t1 = sact(s_da[0:1, 1:2], s_da[0:1, 0:1], AF.Tanh, scale=4.0, inc=True,
              wait=[ld_small] + ln_h1)
    p.drain("s")
    t2 = sact(s_rows[0:1, 0:256], s_da[0:1, 1:2].broadcast_to([1, 256]), AF.Exp,
              scale=0.5 * LN2, inc=True)
    p.drain("s")
    t3 = sact(s_rows[0:1, 256:512], s_da[0:1, 1:2].broadcast_to([1, 256]), AF.Exp,
              scale=-0.5 * LN2, inc=True)
    bd1 = mm(pH[:, 0:256], s_onesr[0:1, 0:128], s_rows[0:1, 0:256], True, True,
             inc=True, wait=[t2] + fw("pH"))
    cd1 = vcp(s_tda[:, 0:1], pH[:, 0:1], inc=True, wait=[bd1])
    bd2 = mm(pH[:, 0:256], s_onesr[0:1, 0:128], s_rows[0:1, 256:512], True, True,
             inc=True, wait=[t3, cd1])
    cd2 = vcp(s_tda[:, 1:2], pH[:, 1:2], inc=True, wait=[bd2])
    mark("pH", cd2)

    gp(lambda h, S: h.memset(s_mka[:], 0.0), wait=[qmid_done, ckv_done])
    gp(lambda h, S: h.affine_select(
        out=s_mka[:], in_=s_mka[:], compare_op=ALU.is_ge, fill=NEG,
        base=0, pattern=[[1, 256]], channel_multiplier=-1))
    gp(lambda h, S: h.memset(s_mkb[:], 0.0))
    gp(lambda h, S: h.affine_select(
        out=s_mkb[:], in_=s_mkb[:], compare_op=ALU.is_ge, fill=NEG,
        base=-128, pattern=[[1, 256]], channel_multiplier=-1))
    rrow_init = gp(lambda h, S: h.memset(s_rrow[:], 0.0), inc=True)
    pt_free = [None, None, None, None]
    ps_free = [None, None]
    py_free = [None, None]
    yst_free = [None, None]
    yraw_free = [None, None]
    pb_free = rrow_init
    last_store = None

    for g_ in range(NG):
        kg_done = proj(None, 512, None, None, LC,
                       [kT_done, ckv_done] if g_ == 0 else [],
                       list(range(2)), list(range(4)),
                       lambda mi, kk, g=g_: vkT[kk][:, 256 * g + 128 * mi:
                                                    256 * g + 128 * (mi + 1)],
                       lambda nj, kk: vckv[kk][:, nj * 512:(nj + 1) * 512],
                       lambda mi, nj: vK[mi][:, nj * 512:(nj + 1) * 512])
        for lc_ in range(LC):
            vmemset(vvaug[lc_], 0.0,
                    wait=[qmid_done] if g_ == 0 and lc_ == 0 else [])
        va_done = None
        for lc_ in range(LC):
            for j in range(4):
                va_done = vcp(vvaug[lc_][:, 65 * j:65 * j + 64],
                              vve[lc_][:, 64 * (4 * g_ + j):64 * (4 * g_ + j) + 64],
                              inc=(lc_ == LC - 1 and j == 3),
                              wait=[ve_done] if (g_ == 0 and lc_ == 0 and j == 0)
                              else [])
        vg_done = proj(None, 260, None, None, LC, [va_done],
                       list(range(NTK)), [0],
                       lambda mi, kk: vckv[kk][:, mi * 128:(mi + 1) * 128],
                       lambda nj, kk: vvaug[kk],
                       lambda mi, nj: vV[mi])
        for j in range(4):
            vg_done = vmemset(vV3[:, :, 65 * j + 64:65 * j + 65], 1.0,
                              inc=(j == 3))
        qg_done = proj(None, 512, None, None, MC, [ld_wuqT, qmid_done],
                       list(range(2)), list(range(4)),
                       lambda mi, kk, g=g_: vwuqT[kk][:, 256 * g + 128 * mi:
                                                      256 * g + 128 * (mi + 1)],
                       lambda nj, kk: vqmid[kk][:, nj * 512:(nj + 1) * 512],
                       lambda mi, nj: vqg[mi][:, nj * 512:(nj + 1) * 512])
        if g_ == 0:
            tap("K0", vK[0], 128, T, wait=[kg_done])
            tap("V0", arA[:, 2 * T:2 * T + 2048], 128, 2048, wait=[vg_done])
            tap("q0", vqg[0], 128, T, wait=[qg_done])

        for pr in range(2):
            for qt in range(NQT):
                kcmax = 2 * qt + 1
                ymm = [None, None]

                def emit_S(kc):
                    out = [None, None]
                    for idx in range(2):
                        j = 2 * pr + idx
                        jc, jo = j // 2, 64 * (j % 2)
                        si = (kc % 2) * 2 + idx
                        w = [kg_done, qg_done] + fw(f"PQ{si}")
                        smm = mm(PQ[si][:, 0:256],
                                 vK[jc][jo:jo + 64, kc * 128:(kc + 1) * 128],
                                 vqg[jc][jo:jo + 64, qt * 256:(qt + 1) * 256],
                                 True, True, inc=True, wait=w)
                        out[idx] = smm
                    if kc >= 2 * qt:
                        msk = s_mka if kc == 2 * qt else s_mkb
                        for idx in range(2):
                            si = (kc % 2) * 2 + idx
                            out[idx] = vtt(PQ[si][:, 0:256], PQ[si][:, 0:256],
                                           msk, ALU.add, inc=True,
                                           wait=[out[idx], rrow_init])
                    return out

                sprods = emit_S(0)
                for kc in range(kcmax + 1):
                    exs = [None, None]
                    for idx in range(2):
                        si = (kc % 2) * 2 + idx
                        pi = (kc % 2) * 2 + idx
                        w = [sprods[idx]] + ([pt_free[pi]] if pt_free[pi] else [])
                        exs[idx] = sact(s_pt[:, pi * 256:(pi + 1) * 256],
                                        PQ[si][:, 0:256], AF.Exp, scale=0.125,
                                        inc=True, wait=w)
                        mark(f"PQ{si}", exs[idx])
                    if kc < kcmax:
                        sprods = emit_S(kc + 1)
                    for idx in range(2):
                        j = 2 * pr + idx
                        pi = (kc % 2) * 2 + idx
                        w = [exs[idx], vg_done]
                        if kc == 0 and py_free[idx] is not None:
                            w.append(py_free[idx])
                        ymm[idx] = mm(pY[idx][0:65, 0:256],
                                      vV[kc][:, 65 * j:65 * j + 65],
                                      s_pt[:, pi * 256:(pi + 1) * 256],
                                      kc == 0, kc == kcmax, inc=True, wait=w)
                        pt_free[pi] = ymm[idx]
                for idx in range(2):
                    j = 2 * pr + idx
                    h_ = 4 * g_ + j
                    ys = idx
                    w = [ymm[idx]] + ([yraw_free[ys]] if yraw_free[ys] else [])
                    cpy = vcp(s_yraw[0:65, 256 * ys:256 * (ys + 1)],
                              pY[idx][0:65, 0:256], inc=True, wait=w)
                    rc = vrecip(s_rrow[64:65, :], pY[idx][64:65, 0:256], inc=True,
                                wait=[ymm[idx]] + ([pb_free] if pb_free else []))
                    py_free[idx] = rc
                    bc = mm(pB[0:64, :], s_e64[:, :], s_rrow[:, :], True, True,
                            inc=True, wait=[rc])
                    w = [bc, cpy] + ([yst_free[ys]] if yst_free[ys] else [])
                    yn = vtt(s_yst[0:64, 256 * ys:256 * (ys + 1)],
                             s_yraw[0:64, 256 * ys:256 * (ys + 1)], pB[0:64, :],
                             ALU.mult, inc=True, wait=w)
                    pb_free = yn
                    yraw_free[ys] = yn
                    last_store = p.store(
                        d_y[64 * h_:64 * (h_ + 1), 256 * qt:256 * (qt + 1)],
                        s_yst[0:64, 256 * ys:256 * (ys + 1)], wait=[yn])
                    yst_free[ys] = last_store

    u_done = None
    for q5 in range(NQ5):
        sl = slice(q5 * 512, (q5 + 1) * 512)
        w = [pb_free, last_store] if q5 == 0 else u_done
        lx = ly = None
        for c_ in range(DC):
            lx = p.load(vxs[c_], d_xf[c_ * 128:(c_ + 1) * 128, sl], wait=w)
            ly = p.load(vys[c_], d_y[c_ * 128:(c_ + 1) * 128, sl])
            w = []
        u_done = ln_tile(vys, [u[:, sl] for u in vu], add=vxs,
                         src_wait=[lx, ly])
    tap("u", vu[0], 128, T, wait=u_done)

    dw1 = sact(s_rdw[:], s_tdw[:], AF.Tanh, scale=4.0, inc=True)
    p.drain("s")
    dw2 = sact(s_tdw[:], s_rdw[:], AF.Exp, scale=0.5 * LN2, inc=True)
    p.drain("s")
    dw3 = sact(s_rdw[:], s_rdw[:], AF.Exp, scale=-0.5 * LN2, inc=True)

    def cascade(am_ap, out_ap, base_k, terms, wait0):
        n = len(terms) + 1
        vts(ck_sl[0], am_ap, 0.0, base_k, ALU.is_gt, ALU.mult, wait=wait0)
        for i, (thr, wgt) in enumerate(terms):
            vts(ck_sl[i + 1], am_ap, thr, wgt, ALU.is_ge, ALU.mult)
        p.drain("v")
        width = n
        while width > 1:
            half = width // 2
            for i in range(half):
                vtt(ck_sl[i], ck_sl[i], ck_sl[width - 1 - i], ALU.add)
            width = width - half
            p.drain("v")
        return vts(out_ap, ck_sl[0], 1.0, None, ALU.add, inc=True)

    def quant_batch4(src4, dst4, tw_ap, rw_ap, wait0, krange=(KMIN, KMAX)):
        am4 = vred_absmax(
            s_am[:].rearrange("p (t b) -> p t b", b=32).unsqueeze(3),
            src4.rearrange("p (t b e) -> p t b e", b=32, e=32),
            inc=True, wait=wait0)
        p.drain("v")
        kmin, kmax = krange
        terms = [(2.0 ** k, 2.0 ** (k - 3)) for k in range(kmin + 1, kmax + 1)]
        rterms = [(2.0 ** k, -(2.0 ** (2 - k))) for k in range(kmin + 1, kmax + 1)]
        c1 = cascade(s_am[:], s_sc[:], 2.0 ** (kmin - 2) - 1.0, terms, [am4])
        c2 = cascade(s_am[:], s_rsc[:], 2.0 ** (2 - kmin) - 1.0, rterms, [])
        p.drain("v")
        if tw_ap.shape[1] == 1:
            vts(s_sc[:], s_sc[:], tw_ap, None, ALU.mult, wait=[cd1, cd2])
            vts(s_rsc[:], s_rsc[:], rw_ap, None, ALU.mult)
        else:
            vtt(s_sc[:], s_sc[:], tw_ap, ALU.mult, wait=[dw2, dw3])
            vtt(s_rsc[:], s_rsc[:], rw_ap, ALU.mult)
        p.drain("v")
        xsm = vtt(src4.rearrange("p (t b e) -> p t b e", b=32, e=32),
                  src4.rearrange("p (t b e) -> p t b e", b=32, e=32),
                  s_rsc[:].rearrange("p (t b) -> p t b", b=32).unsqueeze(3)
                  .broadcast_to([128, 4, 32, 32]), ALU.mult, inc=True)
        last = None
        for t_ in range(4):
            xs = src4[:, t_ * 1024:(t_ + 1) * 1024]
            xq = dst4[:, t_ * 1024:(t_ + 1) * 1024]
            w_s = [xsm] if t_ == 0 else [qsgn]
            sg1 = sact(tmp1[:], xs, AF.Sign, inc=True, wait=w_s)
            sg2 = sact(tmp2[:], xs, AF.Abs, inc=True)
            vts(tmp3[:], tmp2[:], 0.25, 0.5, ALU.is_gt, ALU.mult, wait=[sg2])
            vts(tmp4[:], tmp2[:], 0.75, 0.5, ALU.is_ge, ALU.mult)
            vtt(tmp3[:], tmp3[:], tmp4[:], ALU.add)
            vts(tmp4[:], tmp2[:], 1.25, 0.5, ALU.is_gt, ALU.mult)
            vtt(tmp3[:], tmp3[:], tmp4[:], ALU.add)
            vts(tmp4[:], tmp2[:], 1.75, 0.5, ALU.is_ge, ALU.mult)
            vtt(tmp3[:], tmp3[:], tmp4[:], ALU.add)
            vts(tmp4[:], tmp2[:], 2.5, 1.0, ALU.is_gt, ALU.mult)
            vtt(tmp3[:], tmp3[:], tmp4[:], ALU.add)
            vts(tmp4[:], tmp2[:], 3.5, 1.0, ALU.is_ge, ALU.mult)
            vtt(tmp3[:], tmp3[:], tmp4[:], ALU.add)
            vts(tmp4[:], tmp2[:], 5.0, 2.0, ALU.is_gt, ALU.mult)
            vtt(tmp3[:], tmp3[:], tmp4[:], ALU.add)
            qsgn = vtt(xq, tmp3[:], tmp1[:], ALU.mult, inc=True)
            last = vtt(xq.rearrange("p (b e) -> p b e", e=32),
                       xq.rearrange("p (b e) -> p b e", e=32),
                       s_sc[:, t_ * 32:(t_ + 1) * 32].unsqueeze(2)
                       .broadcast_to([128, 32, 32]), ALU.mult,
                       inc=(t_ == 3))
        return last

    wq_done = None
    for wb in range(2):
        lw = None
        for i in range(4):
            oc = wb * 4 + i
            w = u_done if (wb == 0 and i == 0) else []
            if wb == 1 and i == 0:
                w = [wq_done]
            lw = p.load(qw_src[i], d_wffn[oc * 128:(oc + 1) * 128, :], wait=w)
        qd = quant_batch4(arD[:, 0:4096], arD[:, 0:4096],
                          s_tdw[:, wb * 128:(wb + 1) * 128],
                          s_rdw[:, wb * 128:(wb + 1) * 128],
                          [lw, dw2, dw3] + (u_done if wb == 0 else []),
                          krange=(-11, 1))
        if wb == 0:
            tap("wq0", arD[:, 0:1024], 128, 1024, wait=[qd])
            tap("wam", s_am[:], 128, 128, wait=[qd])
            tap("wsc", s_sc[:], 128, 128, wait=[qd])
            tap("wrsc", s_rsc[:], 128, 128, wait=[qd])
            tap("wtdw", s_tdw[:, 0:128], 128, 128, wait=[qd])
        for i in range(4):
            oc = wb * 4 + i
            for dc_ in range(DC):
                w = [qd] if dc_ == 0 else []
                w += fw(f"PQ{pg_i[0]}")
                tm = mm(pG[pg_i[0]][:, 0:256],
                        arD[:, i * 1024 + dc_ * 128: i * 1024 + (dc_ + 1) * 128],
                        s_id2[:], True, True, inc=True, wait=w)
                wq_done = scp(vwqT[dc_][:, oc * 128:(oc + 1) * 128],
                              pG[pg_i[0]][:, 0:128], inc=True, wait=[tm])
                mark(f"PQ{pg_i[0]}", wq_done)
                pg_i[0] ^= 1
    tap("wqT", vwqT[0], 128, C, wait=[wq_done])

    out_stores = []
    tpq = [2]
    xq_marks = {}
    st_marks = {}

    def emit_transp(src_fn, dst_fn, wait_first):
        done = None
        for t_ in range(4):
            for dc_ in range(DC):
                w = list(wait_first) if (t_ == 0 and dc_ == 0) else []
                i = tpq[0]
                w += fw(f"PQ{i}")
                tm = mm(PQ[i][:, 0:256], src_fn(t_, dc_), s_id2[:],
                        True, True, inc=True, wait=w)
                done = scp(dst_fn(t_, dc_), PQ[i][:, 0:128], inc=True, wait=[tm])
                mark(f"PQ{i}", done)
                tpq[0] = 2 + ((tpq[0] - 2 + 1) % 2)
        return done

    def emit_ffn(q5):
        xsl = slice((q5 % 2) * 512, (q5 % 2) * 512 + 512)
        vfa = vfab[q5 % 2]
        f_done = None
        for oc in range(DC):
            for dc_ in range(DC):
                w = []
                if dc_ == 0:
                    w = list(fw(f"PQ{pg_i[0]}"))
                    if oc == 0:
                        w += [xq_marks[q5], wq_done]
                        if q5 - 2 in st_marks:
                            w.append(st_marks[q5 - 2])
                mmn = mm(PQ[pg_i[0]][:, :], vwqT[dc_][:, oc * 128:(oc + 1) * 128],
                         vxq2[dc_][:, xsl], dc_ == 0, dc_ == DC - 1,
                         inc=(dc_ == DC - 1), wait=w)
            w2 = [mmn]
            if oc == 0 and q5 - 2 in st_marks:
                w2.append(st_marks[q5 - 2])
            f_done = sact(vfa[oc], PQ[pg_i[0]][:, :], AF.Identity,
                          bias=s_bias[:, oc:oc + 1], inc=True, wait=w2)
            mark(f"PQ{pg_i[0]}", f_done)
            pg_i[0] ^= 1
        return f_done

    def emit_lnf(q5, f_done):
        sl = slice(q5 * 512, (q5 + 1) * 512)
        vfa = vfab[q5 % 2]
        c4 = ln_tile(vfa, vfa, add=[u[:, sl] for u in vu], src_wait=[f_done])
        if q5 == 0:
            tap("f0", vfa[0], 128, 512, wait=c4)
        last = None
        for c_ in range(DC):
            last = p.store(d_out[c_ * 128:(c_ + 1) * 128, sl], vfa[c_],
                           wait=c4)
            out_stores.append(last)
        st_marks[q5] = last
        return c4

    def emit_quant_stage(q5, w0):
        sl = slice(q5 * 512, (q5 + 1) * 512)
        h2_done = ln_tile([u[:, sl] for u in vu], qw_h2f, src_wait=w0)
        tp_done = emit_transp(
            lambda t_, dc_: qw_h2f[dc_][:, t_ * 128:(t_ + 1) * 128],
            lambda t_, dc_: arD[:, t_ * 1024 + dc_ * 128: t_ * 1024 + (dc_ + 1) * 128],
            h2_done)
        return tp_done

    f_prev = None
    for q5 in range(NQ5):
        sl = slice(q5 * 512, (q5 + 1) * 512)
        w0 = [wq_done] if q5 == 0 else []
        tp_done = emit_quant_stage(q5, w0)
        if q5 >= 1:
            f_prev = emit_ffn(q5 - 1)
        qd = quant_batch4(arD[:, 0:4096], arD[:, 0:4096],
                          s_tda[:, 0:1], s_tda[:, 1:2], [tp_done],
                          krange=(-6, 4))
        xq_marks[q5] = emit_transp(
            lambda t_, dc_: arD[:, t_ * 1024 + dc_ * 128: t_ * 1024 + (dc_ + 1) * 128],
            lambda t_, dc_, q=q5: vxq2[dc_][:, (q % 2) * 512 + t_ * 128:
                                            (q % 2) * 512 + (t_ + 1) * 128],
            [qd])
        if q5 >= 1:
            emit_lnf(q5 - 1, f_prev)
    f_prev = emit_ffn(NQ5 - 1)
    emit_lnf(NQ5 - 1, f_prev)

    p.ops.append(("y", lambda h, S, n=p.dcnt["st"] * 16: h.wait_ge(S["st"], n)))

    S = {}
    S["ld"] = nc.alloc_semaphore("sem_ld")
    S["st"] = nc.alloc_semaphore("sem_st")
    for e in ("t", "v", "s", "g"):
        S[e] = nc.alloc_semaphore(f"sem_{e}")

    with nc.Block() as blk:
        @blk.tensor
        def _(eng):
            for e, fn in p.ops:
                if e == "t":
                    fn(eng, S)

        @blk.vector
        def _(eng):
            for e, fn in p.ops:
                if e == "v":
                    fn(eng, S)

        @blk.scalar
        def _(eng):
            for e, fn in p.ops:
                if e == "s":
                    fn(eng, S)

        @blk.gpsimd
        def _(eng):
            for e, fn in p.ops:
                if e == "g":
                    fn(eng, S)

        @blk.sync
        def _(eng):
            for e, fn in p.ops:
                if e == "y":
                    fn(eng, S)

    return tap_aps


def _in_map(x_b, W_dq, W_uq, W_dkv, W_uk, W_uv, W_o, Wf, bias, dw, da):
    cc = np.ascontiguousarray
    return {
        "xf": cc(x_b.T),
        "wdqT": cc(W_dq.T),
        "wuqT": cc(W_uq.T),
        "wdkvT": cc(W_dkv.T),
        "wuq": cc(W_uq),
        "wuk": cc(W_uk),
        "wuv": cc(W_uv),
        "wdq": cc(W_dq),
        "woT": cc(W_o.T),
        "wffn": cc(Wf),
        "biasr": cc(bias.reshape(DC, 128).T),
        "dw": cc(dw),
        "da": cc(da.reshape(1, 1)),
    }


def kernel(L0_attn_x, L0_attn_W_dq, L0_attn_W_uq, L0_attn_W_dkv, L0_attn_W_uk,
           L0_attn_W_uv, L0_attn_W_o, L0_ffn_weight, L0_ffn_bias,
           L0_ffn_delta_w, L0_ffn_delta_act):
    x = np.asarray(L0_attn_x, dtype=np.float32)
    args = [np.asarray(a, dtype=np.float32) for a in
            (L0_attn_W_dq, L0_attn_W_uq, L0_attn_W_dkv, L0_attn_W_uk,
             L0_attn_W_uv, L0_attn_W_o, L0_ffn_weight, L0_ffn_bias,
             L0_ffn_delta_w, L0_ffn_delta_act)]
    nc = bass.Bass("TRN2", target_bir_lowering=False, debug=False)
    build(nc, taps=tuple(DEBUG_TAPS))
    in_maps = [_in_map(x[b], *args) for b in range(8)]
    res = run_bass_kernel_spmd(nc, in_maps, core_ids=list(range(8)))
    out = np.stack([np.ascontiguousarray(r["outf"].T) for r in res.results])
    if DEBUG_TAPS:
        kernel.last_results = res
    return out


# version 08
# speedup vs baseline: 1.0030x; 1.0030x over previous
import numpy as np

import concourse.bass as bass
import concourse.mybir as mybir
from concourse.bass_utils import run_bass_kernel_spmd

F32 = mybir.dt.float32
AF = mybir.ActivationFunctionType
ALU = mybir.AluOpType
AX = mybir.AxisListType

C = 1024
T = 2048
QL = 512
KVL = 512
NH = 16
HS = 64
NG = 4
DC = C // 128
LC = KVL // 128
MC = QL // 128
NQT = T // 256
NQ5 = T // 512
NTK = T // 128
LN2 = float(np.log(2.0))
NEG = -1.0e30
KMIN, KMAX = -14, 5

DEBUG_TAPS = []


class Prog:

    ENGS = ("t", "v", "s", "g", "y")

    def __init__(self):
        self.ops = []
        self.cnt = {e: 0 for e in self.ENGS}
        self.dcnt = {"ld": 0, "st": 0}
        self._lastwait = {e: {} for e in self.ENGS}

    def _wait(self, eng, key, val):
        if val <= 0:
            return
        lw = self._lastwait[eng]
        if lw.get(key, -1) >= val:
            return
        lw[key] = val
        self.ops.append((eng, lambda h, S, k=key, v=val: h.wait_ge(S[k], v)))

    def _chk(self, key, val):
        lim = self.cnt[key] if key in self.cnt else self.dcnt[key] * 16
        assert val <= lim, f"wait on future count {key}={val} (now {lim})"

    def op(self, eng, build, inc=False, wait=()):
        for (k, v) in wait:
            self._chk(k, v)
            self._wait(eng, k, v)
        if inc:
            self.cnt[eng] += 1

            def fn(h, S, b=build, e=eng):
                b(h, S).then_inc(S[e], 1)
            self.ops.append((eng, fn))
            return (eng, self.cnt[eng])
        self.ops.append((eng, lambda h, S, b=build: b(h, S)))
        return None

    def drain(self, eng):
        self.ops.append((eng, lambda h, S: h.drain()))

    def load(self, out_ap, in_ap, wait=()):
        for (k, v) in wait:
            self._chk(k, v)
            self._wait("y", k, v)
        self.dcnt["ld"] += 1
        self.ops.append(("y", lambda h, S, o=out_ap, i=in_ap:
                         h.dma_start(out=o, in_=i).then_inc(S["ld"], 16)))
        return ("ld", self.dcnt["ld"] * 16)

    def store(self, out_ap, in_ap, wait=()):
        for (k, v) in wait:
            self._chk(k, v)
            self._wait("g", k, v)
        self.dcnt["st"] += 1
        self.ops.append(("g", lambda h, S, o=out_ap, i=in_ap:
                         h.dma_start(out=o, in_=i).then_inc(S["st"], 16)))
        return ("st", self.dcnt["st"] * 16)


def build(nc: bass.Bass, taps=()):
    d_xf = nc.dram_tensor("xf", [C, T], F32, kind="ExternalInput").ap()
    d_wdqT = nc.dram_tensor("wdqT", [C, QL], F32, kind="ExternalInput").ap()
    d_wuqT = nc.dram_tensor("wuqT", [QL, C], F32, kind="ExternalInput").ap()
    d_wdkvT = nc.dram_tensor("wdkvT", [C, KVL], F32, kind="ExternalInput").ap()
    d_wuq = nc.dram_tensor("wuq", [C, QL], F32, kind="ExternalInput").ap()
    d_wuk = nc.dram_tensor("wuk", [C, KVL], F32, kind="ExternalInput").ap()
    d_wuv = nc.dram_tensor("wuv", [C, KVL], F32, kind="ExternalInput").ap()
    d_wdq = nc.dram_tensor("wdq", [QL, C], F32, kind="ExternalInput").ap()
    d_woT = nc.dram_tensor("woT", [C, C], F32, kind="ExternalInput").ap()
    d_wffn = nc.dram_tensor("wffn", [C, C], F32, kind="ExternalInput").ap()
    d_biasr = nc.dram_tensor("biasr", [128, DC], F32, kind="ExternalInput").ap()
    d_dw = nc.dram_tensor("dw", [C, 32], F32, kind="ExternalInput").ap()
    d_da = nc.dram_tensor("da", [1, 1], F32, kind="ExternalInput").ap()
    d_out = nc.dram_tensor("outf", [C, T], F32, kind="ExternalOutput").ap()
    d_y = nc.dram_tensor("ydram", [C, T], F32, kind="Internal").ap()

    tap_aps = {}
    for name in taps:
        pass

    arA = nc.alloc_sbuf_tensor("arA", [128, 16448], F32)
    arB = nc.alloc_sbuf_tensor("arB", [128, 16384], F32)
    arC = nc.alloc_sbuf_tensor("arC", [128, 8192], F32)
    arD = nc.alloc_sbuf_tensor("arD", [128, 8192], F32)

    def views(ar, start, n, w):
        return [ar[:, start + i * w: start + (i + 1) * w] for i in range(n)]

    vx = views(arA, 0, DC, T)
    vckv = views(arB, 0, LC, T)
    vqmid = views(arB, LC * T, MC, T)
    vkT = views(arC, 0, LC, C)
    vve = views(arC, LC * C, LC, C)
    woT_v = views(arB, 0, DC, C)
    vB = views(arB, DC * C, MC, KVL)
    wuq_v = views(arD, 0, DC, QL)
    wuk_v = views(arD, DC * QL, DC, KVL)
    wdq_v = views(arD, 0, MC, C)
    wuv_v = views(arD, MC * C, DC, KVL)
    wdkvT_v = views(arD, 0, DC, KVL)
    wdqT_v = views(arD, DC * KVL, DC, QL)

    vK = views(arA, 0, 2, T)
    vV = views(arA, 2 * T, NTK, 260)
    vV3 = arA[:, 2 * T: 2 * T + NTK * 260].rearrange("p (t w) -> p t w", w=260)
    vqg = views(arA, 2 * T + NTK * 260, 2, T)
    vwuqT = views(arA, 4 * T + NTK * 260, MC, C)
    vvaug = views(arD, 0, LC, 260)
    s_pt = arD[:, 1040:2064]
    s_yraw = arD[:, 2064:2576]
    s_yst = arD[:, 2576:3088]
    s_rrow = arD[:, 3088:3344]
    s_mka = arD[:, 3344:3600]
    s_mkb = arD[:, 3600:3856]

    vu = views(arB, 0, DC, T)
    vxq2 = views(arA, 0, DC, 1024)
    vfab = [views(arA, DC * 1024, DC, 512),
            views(arA, DC * 1024 + DC * 512, DC, 512)]
    vwqT = views(arC, 0, DC, C)
    vxs = views(arD, 0, DC, 512)
    vys = views(arD, DC * 512, DC, 512)
    qw_src = views(arD, 0, 4, 1024)
    qw_h2f = views(arD, 4096, DC, 512)
    CKB = 4096
    ck_sl = [arD[:, CKB + i * 128: CKB + (i + 1) * 128] for i in range(21)]
    tmp1 = arD[:, 4096:5120]
    tmp2 = arD[:, 5120:6144]
    tmp3 = arD[:, 6144:7168]
    tmp4 = arD[:, 7168:8192]
    vf = views(arD, 0, DC, 512)
    vo = views(arD, DC * 512, DC, 512)

    s_ones = nc.alloc_sbuf_tensor("s_ones", [128, 1], F32)
    s_onesr = nc.alloc_sbuf_tensor("s_onesr", [1, 128], F32)
    s_id2 = nc.alloc_sbuf_tensor("s_id2", [128, 256], F32)
    s_e64 = nc.alloc_sbuf_tensor("s_e64", [128, 64], F32)
    s_lnt = nc.alloc_sbuf_tensor("s_lnt", [128, 512], F32)
    s_lnq = nc.alloc_sbuf_tensor("s_lnq", [128, 512], F32)
    s_lns = nc.alloc_sbuf_tensor("s_lns", [128, 512], F32)
    s_rows = nc.alloc_sbuf_tensor("s_rows", [1, 1024], F32)
    s_eps = nc.alloc_sbuf_tensor("s_eps", [128, 1], F32)
    s_am = nc.alloc_sbuf_tensor("s_am", [128, 128], F32)
    s_sc = nc.alloc_sbuf_tensor("s_sc", [128, 128], F32)
    s_rsc = nc.alloc_sbuf_tensor("s_rsc", [128, 128], F32)
    s_bias = nc.alloc_sbuf_tensor("s_bias", [128, DC], F32)
    s_tdw = nc.alloc_sbuf_tensor("s_tdw", [128, DC * 32], F32)
    s_rdw = nc.alloc_sbuf_tensor("s_rdw", [128, DC * 32], F32)
    s_da = nc.alloc_sbuf_tensor("s_da", [1, 2], F32)
    s_tda = nc.alloc_sbuf_tensor("s_tda", [128, 2], F32)

    PQ = [nc.alloc_psum_tensor(f"PQ{i}", [128, 512], F32) for i in range(4)]
    pY = [nc.alloc_psum_tensor("pY0", [128, 512], F32),
          nc.alloc_psum_tensor("pY1", [128, 512], F32)]
    pG = [PQ[0], PQ[1]]
    pB = nc.alloc_psum_tensor("pB", [64, 256], F32)
    pH = nc.alloc_psum_tensor("pH", [128, 512], F32)

    p = Prog()
    free = {}

    def fw(res):
        v = free.get(res)
        return list(v) if v else []

    def mark(res, mk):
        if mk is None:
            return
        free[res] = list(mk) if isinstance(mk, list) else [mk]

    def mm(out, lhsT, rhs, start, stop, inc=False, wait=()):
        return p.op("t", lambda h, S, o=out, l=lhsT, r=rhs, a=start, b=stop:
                    h.matmul(o, l, r, start=a, stop=b), inc=inc, wait=wait)

    def vcp(out, in_, inc=False, wait=()):
        return p.op("v", lambda h, S, o=out, i=in_: h.tensor_copy(o, i),
                    inc=inc, wait=wait)

    def vtt(out, a, b, alu, inc=False, wait=()):
        return p.op("v", lambda h, S, o=out, x=a, yy=b, u=alu:
                    h.tensor_tensor(out=o, in0=x, in1=yy, op=u), inc=inc, wait=wait)

    def vts(out, in_, s1, s2, op0, op1=None, inc=False, wait=()):
        if s2 is None:
            return p.op("v", lambda h, S, o=out, i=in_, a=s1, u0=op0:
                        h.tensor_scalar(out=o, in0=i, scalar1=a, scalar2=None,
                                        op0=u0), inc=inc, wait=wait)
        return p.op("v", lambda h, S, o=out, i=in_, a=s1, b=s2, u0=op0, u1=op1:
                    h.tensor_scalar(out=o, in0=i, scalar1=a, scalar2=b,
                                    op0=u0, op1=u1), inc=inc, wait=wait)

    def vred_absmax(out, in_, inc=False, wait=()):
        return p.op("v", lambda h, S, o=out, i=in_:
                    h.tensor_reduce(o, i, axis=AX.X, op=ALU.max,
                                    apply_absolute_value=True), inc=inc, wait=wait)

    def vrecip(out, in_, inc=False, wait=()):
        return p.op("v", lambda h, S, o=out, i=in_: h.reciprocal(o, i),
                    inc=inc, wait=wait)

    def vmemset(ap, val, inc=False, wait=()):
        return p.op("v", lambda h, S, a=ap, x=val: h.memset(a, x),
                    inc=inc, wait=wait)

    def sact(out, in_, func, bias=0.0, scale=1.0, inc=False, wait=()):
        return p.op("s", lambda h, S, o=out, i=in_, f=func, b=bias, sc=scale:
                    h.activation(out=o, in_=i, func=f, bias=b, scale=sc),
                    inc=inc, wait=wait)

    def gp(build, inc=False, wait=()):
        return p.op("g", build, inc=inc, wait=wait)

    def gtt(out, a, b, alu, inc=False, wait=()):
        return p.op("g", lambda h, S, o=out, x=a, yy=b, u=alu:
                    h.tensor_tensor(out=o, in0=x, in1=yy, op=u), inc=inc, wait=wait)

    def scp(out, in_, inc=False, wait=()):
        return p.op("s", lambda h, S, o=out, i=in_: h.copy(o, i),
                    inc=inc, wait=wait)

    def tap(name, src_ap, rows, cols, wait=()):
        if name not in taps:
            return
        dt = nc.dram_tensor(f"tap_{name}", [rows, cols], F32,
                            kind="ExternalOutput").ap()
        tap_aps[name] = dt
        p.store(dt, src_ap, wait=wait)

    gp(lambda h, S: h.memset(s_ones[:], 1.0))
    gp(lambda h, S: h.memset(s_onesr[:], 1.0))
    gp(lambda h, S: h.memset(s_eps[:], 1e-5))
    gp(lambda h, S: h.memset(s_id2[:], 0.0))
    gp(lambda h, S: h.affine_select(
        out=s_id2[:].rearrange("p (a b) -> p a b", b=128),
        in_=s_id2[:].rearrange("p (a b) -> p a b", b=128),
        compare_op=ALU.not_equal, fill=1.0, base=0,
        pattern=[[0, 2], [-1, 128]], channel_multiplier=1))
    gp(lambda h, S: h.memset(s_e64[:], 1.0))
    g_setup = gp(lambda h, S: h.affine_select(
        out=s_e64[:], in_=s_e64[:], compare_op=ALU.is_equal, fill=0.0,
        base=-64, pattern=[[0, 64]], channel_multiplier=1), inc=True)

    ld_x = None
    for c_ in range(DC):
        ld_x = p.load(vx[c_], d_xf[c_ * 128:(c_ + 1) * 128, :])
    ld_wuq = p.load(arD[:, 0:DC * QL].rearrange("p (c m) -> p c m", m=QL), d_wuq.rearrange("(c p) m -> p c m", p=128))
    ld_wuk = p.load(arD[:, DC * QL:2 * DC * QL].rearrange("p (c m) -> p c m", m=KVL),
                    d_wuk.rearrange("(c p) m -> p c m", p=128))
    ld_woT = p.load(arB[:, 0:DC * C].rearrange("p (c m) -> p c m", m=C), d_woT.rearrange("(c p) m -> p c m", p=128))
    p.load(s_bias[:], d_biasr)
    p.load(s_tdw[:].rearrange("p (c m) -> p c m", m=32), d_dw.rearrange("(c p) m -> p c m", p=128))
    ld_small = p.load(s_da[0:1, 0:1], d_da)

    R_SUM, R_SQ = slice(0, 512), slice(512, 1024)

    def ln_tile(src, dst, add=None, src_wait=(), tapname=None):
        gates = []
        vtt(s_lnq[:], src[0], src[0], ALU.mult, wait=list(src_wait))
        for c_ in range(1, DC):
            vtt(s_lnt[:], src[c_], src[c_], ALU.mult)
            vtt(s_lnq[:], s_lnq[:], s_lnt[:], ALU.add)
        vq = p.op("v", lambda h, S: h.tensor_scalar(
            out=s_lnq[:], in0=s_lnq[:], scalar1=1.0, scalar2=None,
            op0=ALU.mult), inc=True)
        vtt(s_lns[:], src[0], src[1], ALU.add)
        for c_ in range(2, DC):
            vt = vtt(s_lns[:], s_lns[:], src[c_], ALU.add, inc=(c_ == DC - 1))
        if tapname is not None and f"{tapname}_lnt" in taps:
            gates.append(p.store(
                nc.dram_tensor(f"tap_{tapname}_lnt", [128, 512], F32,
                               kind="ExternalOutput").ap(), s_lns[:], wait=[vt]))
            gates.append(p.store(
                nc.dram_tensor(f"tap_{tapname}_lnq", [128, 512], F32,
                               kind="ExternalOutput").ap(), s_lnq[:], wait=[vq]))
        m1 = mm(pH[0:1, :], s_ones[:], s_lns[:], True, True, inc=True,
                wait=[vt, g_setup] + fw("pH"))
        m2 = mm(pY[0][0:1, :], s_ones[:], s_lnq[:], True, True, inc=True,
                wait=[vq] + fw("pYL"))
        c1 = vcp(s_rows[0:1, R_SUM], pH[0:1, :], inc=True, wait=[m1])
        c2 = vcp(s_rows[0:1, R_SQ], pY[0][0:1, :], inc=True, wait=[m2])
        p.drain("v")
        vm = vts(s_rows[0:1, R_SUM], s_rows[0:1, R_SUM], 1.0 / C, None, ALU.mult,
                 inc=True)
        vts(s_rows[0:1, R_SQ], s_rows[0:1, R_SQ], 1.0 / C, None, ALU.mult)
        p.drain("v")
        vtt(s_lns[0:1, 0:512], s_rows[0:1, R_SUM], s_rows[0:1, R_SUM], ALU.mult)
        p.drain("v")
        vv = vtt(s_rows[0:1, R_SQ], s_rows[0:1, R_SQ], s_lns[0:1, 0:512],
                 ALU.subtract, inc=True)
        sq = sact(s_rows[0:1, R_SQ], s_rows[0:1, R_SQ], AF.Sqrt,
                  bias=s_eps[0:1, :], inc=True, wait=[vv])
        rr = vrecip(s_rows[0:1, R_SQ], s_rows[0:1, R_SQ], inc=True, wait=[sq])
        if tapname is not None and f"{tapname}_rows" in taps:
            gates.append(p.store(
                nc.dram_tensor(f"tap_{tapname}_rows", [1, 1024], F32,
                               kind="ExternalOutput").ap(), s_rows[0:1, :],
                wait=[rr]))
        b1 = mm(pH[:, :], s_onesr[0:1, 0:128], s_rows[0:1, R_SQ], True, True,
                inc=True, wait=[rr, c1])
        b2 = mm(pY[0][:, :], s_onesr[0:1, 0:128], s_rows[0:1, R_SUM], True, True,
                inc=True, wait=[vm, c2])
        e1 = vcp(s_lnt[:], pH[:, :], inc=True, wait=[b1])
        e2 = vcp(s_lnq[:], pY[0][:, :], inc=True, wait=[b2])
        mark("pH", e1)
        mark("pYL", e2)
        vlast = None
        for c_ in range(DC):
            vtt(dst[c_], src[c_], s_lnq[:], ALU.subtract,
                wait=[e1, e2] if c_ == 0 else [])
            if add is None:
                vlast = vtt(dst[c_], dst[c_], s_lnt[:], ALU.mult,
                            inc=(c_ == DC - 1))
            else:
                vtt(dst[c_], dst[c_], s_lnt[:], ALU.mult)
                vlast = vtt(dst[c_], dst[c_], add[c_], ALU.add,
                            inc=(c_ == DC - 1))
        marks = [vlast]
        if gates:
            marks += gates
        return marks

    def ln_full(srcv, dstv, addv=None, src_wait=(), tapname=None):
        last = None
        for q5 in range(NQ5):
            sl = slice(q5 * 512, (q5 + 1) * 512)
            last = ln_tile([s[:, sl] for s in srcv], [d[:, sl] for d in dstv],
                           add=None if addv is None else [a[:, sl] for a in addv],
                           src_wait=src_wait if q5 == 0 else last,
                           tapname=(tapname if q5 == 0 else
                                    (f"{tapname}s3" if (tapname and q5 == 3) else None)))
        return last

    ln_h1 = ln_full(vx, vx, src_wait=[ld_x], tapname="h1")

    pg_i = [0]

    def proj(out_views, out_w, lhs_tiles, rhs_tiles, nk, first_wait,
             m_range, n_range, lhs_sel, rhs_sel, out_sel):
        done = None
        for mi in m_range:
            for nj in n_range:
                for kk in range(nk):
                    w = []
                    if kk == 0:
                        w = list(fw(f"PQ{pg_i[0]}"))
                        if mi == m_range[0] and nj == n_range[0]:
                            w += list(first_wait)
                    mmn = mm(pG[pg_i[0]][:, 0:out_w], lhs_sel(mi, kk),
                             rhs_sel(nj, kk), kk == 0, kk == nk - 1,
                             inc=(kk == nk - 1), wait=w)
                done = vcp(out_sel(mi, nj), pG[pg_i[0]][:, 0:out_w], inc=True,
                           wait=[mmn])
                mark(f"PQ{pg_i[0]}", done)
                pg_i[0] ^= 1
        return done

    b_done = proj(None, 512, None, None, DC, [ld_wuq, ld_wuk],
                  list(range(MC)), [0],
                  lambda mi, kk: wuq_v[kk][:, mi * 128:(mi + 1) * 128],
                  lambda nj, kk: wuk_v[kk],
                  lambda mi, nj: vB[mi])
    ld_wdq = p.load(arD[:, 0:MC * C].rearrange("p (c m) -> p c m", m=C), d_wdq.rearrange("(c p) m -> p c m", p=128),
                    wait=[b_done])
    kT_done = proj(None, 512, None, None, MC, [ld_wdq],
                   list(range(LC)), list(range(2)),
                   lambda mi, kk: vB[kk][:, mi * 128:(mi + 1) * 128],
                   lambda nj, kk: wdq_v[kk][:, nj * 512:(nj + 1) * 512],
                   lambda mi, nj: vkT[mi][:, nj * 512:(nj + 1) * 512])
    ld_wuv = p.load(arD[:, MC * C:MC * C + DC * KVL].rearrange("p (c m) -> p c m", m=KVL),
                    d_wuv.rearrange("(c p) m -> p c m", p=128), wait=[kT_done])
    ve_done = proj(None, 512, None, None, DC, [ld_wuv, ld_woT],
                   list(range(LC)), list(range(2)),
                   lambda mi, kk: wuv_v[kk][:, mi * 128:(mi + 1) * 128],
                   lambda nj, kk: woT_v[kk][:, nj * 512:(nj + 1) * 512],
                   lambda mi, nj: vve[mi][:, nj * 512:(nj + 1) * 512])
    ld_wdkvT = p.load(arD[:, 0:DC * KVL].rearrange("p (c m) -> p c m", m=KVL),
                      d_wdkvT.rearrange("(c p) m -> p c m", p=128),
                      wait=[kT_done])
    ckv_done = proj(None, 512, None, None, DC, [ld_wdkvT] + ln_h1 + [ve_done],
                    list(range(LC)), list(range(4)),
                    lambda mi, kk: wdkvT_v[kk][:, mi * 128:(mi + 1) * 128],
                    lambda nj, kk: vx[kk][:, nj * 512:(nj + 1) * 512],
                    lambda mi, nj: vckv[mi][:, nj * 512:(nj + 1) * 512])
    ld_wdqT = p.load(arD[:, DC * KVL:2 * DC * KVL].rearrange("p (c m) -> p c m", m=QL),
                     d_wdqT.rearrange("(c p) m -> p c m", p=128), wait=[ve_done])
    qmid_done = proj(None, 512, None, None, DC, [ld_wdqT],
                     list(range(MC)), list(range(4)),
                     lambda mi, kk: wdqT_v[kk][:, mi * 128:(mi + 1) * 128],
                     lambda nj, kk: vx[kk][:, nj * 512:(nj + 1) * 512],
                     lambda mi, nj: vqmid[mi][:, nj * 512:(nj + 1) * 512])
    ld_wuqT = p.load(arA[:, 4 * T + NTK * 260: 4 * T + NTK * 260 + MC * C].rearrange("p (c m) -> p c m", m=C),
                     d_wuqT.rearrange("(c p) m -> p c m", p=128),
                     wait=[ckv_done, qmid_done])

    tap("h1", vx[0], 128, T, wait=ln_h1)
    tap("ckv", vckv[0], 128, T, wait=[ckv_done])
    tap("keffT", vkT[0], 128, C, wait=[kT_done])
    tap("veff", vve[0], 128, C, wait=[ve_done])
    tap("qmid", vqmid[0], 128, T, wait=[qmid_done])

    t1 = sact(s_da[0:1, 1:2], s_da[0:1, 0:1], AF.Tanh, scale=4.0, inc=True,
              wait=[ld_small] + ln_h1)
    p.drain("s")
    t2 = sact(s_rows[0:1, 0:256], s_da[0:1, 1:2].broadcast_to([1, 256]), AF.Exp,
              scale=0.5 * LN2, inc=True)
    p.drain("s")
    t3 = sact(s_rows[0:1, 256:512], s_da[0:1, 1:2].broadcast_to([1, 256]), AF.Exp,
              scale=-0.5 * LN2, inc=True)
    bd1 = mm(pH[:, 0:256], s_onesr[0:1, 0:128], s_rows[0:1, 0:256], True, True,
             inc=True, wait=[t2] + fw("pH"))
    cd1 = vcp(s_tda[:, 0:1], pH[:, 0:1], inc=True, wait=[bd1])
    bd2 = mm(pH[:, 0:256], s_onesr[0:1, 0:128], s_rows[0:1, 256:512], True, True,
             inc=True, wait=[t3, cd1])
    cd2 = vcp(s_tda[:, 1:2], pH[:, 1:2], inc=True, wait=[bd2])
    mark("pH", cd2)

    gp(lambda h, S: h.memset(s_mka[:], 0.0), wait=[qmid_done, ckv_done])
    gp(lambda h, S: h.affine_select(
        out=s_mka[:], in_=s_mka[:], compare_op=ALU.is_ge, fill=NEG,
        base=0, pattern=[[1, 256]], channel_multiplier=-1))
    gp(lambda h, S: h.memset(s_mkb[:], 0.0))
    gp(lambda h, S: h.affine_select(
        out=s_mkb[:], in_=s_mkb[:], compare_op=ALU.is_ge, fill=NEG,
        base=-128, pattern=[[1, 256]], channel_multiplier=-1))
    rrow_init = gp(lambda h, S: h.memset(s_rrow[:], 0.0), inc=True)
    pt_free = [None, None, None, None]
    ps_free = [None, None]
    py_free = [None, None]
    yst_free = [None, None]
    yraw_free = [None, None]
    pb_free = rrow_init
    last_store = None

    for g_ in range(NG):
        kg_done = proj(None, 512, None, None, LC,
                       [kT_done, ckv_done] if g_ == 0 else [],
                       list(range(2)), list(range(4)),
                       lambda mi, kk, g=g_: vkT[kk][:, 256 * g + 128 * mi:
                                                    256 * g + 128 * (mi + 1)],
                       lambda nj, kk: vckv[kk][:, nj * 512:(nj + 1) * 512],
                       lambda mi, nj: vK[mi][:, nj * 512:(nj + 1) * 512])
        for lc_ in range(LC):
            vmemset(vvaug[lc_], 0.0,
                    wait=[qmid_done] if g_ == 0 and lc_ == 0 else [])
        va_done = None
        for lc_ in range(LC):
            for j in range(4):
                va_done = vcp(vvaug[lc_][:, 65 * j:65 * j + 64],
                              vve[lc_][:, 64 * (4 * g_ + j):64 * (4 * g_ + j) + 64],
                              inc=(lc_ == LC - 1 and j == 3),
                              wait=[ve_done] if (g_ == 0 and lc_ == 0 and j == 0)
                              else [])
        vg_done = proj(None, 260, None, None, LC, [va_done],
                       list(range(NTK)), [0],
                       lambda mi, kk: vckv[kk][:, mi * 128:(mi + 1) * 128],
                       lambda nj, kk: vvaug[kk],
                       lambda mi, nj: vV[mi])
        for j in range(4):
            vg_done = vmemset(vV3[:, :, 65 * j + 64:65 * j + 65], 1.0,
                              inc=(j == 3))
        qg_done = proj(None, 512, None, None, MC, [ld_wuqT, qmid_done],
                       list(range(2)), list(range(4)),
                       lambda mi, kk, g=g_: vwuqT[kk][:, 256 * g + 128 * mi:
                                                      256 * g + 128 * (mi + 1)],
                       lambda nj, kk: vqmid[kk][:, nj * 512:(nj + 1) * 512],
                       lambda mi, nj: vqg[mi][:, nj * 512:(nj + 1) * 512])
        if g_ == 0:
            tap("K0", vK[0], 128, T, wait=[kg_done])
            tap("V0", arA[:, 2 * T:2 * T + 2048], 128, 2048, wait=[vg_done])
            tap("q0", vqg[0], 128, T, wait=[qg_done])

        for pr in range(2):
            for qt in range(NQT):
                kcmax = 2 * qt + 1
                ymm = [None, None]

                def emit_S(kc):
                    out = [None, None]
                    for idx in range(2):
                        j = 2 * pr + idx
                        jc, jo = j // 2, 64 * (j % 2)
                        si = (kc % 2) * 2 + idx
                        w = [kg_done, qg_done] + fw(f"PQ{si}")
                        smm = mm(PQ[si][:, 0:256],
                                 vK[jc][jo:jo + 64, kc * 128:(kc + 1) * 128],
                                 vqg[jc][jo:jo + 64, qt * 256:(qt + 1) * 256],
                                 True, True, inc=True, wait=w)
                        out[idx] = smm
                    if kc >= 2 * qt:
                        msk = s_mka if kc == 2 * qt else s_mkb
                        for idx in range(2):
                            si = (kc % 2) * 2 + idx
                            out[idx] = vtt(PQ[si][:, 0:256], PQ[si][:, 0:256],
                                           msk, ALU.add, inc=True,
                                           wait=[out[idx], rrow_init])
                    return out

                sprods = emit_S(0)
                for kc in range(kcmax + 1):
                    exs = [None, None]
                    for idx in range(2):
                        si = (kc % 2) * 2 + idx
                        pi = (kc % 2) * 2 + idx
                        w = [sprods[idx]] + ([pt_free[pi]] if pt_free[pi] else [])
                        exs[idx] = sact(s_pt[:, pi * 256:(pi + 1) * 256],
                                        PQ[si][:, 0:256], AF.Exp, scale=0.125,
                                        inc=True, wait=w)
                        mark(f"PQ{si}", exs[idx])
                    if kc < kcmax:
                        sprods = emit_S(kc + 1)
                    for idx in range(2):
                        j = 2 * pr + idx
                        pi = (kc % 2) * 2 + idx
                        w = [exs[idx], vg_done]
                        if kc == 0 and py_free[idx] is not None:
                            w.append(py_free[idx])
                        ymm[idx] = mm(pY[idx][0:65, 0:256],
                                      vV[kc][:, 65 * j:65 * j + 65],
                                      s_pt[:, pi * 256:(pi + 1) * 256],
                                      kc == 0, kc == kcmax, inc=True, wait=w)
                        pt_free[pi] = ymm[idx]
                for idx in range(2):
                    j = 2 * pr + idx
                    h_ = 4 * g_ + j
                    ys = idx
                    w = [ymm[idx]] + ([yraw_free[ys]] if yraw_free[ys] else [])
                    cpy = vcp(s_yraw[0:65, 256 * ys:256 * (ys + 1)],
                              pY[idx][0:65, 0:256], inc=True, wait=w)
                    rc = vrecip(s_rrow[64:65, :], pY[idx][64:65, 0:256], inc=True,
                                wait=[ymm[idx]] + ([pb_free] if pb_free else []))
                    py_free[idx] = rc
                    bc = mm(pB[0:64, :], s_e64[:, :], s_rrow[:, :], True, True,
                            inc=True, wait=[rc])
                    w = [bc, cpy] + ([yst_free[ys]] if yst_free[ys] else [])
                    yn = vtt(s_yst[0:64, 256 * ys:256 * (ys + 1)],
                             s_yraw[0:64, 256 * ys:256 * (ys + 1)], pB[0:64, :],
                             ALU.mult, inc=True, wait=w)
                    pb_free = yn
                    yraw_free[ys] = yn
                    last_store = p.store(
                        d_y[64 * h_:64 * (h_ + 1), 256 * qt:256 * (qt + 1)],
                        s_yst[0:64, 256 * ys:256 * (ys + 1)], wait=[yn])
                    yst_free[ys] = last_store

    u_done = None
    for q5 in range(NQ5):
        sl = slice(q5 * 512, (q5 + 1) * 512)
        w = [pb_free, last_store] if q5 == 0 else u_done
        lx = ly = None
        for c_ in range(DC):
            lx = p.load(vxs[c_], d_xf[c_ * 128:(c_ + 1) * 128, sl], wait=w)
            ly = p.load(vys[c_], d_y[c_ * 128:(c_ + 1) * 128, sl])
            w = []
        u_done = ln_tile(vys, [u[:, sl] for u in vu], add=vxs,
                         src_wait=[lx, ly])
    tap("u", vu[0], 128, T, wait=u_done)

    dw1 = sact(s_rdw[:], s_tdw[:], AF.Tanh, scale=4.0, inc=True)
    p.drain("s")
    dw2 = sact(s_tdw[:], s_rdw[:], AF.Exp, scale=0.5 * LN2, inc=True)
    p.drain("s")
    dw3 = sact(s_rdw[:], s_rdw[:], AF.Exp, scale=-0.5 * LN2, inc=True)

    def cascade(am_ap, out_ap, base_k, terms, wait0):
        n = len(terms) + 1
        vts(ck_sl[0], am_ap, 0.0, base_k, ALU.is_gt, ALU.mult, wait=wait0)
        for i, (thr, wgt) in enumerate(terms):
            vts(ck_sl[i + 1], am_ap, thr, wgt, ALU.is_ge, ALU.mult)
        p.drain("v")
        width = n
        while width > 1:
            half = width // 2
            for i in range(half):
                vtt(ck_sl[i], ck_sl[i], ck_sl[width - 1 - i], ALU.add)
            width = width - half
            p.drain("v")
        return vts(out_ap, ck_sl[0], 1.0, None, ALU.add, inc=True)

    def quant_batch4(src4, dst4, tw_ap, rw_ap, wait0, krange=(KMIN, KMAX)):
        am4 = vred_absmax(
            s_am[:].rearrange("p (t b) -> p t b", b=32).unsqueeze(3),
            src4.rearrange("p (t b e) -> p t b e", b=32, e=32),
            inc=True, wait=wait0)
        p.drain("v")
        kmin, kmax = krange
        terms = [(2.0 ** k, 2.0 ** (k - 3)) for k in range(kmin + 1, kmax + 1)]
        c1 = cascade(s_am[:], s_sc[:], 2.0 ** (kmin - 2) - 1.0, terms, [am4])
        p.drain("v")
        vrecip(s_rsc[:], s_sc[:])
        p.drain("v")
        if tw_ap.shape[1] == 1:
            vts(s_sc[:], s_sc[:], tw_ap, None, ALU.mult, wait=[cd1, cd2])
            vts(s_rsc[:], s_rsc[:], rw_ap, None, ALU.mult)
        else:
            vtt(s_sc[:], s_sc[:], tw_ap, ALU.mult, wait=[dw2, dw3])
            vtt(s_rsc[:], s_rsc[:], rw_ap, ALU.mult)
        p.drain("v")
        xsm = vtt(src4.rearrange("p (t b e) -> p t b e", b=32, e=32),
                  src4.rearrange("p (t b e) -> p t b e", b=32, e=32),
                  s_rsc[:].rearrange("p (t b) -> p t b", b=32).unsqueeze(3)
                  .broadcast_to([128, 4, 32, 32]), ALU.mult, inc=True)
        last = None
        for t_ in range(4):
            xs = src4[:, t_ * 1024:(t_ + 1) * 1024]
            xq = dst4[:, t_ * 1024:(t_ + 1) * 1024]
            w_s = [xsm] if t_ == 0 else [qsgn]
            sg1 = sact(tmp1[:], xs, AF.Sign, inc=True, wait=w_s)
            sg2 = sact(tmp2[:], xs, AF.Abs, inc=True)
            vts(tmp3[:], tmp2[:], 0.25, 0.5, ALU.is_gt, ALU.mult, wait=[sg2])
            vts(tmp4[:], tmp2[:], 0.75, 0.5, ALU.is_ge, ALU.mult)
            vtt(tmp3[:], tmp3[:], tmp4[:], ALU.add)
            vts(tmp4[:], tmp2[:], 1.25, 0.5, ALU.is_gt, ALU.mult)
            vtt(tmp3[:], tmp3[:], tmp4[:], ALU.add)
            vts(tmp4[:], tmp2[:], 1.75, 0.5, ALU.is_ge, ALU.mult)
            vtt(tmp3[:], tmp3[:], tmp4[:], ALU.add)
            vts(tmp4[:], tmp2[:], 2.5, 1.0, ALU.is_gt, ALU.mult)
            vtt(tmp3[:], tmp3[:], tmp4[:], ALU.add)
            vts(tmp4[:], tmp2[:], 3.5, 1.0, ALU.is_ge, ALU.mult)
            vtt(tmp3[:], tmp3[:], tmp4[:], ALU.add)
            vts(tmp4[:], tmp2[:], 5.0, 2.0, ALU.is_gt, ALU.mult)
            vtt(tmp3[:], tmp3[:], tmp4[:], ALU.add)
            qsgn = vtt(xq, tmp3[:], tmp1[:], ALU.mult, inc=True)
            last = vtt(xq.rearrange("p (b e) -> p b e", e=32),
                       xq.rearrange("p (b e) -> p b e", e=32),
                       s_sc[:, t_ * 32:(t_ + 1) * 32].unsqueeze(2)
                       .broadcast_to([128, 32, 32]), ALU.mult,
                       inc=(t_ == 3))
        return last

    wq_done = None
    for wb in range(2):
        lw = None
        for i in range(4):
            oc = wb * 4 + i
            w = u_done if (wb == 0 and i == 0) else []
            if wb == 1 and i == 0:
                w = [wq_done]
            lw = p.load(qw_src[i], d_wffn[oc * 128:(oc + 1) * 128, :], wait=w)
        qd = quant_batch4(arD[:, 0:4096], arD[:, 0:4096],
                          s_tdw[:, wb * 128:(wb + 1) * 128],
                          s_rdw[:, wb * 128:(wb + 1) * 128],
                          [lw, dw2, dw3] + (u_done if wb == 0 else []),
                          krange=(-11, 1))
        if wb == 0:
            tap("wq0", arD[:, 0:1024], 128, 1024, wait=[qd])
            tap("wam", s_am[:], 128, 128, wait=[qd])
            tap("wsc", s_sc[:], 128, 128, wait=[qd])
            tap("wrsc", s_rsc[:], 128, 128, wait=[qd])
            tap("wtdw", s_tdw[:, 0:128], 128, 128, wait=[qd])
        for i in range(4):
            oc = wb * 4 + i
            for dc_ in range(DC):
                w = [qd] if dc_ == 0 else []
                w += fw(f"PQ{pg_i[0]}")
                tm = mm(pG[pg_i[0]][:, 0:256],
                        arD[:, i * 1024 + dc_ * 128: i * 1024 + (dc_ + 1) * 128],
                        s_id2[:], True, True, inc=True, wait=w)
                wq_done = scp(vwqT[dc_][:, oc * 128:(oc + 1) * 128],
                              pG[pg_i[0]][:, 0:128], inc=True, wait=[tm])
                mark(f"PQ{pg_i[0]}", wq_done)
                pg_i[0] ^= 1
    tap("wqT", vwqT[0], 128, C, wait=[wq_done])

    out_stores = []
    tpq = [2]
    xq_marks = {}
    st_marks = {}

    def emit_transp(src_fn, dst_fn, wait_first):
        done = None
        for t_ in range(4):
            for dc_ in range(DC):
                w = list(wait_first) if (t_ == 0 and dc_ == 0) else []
                i = tpq[0]
                w += fw(f"PQ{i}")
                tm = mm(PQ[i][:, 0:256], src_fn(t_, dc_), s_id2[:],
                        True, True, inc=True, wait=w)
                done = scp(dst_fn(t_, dc_), PQ[i][:, 0:128], inc=True, wait=[tm])
                mark(f"PQ{i}", done)
                tpq[0] = 2 + ((tpq[0] - 2 + 1) % 2)
        return done

    def emit_ffn(q5):
        xsl = slice((q5 % 2) * 512, (q5 % 2) * 512 + 512)
        vfa = vfab[q5 % 2]
        f_done = None
        for oc in range(DC):
            for dc_ in range(DC):
                w = []
                if dc_ == 0:
                    w = list(fw(f"PQ{pg_i[0]}"))
                    if oc == 0:
                        w += [xq_marks[q5], wq_done]
                        if q5 - 2 in st_marks:
                            w.append(st_marks[q5 - 2])
                mmn = mm(PQ[pg_i[0]][:, :], vwqT[dc_][:, oc * 128:(oc + 1) * 128],
                         vxq2[dc_][:, xsl], dc_ == 0, dc_ == DC - 1,
                         inc=(dc_ == DC - 1), wait=w)
            w2 = [mmn]
            if oc == 0 and q5 - 2 in st_marks:
                w2.append(st_marks[q5 - 2])
            f_done = sact(vfa[oc], PQ[pg_i[0]][:, :], AF.Identity,
                          bias=s_bias[:, oc:oc + 1], inc=True, wait=w2)
            mark(f"PQ{pg_i[0]}", f_done)
            pg_i[0] ^= 1
        return f_done

    def emit_lnf(q5, f_done):
        sl = slice(q5 * 512, (q5 + 1) * 512)
        vfa = vfab[q5 % 2]
        c4 = ln_tile(vfa, vfa, add=[u[:, sl] for u in vu], src_wait=[f_done])
        if q5 == 0:
            tap("f0", vfa[0], 128, 512, wait=c4)
        last = None
        for c_ in range(DC):
            last = p.store(d_out[c_ * 128:(c_ + 1) * 128, sl], vfa[c_],
                           wait=c4)
            out_stores.append(last)
        st_marks[q5] = last
        return c4

    def emit_quant_stage(q5, w0):
        sl = slice(q5 * 512, (q5 + 1) * 512)
        h2_done = ln_tile([u[:, sl] for u in vu], qw_h2f, src_wait=w0)
        tp_done = emit_transp(
            lambda t_, dc_: qw_h2f[dc_][:, t_ * 128:(t_ + 1) * 128],
            lambda t_, dc_: arD[:, t_ * 1024 + dc_ * 128: t_ * 1024 + (dc_ + 1) * 128],
            h2_done)
        return tp_done

    f_prev = None
    for q5 in range(NQ5):
        sl = slice(q5 * 512, (q5 + 1) * 512)
        w0 = [wq_done] if q5 == 0 else []
        tp_done = emit_quant_stage(q5, w0)
        if q5 >= 1:
            f_prev = emit_ffn(q5 - 1)
        qd = quant_batch4(arD[:, 0:4096], arD[:, 0:4096],
                          s_tda[:, 0:1], s_tda[:, 1:2], [tp_done],
                          krange=(-6, 4))
        xq_marks[q5] = emit_transp(
            lambda t_, dc_: arD[:, t_ * 1024 + dc_ * 128: t_ * 1024 + (dc_ + 1) * 128],
            lambda t_, dc_, q=q5: vxq2[dc_][:, (q % 2) * 512 + t_ * 128:
                                            (q % 2) * 512 + (t_ + 1) * 128],
            [qd])
        if q5 >= 1:
            emit_lnf(q5 - 1, f_prev)
    f_prev = emit_ffn(NQ5 - 1)
    emit_lnf(NQ5 - 1, f_prev)

    p.ops.append(("y", lambda h, S, n=p.dcnt["st"] * 16: h.wait_ge(S["st"], n)))

    S = {}
    S["ld"] = nc.alloc_semaphore("sem_ld")
    S["st"] = nc.alloc_semaphore("sem_st")
    for e in ("t", "v", "s", "g"):
        S[e] = nc.alloc_semaphore(f"sem_{e}")

    with nc.Block() as blk:
        @blk.tensor
        def _(eng):
            for e, fn in p.ops:
                if e == "t":
                    fn(eng, S)

        @blk.vector
        def _(eng):
            for e, fn in p.ops:
                if e == "v":
                    fn(eng, S)

        @blk.scalar
        def _(eng):
            for e, fn in p.ops:
                if e == "s":
                    fn(eng, S)

        @blk.gpsimd
        def _(eng):
            for e, fn in p.ops:
                if e == "g":
                    fn(eng, S)

        @blk.sync
        def _(eng):
            for e, fn in p.ops:
                if e == "y":
                    fn(eng, S)

    return tap_aps


def _in_map(x_b, W_dq, W_uq, W_dkv, W_uk, W_uv, W_o, Wf, bias, dw, da):
    cc = np.ascontiguousarray
    return {
        "xf": cc(x_b.T),
        "wdqT": cc(W_dq.T),
        "wuqT": cc(W_uq.T),
        "wdkvT": cc(W_dkv.T),
        "wuq": cc(W_uq),
        "wuk": cc(W_uk),
        "wuv": cc(W_uv),
        "wdq": cc(W_dq),
        "woT": cc(W_o.T),
        "wffn": cc(Wf),
        "biasr": cc(bias.reshape(DC, 128).T),
        "dw": cc(dw),
        "da": cc(da.reshape(1, 1)),
    }


def kernel(L0_attn_x, L0_attn_W_dq, L0_attn_W_uq, L0_attn_W_dkv, L0_attn_W_uk,
           L0_attn_W_uv, L0_attn_W_o, L0_ffn_weight, L0_ffn_bias,
           L0_ffn_delta_w, L0_ffn_delta_act):
    x = np.asarray(L0_attn_x, dtype=np.float32)
    args = [np.asarray(a, dtype=np.float32) for a in
            (L0_attn_W_dq, L0_attn_W_uq, L0_attn_W_dkv, L0_attn_W_uk,
             L0_attn_W_uv, L0_attn_W_o, L0_ffn_weight, L0_ffn_bias,
             L0_ffn_delta_w, L0_ffn_delta_act)]
    nc = bass.Bass("TRN2", target_bir_lowering=False, debug=False)
    build(nc, taps=tuple(DEBUG_TAPS))
    in_maps = [_in_map(x[b], *args) for b in range(8)]
    res = run_bass_kernel_spmd(nc, in_maps, core_ids=list(range(8)))
    out = np.stack([np.ascontiguousarray(r["outf"].T) for r in res.results])
    if DEBUG_TAPS:
        kernel.last_results = res
    return out


# version 11
# speedup vs baseline: 1.0059x; 1.0028x over previous
import numpy as np

import concourse.bass as bass
import concourse.mybir as mybir
from concourse.bass_utils import run_bass_kernel_spmd

F32 = mybir.dt.float32
AF = mybir.ActivationFunctionType
ALU = mybir.AluOpType
AX = mybir.AxisListType

C = 1024
T = 2048
QL = 512
KVL = 512
NH = 16
HS = 64
NG = 4
DC = C // 128
LC = KVL // 128
MC = QL // 128
NQT = T // 256
NQ5 = T // 512
NTK = T // 128
LN2 = float(np.log(2.0))
NEG = -1.0e30
KMIN, KMAX = -14, 5

DEBUG_TAPS = []


class Prog:

    ENGS = ("t", "v", "s", "g", "y")

    def __init__(self):
        self.ops = []
        self.cnt = {e: 0 for e in self.ENGS}
        self.dcnt = {"ld": 0, "st": 0}
        self._lastwait = {e: {} for e in self.ENGS}

    def _wait(self, eng, key, val):
        if val <= 0:
            return
        lw = self._lastwait[eng]
        if lw.get(key, -1) >= val:
            return
        lw[key] = val
        self.ops.append((eng, lambda h, S, k=key, v=val: h.wait_ge(S[k], v)))

    def _chk(self, key, val):
        lim = self.cnt[key] if key in self.cnt else self.dcnt[key] * 16
        assert val <= lim, f"wait on future count {key}={val} (now {lim})"

    def op(self, eng, build, inc=False, wait=()):
        for (k, v) in wait:
            self._chk(k, v)
            self._wait(eng, k, v)
        if inc:
            self.cnt[eng] += 1

            def fn(h, S, b=build, e=eng):
                b(h, S).then_inc(S[e], 1)
            self.ops.append((eng, fn))
            return (eng, self.cnt[eng])
        self.ops.append((eng, lambda h, S, b=build: b(h, S)))
        return None

    def drain(self, eng):
        self.ops.append((eng, lambda h, S: h.drain()))

    def load(self, out_ap, in_ap, wait=()):
        for (k, v) in wait:
            self._chk(k, v)
            self._wait("y", k, v)
        self.dcnt["ld"] += 1
        self.ops.append(("y", lambda h, S, o=out_ap, i=in_ap:
                         h.dma_start(out=o, in_=i).then_inc(S["ld"], 16)))
        return ("ld", self.dcnt["ld"] * 16)

    def store(self, out_ap, in_ap, wait=()):
        for (k, v) in wait:
            self._chk(k, v)
            self._wait("g", k, v)
        self.dcnt["st"] += 1
        self.ops.append(("g", lambda h, S, o=out_ap, i=in_ap:
                         h.dma_start(out=o, in_=i).then_inc(S["st"], 16)))
        return ("st", self.dcnt["st"] * 16)


def build(nc: bass.Bass, taps=()):
    d_xf = nc.dram_tensor("xf", [C, T], F32, kind="ExternalInput").ap()
    d_wdqT = nc.dram_tensor("wdqT", [C, QL], F32, kind="ExternalInput").ap()
    d_wuqT = nc.dram_tensor("wuqT", [QL, C], F32, kind="ExternalInput").ap()
    d_wdkvT = nc.dram_tensor("wdkvT", [C, KVL], F32, kind="ExternalInput").ap()
    d_wuq = nc.dram_tensor("wuq", [C, QL], F32, kind="ExternalInput").ap()
    d_wuk = nc.dram_tensor("wuk", [C, KVL], F32, kind="ExternalInput").ap()
    d_wuv = nc.dram_tensor("wuv", [C, KVL], F32, kind="ExternalInput").ap()
    d_wdq = nc.dram_tensor("wdq", [QL, C], F32, kind="ExternalInput").ap()
    d_woT = nc.dram_tensor("woT", [C, C], F32, kind="ExternalInput").ap()
    d_wffn = nc.dram_tensor("wffn", [C, C], F32, kind="ExternalInput").ap()
    d_biasr = nc.dram_tensor("biasr", [128, DC], F32, kind="ExternalInput").ap()
    d_dw = nc.dram_tensor("dw", [C, 32], F32, kind="ExternalInput").ap()
    d_da = nc.dram_tensor("da", [1, 1], F32, kind="ExternalInput").ap()
    d_out = nc.dram_tensor("outf", [C, T], F32, kind="ExternalOutput").ap()
    d_y = nc.dram_tensor("ydram", [C, T], F32, kind="Internal").ap()

    tap_aps = {}
    for name in taps:
        pass

    arA = nc.alloc_sbuf_tensor("arA", [128, 16448], F32)
    arB = nc.alloc_sbuf_tensor("arB", [128, 16384], F32)
    arC = nc.alloc_sbuf_tensor("arC", [128, 8192], F32)
    arD = nc.alloc_sbuf_tensor("arD", [128, 8192], F32)

    def views(ar, start, n, w):
        return [ar[:, start + i * w: start + (i + 1) * w] for i in range(n)]

    vx = views(arA, 0, DC, T)
    vckv = views(arB, 0, LC, T)
    vqmid = views(arB, LC * T, MC, T)
    vkT = views(arC, 0, LC, C)
    vve = views(arC, LC * C, LC, C)
    woT_v = views(arB, 0, DC, C)
    vB = views(arB, DC * C, MC, KVL)
    wuq_v = views(arD, 0, DC, QL)
    wuk_v = views(arD, DC * QL, DC, KVL)
    wdq_v = views(arD, 0, MC, C)
    wuv_v = views(arD, MC * C, DC, KVL)
    wdkvT_v = views(arD, 0, DC, KVL)
    wdqT_v = views(arD, DC * KVL, DC, QL)

    vK = views(arA, 0, 2, T)
    vV = views(arA, 2 * T, NTK, 260)
    vV3 = arA[:, 2 * T: 2 * T + NTK * 260].rearrange("p (t w) -> p t w", w=260)
    vqg = views(arA, 2 * T + NTK * 260, 2, T)
    vwuqT = views(arA, 4 * T + NTK * 260, MC, C)
    vvaug = views(arD, 0, LC, 260)
    s_pt = arD[:, 1040:2064]
    s_yraw = arD[:, 2064:2576]
    s_yst = arD[:, 2576:3088]
    s_rrow = arD[:, 3088:3344]
    s_mka = arD[:, 3344:3600]
    s_mkb = arD[:, 3600:3856]

    vu = views(arB, 0, DC, T)
    vxq2 = views(arA, 0, DC, 1024)
    vfab = [views(arA, DC * 1024, DC, 512),
            views(arA, DC * 1024 + DC * 512, DC, 512)]
    vwqT = views(arC, 0, DC, C)
    vxs = views(arD, 0, DC, 512)
    vys = views(arD, DC * 512, DC, 512)
    qw_src = views(arD, 0, 4, 1024)
    qw_h2f = views(arD, 4096, DC, 512)
    CKB = 4096
    ck_sl = [arD[:, CKB + i * 128: CKB + (i + 1) * 128] for i in range(21)]
    tmp1 = arD[:, 4096:5120]
    tmp2 = arD[:, 5120:6144]
    tmp3 = arD[:, 6144:7168]
    tmp4 = arD[:, 7168:8192]
    vf = views(arD, 0, DC, 512)
    vo = views(arD, DC * 512, DC, 512)

    s_ones = nc.alloc_sbuf_tensor("s_ones", [128, 1], F32)
    s_onesr = nc.alloc_sbuf_tensor("s_onesr", [1, 128], F32)
    s_id2 = nc.alloc_sbuf_tensor("s_id2", [128, 256], F32)
    s_e64 = nc.alloc_sbuf_tensor("s_e64", [128, 64], F32)
    s_lnt = nc.alloc_sbuf_tensor("s_lnt", [128, 512], F32)
    s_lnq = nc.alloc_sbuf_tensor("s_lnq", [128, 512], F32)
    s_lns = nc.alloc_sbuf_tensor("s_lns", [128, 512], F32)
    s_rows = nc.alloc_sbuf_tensor("s_rows", [1, 1024], F32)
    s_eps = nc.alloc_sbuf_tensor("s_eps", [128, 1], F32)
    s_am = nc.alloc_sbuf_tensor("s_am", [128, 128], F32)
    s_sc = nc.alloc_sbuf_tensor("s_sc", [128, 128], F32)
    s_rsc = nc.alloc_sbuf_tensor("s_rsc", [128, 128], F32)
    s_bias = nc.alloc_sbuf_tensor("s_bias", [128, DC], F32)
    s_tdw = nc.alloc_sbuf_tensor("s_tdw", [128, DC * 32], F32)
    s_rdw = nc.alloc_sbuf_tensor("s_rdw", [128, DC * 32], F32)
    s_da = nc.alloc_sbuf_tensor("s_da", [1, 2], F32)
    s_tda = nc.alloc_sbuf_tensor("s_tda", [128, 2], F32)

    PQ = [nc.alloc_psum_tensor(f"PQ{i}", [128, 512], F32) for i in range(4)]
    pY = [nc.alloc_psum_tensor("pY0", [128, 512], F32),
          nc.alloc_psum_tensor("pY1", [128, 512], F32)]
    pG = [PQ[0], PQ[1]]
    pB = nc.alloc_psum_tensor("pB", [64, 256], F32)
    pH = nc.alloc_psum_tensor("pH", [128, 512], F32)

    p = Prog()
    free = {}

    def fw(res):
        v = free.get(res)
        return list(v) if v else []

    def mark(res, mk):
        if mk is None:
            return
        free[res] = list(mk) if isinstance(mk, list) else [mk]

    def mm(out, lhsT, rhs, start, stop, inc=False, wait=()):
        return p.op("t", lambda h, S, o=out, l=lhsT, r=rhs, a=start, b=stop:
                    h.matmul(o, l, r, start=a, stop=b), inc=inc, wait=wait)

    def vcp(out, in_, inc=False, wait=()):
        return p.op("v", lambda h, S, o=out, i=in_: h.tensor_copy(o, i),
                    inc=inc, wait=wait)

    def vtt(out, a, b, alu, inc=False, wait=()):
        return p.op("v", lambda h, S, o=out, x=a, yy=b, u=alu:
                    h.tensor_tensor(out=o, in0=x, in1=yy, op=u), inc=inc, wait=wait)

    def vts(out, in_, s1, s2, op0, op1=None, inc=False, wait=()):
        if s2 is None:
            return p.op("v", lambda h, S, o=out, i=in_, a=s1, u0=op0:
                        h.tensor_scalar(out=o, in0=i, scalar1=a, scalar2=None,
                                        op0=u0), inc=inc, wait=wait)
        return p.op("v", lambda h, S, o=out, i=in_, a=s1, b=s2, u0=op0, u1=op1:
                    h.tensor_scalar(out=o, in0=i, scalar1=a, scalar2=b,
                                    op0=u0, op1=u1), inc=inc, wait=wait)

    def vred_absmax(out, in_, inc=False, wait=()):
        return p.op("v", lambda h, S, o=out, i=in_:
                    h.tensor_reduce(o, i, axis=AX.X, op=ALU.max,
                                    apply_absolute_value=True), inc=inc, wait=wait)

    def vrecip(out, in_, inc=False, wait=()):
        return p.op("v", lambda h, S, o=out, i=in_: h.reciprocal(o, i),
                    inc=inc, wait=wait)

    def vmemset(ap, val, inc=False, wait=()):
        return p.op("v", lambda h, S, a=ap, x=val: h.memset(a, x),
                    inc=inc, wait=wait)

    def sact(out, in_, func, bias=0.0, scale=1.0, inc=False, wait=()):
        return p.op("s", lambda h, S, o=out, i=in_, f=func, b=bias, sc=scale:
                    h.activation(out=o, in_=i, func=f, bias=b, scale=sc),
                    inc=inc, wait=wait)

    def gp(build, inc=False, wait=()):
        return p.op("g", build, inc=inc, wait=wait)

    def gtt(out, a, b, alu, inc=False, wait=()):
        return p.op("g", lambda h, S, o=out, x=a, yy=b, u=alu:
                    h.tensor_tensor(out=o, in0=x, in1=yy, op=u), inc=inc, wait=wait)

    def scp(out, in_, inc=False, wait=()):
        return p.op("s", lambda h, S, o=out, i=in_: h.copy(o, i),
                    inc=inc, wait=wait)

    def tap(name, src_ap, rows, cols, wait=()):
        if name not in taps:
            return
        dt = nc.dram_tensor(f"tap_{name}", [rows, cols], F32,
                            kind="ExternalOutput").ap()
        tap_aps[name] = dt
        p.store(dt, src_ap, wait=wait)

    gp(lambda h, S: h.memset(s_ones[:], 1.0))
    gp(lambda h, S: h.memset(s_onesr[:], 1.0))
    gp(lambda h, S: h.memset(s_eps[:], 1e-5))
    gp(lambda h, S: h.memset(s_id2[:], 0.0))
    gp(lambda h, S: h.affine_select(
        out=s_id2[:].rearrange("p (a b) -> p a b", b=128),
        in_=s_id2[:].rearrange("p (a b) -> p a b", b=128),
        compare_op=ALU.not_equal, fill=1.0, base=0,
        pattern=[[0, 2], [-1, 128]], channel_multiplier=1))
    gp(lambda h, S: h.memset(s_e64[:], 1.0))
    g_setup = gp(lambda h, S: h.affine_select(
        out=s_e64[:], in_=s_e64[:], compare_op=ALU.is_equal, fill=0.0,
        base=-64, pattern=[[0, 64]], channel_multiplier=1), inc=True)

    ld_x = None
    for c_ in range(DC):
        ld_x = p.load(vx[c_], d_xf[c_ * 128:(c_ + 1) * 128, :])
    ld_wuq = p.load(arD[:, 0:DC * QL].rearrange("p (c m) -> p c m", m=QL), d_wuq.rearrange("(c p) m -> p c m", p=128))
    ld_wuk = p.load(arD[:, DC * QL:2 * DC * QL].rearrange("p (c m) -> p c m", m=KVL),
                    d_wuk.rearrange("(c p) m -> p c m", p=128))
    ld_woT = p.load(arB[:, 0:DC * C].rearrange("p (c m) -> p c m", m=C), d_woT.rearrange("(c p) m -> p c m", p=128))
    p.load(s_bias[:], d_biasr)
    p.load(s_tdw[:].rearrange("p (c m) -> p c m", m=32), d_dw.rearrange("(c p) m -> p c m", p=128))
    ld_small = p.load(s_da[0:1, 0:1], d_da)

    R_SUM, R_SQ = slice(0, 512), slice(512, 1024)

    ln_chain = {"prev": []}

    def ln_tile(src, dst, add=None, src_wait=(), tapname=None):
        gates = []
        prevs = ln_chain["prev"]
        m1 = None
        for c_ in range(DC):
            w = list(src_wait) + fw("pH") + [g_setup] if c_ == 0 else []
            m1 = mm(pH[0:1, :], s_ones[:], src[c_], c_ == 0, c_ == DC - 1,
                    inc=(c_ == DC - 1), wait=w)
        vtt(s_lnq[:], src[0], src[0], ALU.mult,
            wait=list(src_wait) + prevs)
        for c_ in range(1, DC):
            vtt(s_lnt[:], src[c_], src[c_], ALU.mult)
            vtt(s_lnq[:], s_lnq[:], s_lnt[:], ALU.add)
        vq = p.op("v", lambda h, S: h.tensor_scalar(
            out=s_lnq[:], in0=s_lnq[:], scalar1=1.0, scalar2=None,
            op0=ALU.mult), inc=True)
        m2 = mm(pY[0][0:1, :], s_ones[:], s_lnq[:], True, True, inc=True,
                wait=[vq] + fw("pYL"))
        c1 = vcp(s_rows[0:1, R_SUM], pH[0:1, :], inc=True, wait=[m1])
        c2 = vcp(s_rows[0:1, R_SQ], pY[0][0:1, :], inc=True, wait=[m2])
        p.drain("v")
        vm = vts(s_rows[0:1, R_SUM], s_rows[0:1, R_SUM], 1.0 / C, None, ALU.mult,
                 inc=True)
        vts(s_rows[0:1, R_SQ], s_rows[0:1, R_SQ], 1.0 / C, None, ALU.mult)
        p.drain("v")
        vtt(s_lns[0:1, 0:512], s_rows[0:1, R_SUM], s_rows[0:1, R_SUM], ALU.mult)
        p.drain("v")
        vv = vtt(s_rows[0:1, R_SQ], s_rows[0:1, R_SQ], s_lns[0:1, 0:512],
                 ALU.subtract, inc=True)
        sq = sact(s_rows[0:1, R_SQ], s_rows[0:1, R_SQ], AF.Sqrt,
                  bias=s_eps[0:1, :], inc=True, wait=[vv])
        rr = vrecip(s_rows[0:1, R_SQ], s_rows[0:1, R_SQ], inc=True, wait=[sq])
        if tapname is not None and f"{tapname}_rows" in taps:
            gates.append(p.store(
                nc.dram_tensor(f"tap_{tapname}_rows", [1, 1024], F32,
                               kind="ExternalOutput").ap(), s_rows[0:1, :],
                wait=[rr]))
        b1 = mm(pH[:, :], s_onesr[0:1, 0:128], s_rows[0:1, R_SQ], True, True,
                inc=True, wait=[rr, c1])
        b2 = mm(pY[0][:, :], s_onesr[0:1, 0:128], s_rows[0:1, R_SUM], True, True,
                inc=True, wait=[vm, c2])
        e1 = vcp(s_lnt[:], pH[:, :], inc=True, wait=[b1])
        e2 = vcp(s_lns[:], pY[0][:, :], inc=True, wait=[b2])
        mark("pH", e1)
        mark("pYL", e2)
        vlast = None
        for c_ in range(DC):
            vtt(dst[c_], src[c_], s_lns[:], ALU.subtract,
                wait=[e1, e2] if c_ == 0 else [])
            if add is None:
                vlast = vtt(dst[c_], dst[c_], s_lnt[:], ALU.mult,
                            inc=(c_ == DC - 1))
            else:
                vtt(dst[c_], dst[c_], s_lnt[:], ALU.mult)
                vlast = vtt(dst[c_], dst[c_], add[c_], ALU.add,
                            inc=(c_ == DC - 1))
        ln_chain["prev"] = [vlast]
        marks = [vlast]
        if gates:
            marks += gates
        return marks

    def ln_full(srcv, dstv, addv=None, src_wait=(), tapname=None):
        last = None
        for q5 in range(NQ5):
            sl = slice(q5 * 512, (q5 + 1) * 512)
            last = ln_tile([s[:, sl] for s in srcv], [d[:, sl] for d in dstv],
                           add=None if addv is None else [a[:, sl] for a in addv],
                           src_wait=src_wait if q5 == 0 else last,
                           tapname=(tapname if q5 == 0 else
                                    (f"{tapname}s3" if (tapname and q5 == 3) else None)))
        return last

    ln_h1 = ln_full(vx, vx, src_wait=[ld_x], tapname="h1")

    pg_i = [0]

    def proj(out_views, out_w, lhs_tiles, rhs_tiles, nk, first_wait,
             m_range, n_range, lhs_sel, rhs_sel, out_sel):
        done = None
        for mi in m_range:
            for nj in n_range:
                for kk in range(nk):
                    w = []
                    if kk == 0:
                        w = list(fw(f"PQ{pg_i[0]}"))
                        if mi == m_range[0] and nj == n_range[0]:
                            w += list(first_wait)
                    mmn = mm(pG[pg_i[0]][:, 0:out_w], lhs_sel(mi, kk),
                             rhs_sel(nj, kk), kk == 0, kk == nk - 1,
                             inc=(kk == nk - 1), wait=w)
                done = vcp(out_sel(mi, nj), pG[pg_i[0]][:, 0:out_w], inc=True,
                           wait=[mmn])
                mark(f"PQ{pg_i[0]}", done)
                pg_i[0] ^= 1
        return done

    b_done = proj(None, 512, None, None, DC, [ld_wuq, ld_wuk],
                  list(range(MC)), [0],
                  lambda mi, kk: wuq_v[kk][:, mi * 128:(mi + 1) * 128],
                  lambda nj, kk: wuk_v[kk],
                  lambda mi, nj: vB[mi])
    ld_wdq = p.load(arD[:, 0:MC * C].rearrange("p (c m) -> p c m", m=C), d_wdq.rearrange("(c p) m -> p c m", p=128),
                    wait=[b_done])
    kT_done = proj(None, 512, None, None, MC, [ld_wdq],
                   list(range(LC)), list(range(2)),
                   lambda mi, kk: vB[kk][:, mi * 128:(mi + 1) * 128],
                   lambda nj, kk: wdq_v[kk][:, nj * 512:(nj + 1) * 512],
                   lambda mi, nj: vkT[mi][:, nj * 512:(nj + 1) * 512])
    ld_wuv = p.load(arD[:, MC * C:MC * C + DC * KVL].rearrange("p (c m) -> p c m", m=KVL),
                    d_wuv.rearrange("(c p) m -> p c m", p=128), wait=[kT_done])
    ve_done = proj(None, 512, None, None, DC, [ld_wuv, ld_woT],
                   list(range(LC)), list(range(2)),
                   lambda mi, kk: wuv_v[kk][:, mi * 128:(mi + 1) * 128],
                   lambda nj, kk: woT_v[kk][:, nj * 512:(nj + 1) * 512],
                   lambda mi, nj: vve[mi][:, nj * 512:(nj + 1) * 512])
    ld_wdkvT = p.load(arD[:, 0:DC * KVL].rearrange("p (c m) -> p c m", m=KVL),
                      d_wdkvT.rearrange("(c p) m -> p c m", p=128),
                      wait=[kT_done])
    ckv_done = proj(None, 512, None, None, DC, [ld_wdkvT] + ln_h1 + [ve_done],
                    list(range(LC)), list(range(4)),
                    lambda mi, kk: wdkvT_v[kk][:, mi * 128:(mi + 1) * 128],
                    lambda nj, kk: vx[kk][:, nj * 512:(nj + 1) * 512],
                    lambda mi, nj: vckv[mi][:, nj * 512:(nj + 1) * 512])
    ld_wdqT = p.load(arD[:, DC * KVL:2 * DC * KVL].rearrange("p (c m) -> p c m", m=QL),
                     d_wdqT.rearrange("(c p) m -> p c m", p=128), wait=[ve_done])
    qmid_done = proj(None, 512, None, None, DC, [ld_wdqT],
                     list(range(MC)), list(range(4)),
                     lambda mi, kk: wdqT_v[kk][:, mi * 128:(mi + 1) * 128],
                     lambda nj, kk: vx[kk][:, nj * 512:(nj + 1) * 512],
                     lambda mi, nj: vqmid[mi][:, nj * 512:(nj + 1) * 512])
    ld_wuqT = p.load(arA[:, 4 * T + NTK * 260: 4 * T + NTK * 260 + MC * C].rearrange("p (c m) -> p c m", m=C),
                     d_wuqT.rearrange("(c p) m -> p c m", p=128),
                     wait=[ckv_done, qmid_done])

    tap("h1", vx[0], 128, T, wait=ln_h1)
    tap("ckv", vckv[0], 128, T, wait=[ckv_done])
    tap("keffT", vkT[0], 128, C, wait=[kT_done])
    tap("veff", vve[0], 128, C, wait=[ve_done])
    tap("qmid", vqmid[0], 128, T, wait=[qmid_done])

    t1 = sact(s_da[0:1, 1:2], s_da[0:1, 0:1], AF.Tanh, scale=4.0, inc=True,
              wait=[ld_small] + ln_h1)
    p.drain("s")
    t2 = sact(s_rows[0:1, 0:256], s_da[0:1, 1:2].broadcast_to([1, 256]), AF.Exp,
              scale=0.5 * LN2, inc=True)
    p.drain("s")
    t3 = sact(s_rows[0:1, 256:512], s_da[0:1, 1:2].broadcast_to([1, 256]), AF.Exp,
              scale=-0.5 * LN2, inc=True)
    bd1 = mm(pH[:, 0:256], s_onesr[0:1, 0:128], s_rows[0:1, 0:256], True, True,
             inc=True, wait=[t2] + fw("pH"))
    cd1 = vcp(s_tda[:, 0:1], pH[:, 0:1], inc=True, wait=[bd1])
    bd2 = mm(pH[:, 0:256], s_onesr[0:1, 0:128], s_rows[0:1, 256:512], True, True,
             inc=True, wait=[t3, cd1])
    cd2 = vcp(s_tda[:, 1:2], pH[:, 1:2], inc=True, wait=[bd2])
    mark("pH", cd2)

    gp(lambda h, S: h.memset(s_mka[:], 0.0), wait=[qmid_done, ckv_done])
    gp(lambda h, S: h.affine_select(
        out=s_mka[:], in_=s_mka[:], compare_op=ALU.is_ge, fill=NEG,
        base=0, pattern=[[1, 256]], channel_multiplier=-1))
    gp(lambda h, S: h.memset(s_mkb[:], 0.0))
    gp(lambda h, S: h.affine_select(
        out=s_mkb[:], in_=s_mkb[:], compare_op=ALU.is_ge, fill=NEG,
        base=-128, pattern=[[1, 256]], channel_multiplier=-1))
    rrow_init = gp(lambda h, S: h.memset(s_rrow[:], 0.0), inc=True)
    pt_free = [None, None, None, None]
    ps_free = [None, None]
    py_free = [None, None]
    yst_free = [None, None]
    yraw_free = [None, None]
    pb_free = rrow_init
    last_store = None

    for g_ in range(NG):
        kg_done = proj(None, 512, None, None, LC,
                       [kT_done, ckv_done] if g_ == 0 else [],
                       list(range(2)), list(range(4)),
                       lambda mi, kk, g=g_: vkT[kk][:, 256 * g + 128 * mi:
                                                    256 * g + 128 * (mi + 1)],
                       lambda nj, kk: vckv[kk][:, nj * 512:(nj + 1) * 512],
                       lambda mi, nj: vK[mi][:, nj * 512:(nj + 1) * 512])
        for lc_ in range(LC):
            vmemset(vvaug[lc_], 0.0,
                    wait=[qmid_done] if g_ == 0 and lc_ == 0 else [])
        va_done = None
        for lc_ in range(LC):
            for j in range(4):
                va_done = vcp(vvaug[lc_][:, 65 * j:65 * j + 64],
                              vve[lc_][:, 64 * (4 * g_ + j):64 * (4 * g_ + j) + 64],
                              inc=(lc_ == LC - 1 and j == 3),
                              wait=[ve_done] if (g_ == 0 and lc_ == 0 and j == 0)
                              else [])
        vg_done = proj(None, 260, None, None, LC, [va_done],
                       list(range(NTK)), [0],
                       lambda mi, kk: vckv[kk][:, mi * 128:(mi + 1) * 128],
                       lambda nj, kk: vvaug[kk],
                       lambda mi, nj: vV[mi])
        for j in range(4):
            vg_done = vmemset(vV3[:, :, 65 * j + 64:65 * j + 65], 1.0,
                              inc=(j == 3))
        qg_done = proj(None, 512, None, None, MC, [ld_wuqT, qmid_done],
                       list(range(2)), list(range(4)),
                       lambda mi, kk, g=g_: vwuqT[kk][:, 256 * g + 128 * mi:
                                                      256 * g + 128 * (mi + 1)],
                       lambda nj, kk: vqmid[kk][:, nj * 512:(nj + 1) * 512],
                       lambda mi, nj: vqg[mi][:, nj * 512:(nj + 1) * 512])
        if g_ == 0:
            tap("K0", vK[0], 128, T, wait=[kg_done])
            tap("V0", arA[:, 2 * T:2 * T + 2048], 128, 2048, wait=[vg_done])
            tap("q0", vqg[0], 128, T, wait=[qg_done])

        for pr in range(2):
            for qt in range(NQT):
                kcmax = 2 * qt + 1
                ymm = [None, None]

                def emit_S(kc):
                    out = [None, None]
                    for idx in range(2):
                        j = 2 * pr + idx
                        jc, jo = j // 2, 64 * (j % 2)
                        si = (kc % 2) * 2 + idx
                        w = [kg_done, qg_done] + fw(f"PQ{si}")
                        smm = mm(PQ[si][:, 0:256],
                                 vK[jc][jo:jo + 64, kc * 128:(kc + 1) * 128],
                                 vqg[jc][jo:jo + 64, qt * 256:(qt + 1) * 256],
                                 True, True, inc=True, wait=w)
                        out[idx] = smm
                    if kc >= 2 * qt:
                        msk = s_mka if kc == 2 * qt else s_mkb
                        for idx in range(2):
                            si = (kc % 2) * 2 + idx
                            out[idx] = vtt(PQ[si][:, 0:256], PQ[si][:, 0:256],
                                           msk, ALU.add, inc=True,
                                           wait=[out[idx], rrow_init])
                    return out

                sprods = emit_S(0)
                for kc in range(kcmax + 1):
                    exs = [None, None]
                    for idx in range(2):
                        si = (kc % 2) * 2 + idx
                        pi = (kc % 2) * 2 + idx
                        w = [sprods[idx]] + ([pt_free[pi]] if pt_free[pi] else [])
                        exs[idx] = sact(s_pt[:, pi * 256:(pi + 1) * 256],
                                        PQ[si][:, 0:256], AF.Exp, scale=0.125,
                                        inc=True, wait=w)
                        mark(f"PQ{si}", exs[idx])
                    if kc < kcmax:
                        sprods = emit_S(kc + 1)
                    for idx in range(2):
                        j = 2 * pr + idx
                        pi = (kc % 2) * 2 + idx
                        w = [exs[idx], vg_done]
                        if kc == 0 and py_free[idx] is not None:
                            w.append(py_free[idx])
                        ymm[idx] = mm(pY[idx][0:65, 0:256],
                                      vV[kc][:, 65 * j:65 * j + 65],
                                      s_pt[:, pi * 256:(pi + 1) * 256],
                                      kc == 0, kc == kcmax, inc=True, wait=w)
                        pt_free[pi] = ymm[idx]
                for idx in range(2):
                    j = 2 * pr + idx
                    h_ = 4 * g_ + j
                    ys = idx
                    w = [ymm[idx]] + ([yraw_free[ys]] if yraw_free[ys] else [])
                    cpy = vcp(s_yraw[0:65, 256 * ys:256 * (ys + 1)],
                              pY[idx][0:65, 0:256], inc=True, wait=w)
                    rc = vrecip(s_rrow[64:65, :], pY[idx][64:65, 0:256], inc=True,
                                wait=[ymm[idx]] + ([pb_free] if pb_free else []))
                    py_free[idx] = rc
                    bc = mm(pB[0:64, :], s_e64[:, :], s_rrow[:, :], True, True,
                            inc=True, wait=[rc])
                    w = [bc, cpy] + ([yst_free[ys]] if yst_free[ys] else [])
                    yn = vtt(s_yst[0:64, 256 * ys:256 * (ys + 1)],
                             s_yraw[0:64, 256 * ys:256 * (ys + 1)], pB[0:64, :],
                             ALU.mult, inc=True, wait=w)
                    pb_free = yn
                    yraw_free[ys] = yn
                    last_store = p.store(
                        d_y[64 * h_:64 * (h_ + 1), 256 * qt:256 * (qt + 1)],
                        s_yst[0:64, 256 * ys:256 * (ys + 1)], wait=[yn])
                    yst_free[ys] = last_store

    dw1 = sact(s_rdw[:], s_tdw[:], AF.Tanh, scale=4.0, inc=True,
               wait=ln_h1 + [ld_small])
    p.drain("s")
    dw2 = sact(s_tdw[:], s_rdw[:], AF.Exp, scale=0.5 * LN2, inc=True)
    p.drain("s")
    dw3 = sact(s_rdw[:], s_rdw[:], AF.Exp, scale=-0.5 * LN2, inc=True)

    def cascade(am_ap, out_ap, base_k, terms, wait0):
        n = len(terms) + 1
        vts(ck_sl[0], am_ap, 0.0, base_k, ALU.is_gt, ALU.mult, wait=wait0)
        for i, (thr, wgt) in enumerate(terms):
            vts(ck_sl[i + 1], am_ap, thr, wgt, ALU.is_ge, ALU.mult)
        p.drain("v")
        width = n
        while width > 1:
            half = width // 2
            for i in range(half):
                vtt(ck_sl[i], ck_sl[i], ck_sl[width - 1 - i], ALU.add)
            width = width - half
            p.drain("v")
        return vts(out_ap, ck_sl[0], 1.0, None, ALU.add, inc=True)

    def quant_batch4(src4, dst4, tw_ap, rw_ap, wait0, krange=(KMIN, KMAX)):
        am4 = vred_absmax(
            s_am[:].rearrange("p (t b) -> p t b", b=32).unsqueeze(3),
            src4.rearrange("p (t b e) -> p t b e", b=32, e=32),
            inc=True, wait=wait0)
        p.drain("v")
        kmin, kmax = krange
        terms = [(2.0 ** k, 2.0 ** (k - 3)) for k in range(kmin + 1, kmax + 1)]
        c1 = cascade(s_am[:], s_sc[:], 2.0 ** (kmin - 2) - 1.0, terms, [am4])
        p.drain("v")
        vrecip(s_rsc[:], s_sc[:])
        p.drain("v")
        if tw_ap.shape[1] == 1:
            vts(s_sc[:], s_sc[:], tw_ap, None, ALU.mult, wait=[cd1, cd2])
            vts(s_rsc[:], s_rsc[:], rw_ap, None, ALU.mult)
        else:
            vtt(s_sc[:], s_sc[:], tw_ap, ALU.mult, wait=[dw2, dw3])
            vtt(s_rsc[:], s_rsc[:], rw_ap, ALU.mult)
        p.drain("v")
        xsm = vtt(src4.rearrange("p (t b e) -> p t b e", b=32, e=32),
                  src4.rearrange("p (t b e) -> p t b e", b=32, e=32),
                  s_rsc[:].rearrange("p (t b) -> p t b", b=32).unsqueeze(3)
                  .broadcast_to([128, 4, 32, 32]), ALU.mult, inc=True)
        last = None
        for t_ in range(4):
            xs = src4[:, t_ * 1024:(t_ + 1) * 1024]
            xq = dst4[:, t_ * 1024:(t_ + 1) * 1024]
            w_s = [xsm] if t_ == 0 else [qsgn]
            sg1 = sact(tmp1[:], xs, AF.Sign, inc=True, wait=w_s)
            sg2 = sact(tmp2[:], xs, AF.Abs, inc=True)
            vts(tmp3[:], tmp2[:], 0.25, 0.5, ALU.is_gt, ALU.mult, wait=[sg2])
            vts(tmp4[:], tmp2[:], 0.75, 0.5, ALU.is_ge, ALU.mult)
            vtt(tmp3[:], tmp3[:], tmp4[:], ALU.add)
            vts(tmp4[:], tmp2[:], 1.25, 0.5, ALU.is_gt, ALU.mult)
            vtt(tmp3[:], tmp3[:], tmp4[:], ALU.add)
            vts(tmp4[:], tmp2[:], 1.75, 0.5, ALU.is_ge, ALU.mult)
            vtt(tmp3[:], tmp3[:], tmp4[:], ALU.add)
            vts(tmp4[:], tmp2[:], 2.5, 1.0, ALU.is_gt, ALU.mult)
            vtt(tmp3[:], tmp3[:], tmp4[:], ALU.add)
            vts(tmp4[:], tmp2[:], 3.5, 1.0, ALU.is_ge, ALU.mult)
            vtt(tmp3[:], tmp3[:], tmp4[:], ALU.add)
            vts(tmp4[:], tmp2[:], 5.0, 2.0, ALU.is_gt, ALU.mult)
            vtt(tmp3[:], tmp3[:], tmp4[:], ALU.add)
            qsgn = vtt(xq, tmp3[:], tmp1[:], ALU.mult, inc=True)
            last = vtt(xq.rearrange("p (b e) -> p b e", e=32),
                       xq.rearrange("p (b e) -> p b e", e=32),
                       s_sc[:, t_ * 32:(t_ + 1) * 32].unsqueeze(2)
                       .broadcast_to([128, 32, 32]), ALU.mult,
                       inc=(t_ == 3))
        return last

    out_stores = []
    tpq = [2]
    xq_marks = {}
    st_marks = {}

    def emit_transp(src_fn, dst_fn, wait_first):
        done = None
        for t_ in range(4):
            for dc_ in range(DC):
                w = list(wait_first) if (t_ == 0 and dc_ == 0) else []
                i = tpq[0]
                w += fw(f"PQ{i}")
                tm = mm(PQ[i][:, 0:256], src_fn(t_, dc_), s_id2[:],
                        True, True, inc=True, wait=w)
                done = scp(dst_fn(t_, dc_), PQ[i][:, 0:128], inc=True, wait=[tm])
                mark(f"PQ{i}", done)
                tpq[0] = 2 + ((tpq[0] - 2 + 1) % 2)
        return done

    def emit_ffn(q5):
        xsl = slice((q5 % 2) * 512, (q5 % 2) * 512 + 512)
        vfa = vfab[q5 % 2]
        f_done = None
        for oc in range(DC):
            for dc_ in range(DC):
                w = []
                if dc_ == 0:
                    w = list(fw(f"PQ{pg_i[0]}"))
                    if oc == 0:
                        w += [xq_marks[q5], wq_done]
                        if q5 - 2 in st_marks:
                            w.append(st_marks[q5 - 2])
                mmn = mm(PQ[pg_i[0]][:, :], vwqT[dc_][:, oc * 128:(oc + 1) * 128],
                         vxq2[dc_][:, xsl], dc_ == 0, dc_ == DC - 1,
                         inc=(dc_ == DC - 1), wait=w)
            w2 = [mmn]
            if oc == 0 and q5 - 2 in st_marks:
                w2.append(st_marks[q5 - 2])
            f_done = sact(vfa[oc], PQ[pg_i[0]][:, :], AF.Identity,
                          bias=s_bias[:, oc:oc + 1], inc=True, wait=w2)
            mark(f"PQ{pg_i[0]}", f_done)
            pg_i[0] ^= 1
        return f_done

    def emit_lnf(q5, f_done):
        sl = slice(q5 * 512, (q5 + 1) * 512)
        vfa = vfab[q5 % 2]
        c4 = ln_tile(vfa, vfa, add=[u[:, sl] for u in vu], src_wait=[f_done])
        if q5 == 0:
            tap("f0", vfa[0], 128, 512, wait=c4)
        last = None
        for c_ in range(DC):
            last = p.store(d_out[c_ * 128:(c_ + 1) * 128, sl], vfa[c_],
                           wait=c4)
            out_stores.append(last)
        st_marks[q5] = last
        return c4

    u_marks = {}

    def emit_stage(q5):
        sl = slice(q5 * 512, (q5 + 1) * 512)
        h2_done = ln_tile([u[:, sl] for u in vu], qw_h2f,
                          src_wait=u_marks[q5])
        tp_done = emit_transp(
            lambda t_, dc_: qw_h2f[dc_][:, t_ * 128:(t_ + 1) * 128],
            lambda t_, dc_: arD[:, t_ * 1024 + dc_ * 128: t_ * 1024 + (dc_ + 1) * 128],
            h2_done)
        qd = quant_batch4(arD[:, 0:4096], arD[:, 0:4096],
                          s_tda[:, 0:1], s_tda[:, 1:2], [tp_done],
                          krange=(-6, 4))
        xq_marks[q5] = emit_transp(
            lambda t_, dc_: arD[:, t_ * 1024 + dc_ * 128: t_ * 1024 + (dc_ + 1) * 128],
            lambda t_, dc_, q=q5: vxq2[dc_][:, (q % 2) * 512 + t_ * 128:
                                            (q % 2) * 512 + (t_ + 1) * 128],
            [qd])
        return xq_marks[q5]


    att_end = [pb_free, last_store]
    wq_done = None
    u_done = []
    wb_done = {}
    for q5 in range(NQ5):
        sl = slice(q5 * 512, (q5 + 1) * 512)
        if q5 == 0:
            w = list(att_end)
        else:
            w = list(u_done)
            if q5 == 1 and 0 in wb_done:
                w.append(wb_done[0])
            if q5 - 2 in xq_marks:
                w.append(xq_marks[q5 - 2])
        lx = ly = None
        for c_ in range(DC):
            lx = p.load(vxs[c_], d_xf[c_ * 128:(c_ + 1) * 128, sl], wait=w)
            ly = p.load(vys[c_], d_y[c_ * 128:(c_ + 1) * 128, sl])
            w = []
        u_done = ln_tile(vys, [u[:, sl] for u in vu], add=vxs,
                         src_wait=[lx, ly])
        u_marks[q5] = u_done
        if q5 < 2:
            wb = q5
            lw = None
            for i in range(4):
                oc = wb * 4 + i
                wl = (att_end + u_done) if (wb == 0 and i == 0) else (
                    ([wq_done] + u_done) if (wb == 1 and i == 0) else [])
                lw = p.load(qw_src[i], d_wffn[oc * 128:(oc + 1) * 128, :],
                            wait=wl)
            qd = quant_batch4(arD[:, 0:4096], arD[:, 0:4096],
                              s_tdw[:, wb * 128:(wb + 1) * 128],
                              s_rdw[:, wb * 128:(wb + 1) * 128],
                              [lw, dw2, dw3], krange=(-11, 1))
            if wb == 0:
                tap("wq0", arD[:, 0:1024], 128, 1024, wait=[qd])
                tap("wam", s_am[:], 128, 128, wait=[qd])
                tap("wsc", s_sc[:], 128, 128, wait=[qd])
                tap("wrsc", s_rsc[:], 128, 128, wait=[qd])
                tap("wtdw", s_tdw[:, 0:128], 128, 128, wait=[qd])
            for i in range(4):
                oc = wb * 4 + i
                for dc_ in range(DC):
                    w2 = [qd] if (i == 0 and dc_ == 0) else []
                    w2 += fw(f"PQ{pg_i[0]}")
                    tm = mm(pG[pg_i[0]][:, 0:256],
                            arD[:, i * 1024 + dc_ * 128: i * 1024 + (dc_ + 1) * 128],
                            s_id2[:], True, True, inc=True, wait=w2)
                    wq_done = scp(vwqT[dc_][:, oc * 128:(oc + 1) * 128],
                                  pG[pg_i[0]][:, 0:128], inc=True, wait=[tm])
                    mark(f"PQ{pg_i[0]}", wq_done)
                    pg_i[0] ^= 1
        if q5 < 2:
            wb_done[q5] = wq_done
        if q5 >= 1:
            emit_stage(q5 - 1)
        if q5 >= 2:
            emit_lnf(q5 - 2, emit_ffn(q5 - 2))
    emit_stage(NQ5 - 1)
    for q5f in (NQ5 - 2, NQ5 - 1):
        emit_lnf(q5f, emit_ffn(q5f))
    tap("u", vu[0], 128, T, wait=u_done)
    tap("wqT", vwqT[0], 128, C, wait=[wq_done])

    p.ops.append(("y", lambda h, S, n=p.dcnt["st"] * 16: h.wait_ge(S["st"], n)))

    S = {}
    S["ld"] = nc.alloc_semaphore("sem_ld")
    S["st"] = nc.alloc_semaphore("sem_st")
    for e in ("t", "v", "s", "g"):
        S[e] = nc.alloc_semaphore(f"sem_{e}")

    with nc.Block() as blk:
        @blk.tensor
        def _(eng):
            for e, fn in p.ops:
                if e == "t":
                    fn(eng, S)

        @blk.vector
        def _(eng):
            for e, fn in p.ops:
                if e == "v":
                    fn(eng, S)

        @blk.scalar
        def _(eng):
            for e, fn in p.ops:
                if e == "s":
                    fn(eng, S)

        @blk.gpsimd
        def _(eng):
            for e, fn in p.ops:
                if e == "g":
                    fn(eng, S)

        @blk.sync
        def _(eng):
            for e, fn in p.ops:
                if e == "y":
                    fn(eng, S)

    return tap_aps


def _in_map(x_b, W_dq, W_uq, W_dkv, W_uk, W_uv, W_o, Wf, bias, dw, da):
    cc = np.ascontiguousarray
    return {
        "xf": cc(x_b.T),
        "wdqT": cc(W_dq.T),
        "wuqT": cc(W_uq.T),
        "wdkvT": cc(W_dkv.T),
        "wuq": cc(W_uq),
        "wuk": cc(W_uk),
        "wuv": cc(W_uv),
        "wdq": cc(W_dq),
        "woT": cc(W_o.T),
        "wffn": cc(Wf),
        "biasr": cc(bias.reshape(DC, 128).T),
        "dw": cc(dw),
        "da": cc(da.reshape(1, 1)),
    }


def kernel(L0_attn_x, L0_attn_W_dq, L0_attn_W_uq, L0_attn_W_dkv, L0_attn_W_uk,
           L0_attn_W_uv, L0_attn_W_o, L0_ffn_weight, L0_ffn_bias,
           L0_ffn_delta_w, L0_ffn_delta_act):
    x = np.asarray(L0_attn_x, dtype=np.float32)
    args = [np.asarray(a, dtype=np.float32) for a in
            (L0_attn_W_dq, L0_attn_W_uq, L0_attn_W_dkv, L0_attn_W_uk,
             L0_attn_W_uv, L0_attn_W_o, L0_ffn_weight, L0_ffn_bias,
             L0_ffn_delta_w, L0_ffn_delta_act)]
    nc = bass.Bass("TRN2", target_bir_lowering=False, debug=False)
    build(nc, taps=tuple(DEBUG_TAPS))
    in_maps = [_in_map(x[b], *args) for b in range(8)]
    res = run_bass_kernel_spmd(nc, in_maps, core_ids=list(range(8)))
    out = np.stack([np.ascontiguousarray(r["outf"].T) for r in res.results])
    if DEBUG_TAPS:
        kernel.last_results = res
    return out


# version 12
# speedup vs baseline: 1.0483x; 1.0422x over previous
import numpy as np

import concourse.bass as bass
import concourse.mybir as mybir
from concourse.bass_utils import run_bass_kernel_spmd

F32 = mybir.dt.float32
AF = mybir.ActivationFunctionType
ALU = mybir.AluOpType
AX = mybir.AxisListType

C = 1024
T = 2048
QL = 512
KVL = 512
NH = 16
HS = 64
NG = 4
DC = C // 128
LC = KVL // 128
MC = QL // 128
NQT = T // 256
NQ5 = T // 512
NTK = T // 128
LN2 = float(np.log(2.0))
NEG = -1.0e30
KMIN, KMAX = -14, 5

DEBUG_TAPS = []


class Prog:

    ENGS = ("t", "v", "s", "g", "y")

    def __init__(self):
        self.ops = []
        self.cnt = {e: 0 for e in self.ENGS}
        self.dcnt = {"ld": 0, "st": 0}
        self._lastwait = {e: {} for e in self.ENGS}

    def _wait(self, eng, key, val):
        if val <= 0:
            return
        lw = self._lastwait[eng]
        if lw.get(key, -1) >= val:
            return
        lw[key] = val
        self.ops.append((eng, lambda h, S, k=key, v=val: h.wait_ge(S[k], v)))

    def _chk(self, key, val):
        lim = self.cnt[key] if key in self.cnt else self.dcnt[key] * 16
        assert val <= lim, f"wait on future count {key}={val} (now {lim})"

    def op(self, eng, build, inc=False, wait=()):
        for (k, v) in wait:
            self._chk(k, v)
            self._wait(eng, k, v)
        if inc:
            self.cnt[eng] += 1

            def fn(h, S, b=build, e=eng):
                b(h, S).then_inc(S[e], 1)
            self.ops.append((eng, fn))
            return (eng, self.cnt[eng])
        self.ops.append((eng, lambda h, S, b=build: b(h, S)))
        return None

    def drain(self, eng):
        self.ops.append((eng, lambda h, S: h.drain()))

    def load(self, out_ap, in_ap, wait=()):
        for (k, v) in wait:
            self._chk(k, v)
            self._wait("y", k, v)
        self.dcnt["ld"] += 1
        self.ops.append(("y", lambda h, S, o=out_ap, i=in_ap:
                         h.dma_start(out=o, in_=i).then_inc(S["ld"], 16)))
        return ("ld", self.dcnt["ld"] * 16)

    def store(self, out_ap, in_ap, wait=()):
        for (k, v) in wait:
            self._chk(k, v)
            self._wait("g", k, v)
        self.dcnt["st"] += 1
        self.ops.append(("g", lambda h, S, o=out_ap, i=in_ap:
                         h.dma_start(out=o, in_=i).then_inc(S["st"], 16)))
        return ("st", self.dcnt["st"] * 16)


def build(nc: bass.Bass, taps=()):
    d_xf = nc.dram_tensor("xf", [C, T], F32, kind="ExternalInput").ap()
    d_wdqT = nc.dram_tensor("wdqT", [C, QL], F32, kind="ExternalInput").ap()
    d_wuqT = nc.dram_tensor("wuqT", [QL, C], F32, kind="ExternalInput").ap()
    d_wdkvT = nc.dram_tensor("wdkvT", [C, KVL], F32, kind="ExternalInput").ap()
    d_wuq = nc.dram_tensor("wuq", [C, QL], F32, kind="ExternalInput").ap()
    d_wuk = nc.dram_tensor("wuk", [C, KVL], F32, kind="ExternalInput").ap()
    d_wuv = nc.dram_tensor("wuv", [C, KVL], F32, kind="ExternalInput").ap()
    d_wdq = nc.dram_tensor("wdq", [QL, C], F32, kind="ExternalInput").ap()
    d_woT = nc.dram_tensor("woT", [C, C], F32, kind="ExternalInput").ap()
    d_wffn = nc.dram_tensor("wffn", [C, C], F32, kind="ExternalInput").ap()
    d_biasr = nc.dram_tensor("biasr", [128, DC], F32, kind="ExternalInput").ap()
    d_dw = nc.dram_tensor("dw", [C, 32], F32, kind="ExternalInput").ap()
    d_da = nc.dram_tensor("da", [1, 1], F32, kind="ExternalInput").ap()
    d_out = nc.dram_tensor("outf", [C, T], F32, kind="ExternalOutput").ap()
    d_y = nc.dram_tensor("ydram", [C, T], F32, kind="Internal").ap()

    tap_aps = {}
    for name in taps:
        pass

    arA = nc.alloc_sbuf_tensor("arA", [128, 16448], F32)
    arB = nc.alloc_sbuf_tensor("arB", [128, 16384], F32)
    arC = nc.alloc_sbuf_tensor("arC", [128, 8192], F32)
    arD = nc.alloc_sbuf_tensor("arD", [128, 8192], F32)

    def views(ar, start, n, w):
        return [ar[:, start + i * w: start + (i + 1) * w] for i in range(n)]

    vx = views(arA, 0, DC, T)
    vckv = views(arB, 0, LC, T)
    vqmid = views(arB, LC * T, MC, T)
    vkT = views(arC, 0, LC, C)
    vve = views(arC, LC * C, LC, C)
    woT_v = views(arB, 0, DC, C)
    vB = views(arB, DC * C, MC, KVL)
    wuq_v = views(arD, 0, DC, QL)
    wuk_v = views(arD, DC * QL, DC, KVL)
    wdq_v = views(arD, 0, MC, C)
    wuv_v = views(arD, MC * C, DC, KVL)
    wdkvT_v = views(arD, 0, DC, KVL)
    wdqT_v = views(arD, DC * KVL, DC, QL)

    vK = views(arA, 0, 2, T)
    vV = views(arA, 2 * T, NTK, 260)
    vV3 = arA[:, 2 * T: 2 * T + NTK * 260].rearrange("p (t w) -> p t w", w=260)
    vqg = views(arA, 2 * T + NTK * 260, 2, T)
    vwuqT = views(arA, 4 * T + NTK * 260, MC, C)
    vvaug = views(arD, 0, LC, 260)
    s_pt = arD[:, 1040:2064]
    s_yraw = arD[:, 2064:2576]
    s_yst = arD[:, 2576:3088]
    s_rrow = arD[:, 3088:3344]
    s_mka = arD[:, 3344:3600]
    s_mkb = arD[:, 3600:3856]

    vu = views(arB, 0, DC, T)
    vxq2 = views(arA, 0, DC, 1024)
    vfab = [views(arA, DC * 1024, DC, 512),
            views(arA, DC * 1024 + DC * 512, DC, 512)]
    vwqT = views(arC, 0, DC, C)
    vxs = views(arA, 0, DC, 512)
    vys = views(arA, DC * 512, DC, 512)
    qw_src = views(arD, 0, 4, 1024)
    qw_h2f = views(arD, 4096, DC, 512)
    CKB = 4096
    ck_sl = [arD[:, CKB + i * 128: CKB + (i + 1) * 128] for i in range(21)]
    tmp1 = arD[:, 4096:5120]
    tmp2 = arD[:, 5120:6144]
    tmp3 = arD[:, 6144:7168]
    tmp4 = arD[:, 7168:8192]
    vf = views(arD, 0, DC, 512)
    vo = views(arD, DC * 512, DC, 512)

    s_ones = nc.alloc_sbuf_tensor("s_ones", [128, 1], F32)
    s_onesr = nc.alloc_sbuf_tensor("s_onesr", [1, 128], F32)
    s_id2 = nc.alloc_sbuf_tensor("s_id2", [128, 256], F32)
    s_e64 = nc.alloc_sbuf_tensor("s_e64", [128, 64], F32)
    s_lnt = nc.alloc_sbuf_tensor("s_lnt", [128, 512], F32)
    s_lnq = nc.alloc_sbuf_tensor("s_lnq", [128, 512], F32)
    s_lns = nc.alloc_sbuf_tensor("s_lns", [128, 512], F32)
    s_rows = nc.alloc_sbuf_tensor("s_rows", [1, 1024], F32)
    s_eps = nc.alloc_sbuf_tensor("s_eps", [128, 1], F32)
    s_am = nc.alloc_sbuf_tensor("s_am", [128, 128], F32)
    s_sc = nc.alloc_sbuf_tensor("s_sc", [128, 128], F32)
    s_rsc = nc.alloc_sbuf_tensor("s_rsc", [128, 128], F32)
    s_bias = nc.alloc_sbuf_tensor("s_bias", [128, DC], F32)
    s_tdw = nc.alloc_sbuf_tensor("s_tdw", [128, DC * 32], F32)
    s_rdw = nc.alloc_sbuf_tensor("s_rdw", [128, DC * 32], F32)
    s_da = nc.alloc_sbuf_tensor("s_da", [1, 2], F32)
    s_tda = nc.alloc_sbuf_tensor("s_tda", [128, 2], F32)

    PQ = [nc.alloc_psum_tensor(f"PQ{i}", [128, 512], F32) for i in range(4)]
    pY = [nc.alloc_psum_tensor("pY0", [128, 512], F32),
          nc.alloc_psum_tensor("pY1", [128, 512], F32)]
    pG = [PQ[0], PQ[1]]
    pB = nc.alloc_psum_tensor("pB", [64, 256], F32)
    pH = nc.alloc_psum_tensor("pH", [128, 512], F32)

    p = Prog()
    free = {}

    def fw(res):
        v = free.get(res)
        return list(v) if v else []

    def mark(res, mk):
        if mk is None:
            return
        free[res] = list(mk) if isinstance(mk, list) else [mk]

    def mm(out, lhsT, rhs, start, stop, inc=False, wait=()):
        return p.op("t", lambda h, S, o=out, l=lhsT, r=rhs, a=start, b=stop:
                    h.matmul(o, l, r, start=a, stop=b), inc=inc, wait=wait)

    def vcp(out, in_, inc=False, wait=()):
        return p.op("v", lambda h, S, o=out, i=in_: h.tensor_copy(o, i),
                    inc=inc, wait=wait)

    def vtt(out, a, b, alu, inc=False, wait=()):
        return p.op("v", lambda h, S, o=out, x=a, yy=b, u=alu:
                    h.tensor_tensor(out=o, in0=x, in1=yy, op=u), inc=inc, wait=wait)

    def vts(out, in_, s1, s2, op0, op1=None, inc=False, wait=()):
        if s2 is None:
            return p.op("v", lambda h, S, o=out, i=in_, a=s1, u0=op0:
                        h.tensor_scalar(out=o, in0=i, scalar1=a, scalar2=None,
                                        op0=u0), inc=inc, wait=wait)
        return p.op("v", lambda h, S, o=out, i=in_, a=s1, b=s2, u0=op0, u1=op1:
                    h.tensor_scalar(out=o, in0=i, scalar1=a, scalar2=b,
                                    op0=u0, op1=u1), inc=inc, wait=wait)

    def vred_absmax(out, in_, inc=False, wait=()):
        return p.op("v", lambda h, S, o=out, i=in_:
                    h.tensor_reduce(o, i, axis=AX.X, op=ALU.max,
                                    apply_absolute_value=True), inc=inc, wait=wait)

    def vrecip(out, in_, inc=False, wait=()):
        return p.op("v", lambda h, S, o=out, i=in_: h.reciprocal(o, i),
                    inc=inc, wait=wait)

    def vmemset(ap, val, inc=False, wait=()):
        return p.op("v", lambda h, S, a=ap, x=val: h.memset(a, x),
                    inc=inc, wait=wait)

    def sact(out, in_, func, bias=0.0, scale=1.0, inc=False, wait=()):
        return p.op("s", lambda h, S, o=out, i=in_, f=func, b=bias, sc=scale:
                    h.activation(out=o, in_=i, func=f, bias=b, scale=sc),
                    inc=inc, wait=wait)

    def gp(build, inc=False, wait=()):
        return p.op("g", build, inc=inc, wait=wait)

    def gtt(out, a, b, alu, inc=False, wait=()):
        return p.op("g", lambda h, S, o=out, x=a, yy=b, u=alu:
                    h.tensor_tensor(out=o, in0=x, in1=yy, op=u), inc=inc, wait=wait)

    def scp(out, in_, inc=False, wait=()):
        return p.op("s", lambda h, S, o=out, i=in_: h.copy(o, i),
                    inc=inc, wait=wait)

    def tap(name, src_ap, rows, cols, wait=()):
        if name not in taps:
            return
        dt = nc.dram_tensor(f"tap_{name}", [rows, cols], F32,
                            kind="ExternalOutput").ap()
        tap_aps[name] = dt
        p.store(dt, src_ap, wait=wait)

    gp(lambda h, S: h.memset(s_ones[:], 1.0))
    gp(lambda h, S: h.memset(s_onesr[:], 1.0))
    gp(lambda h, S: h.memset(s_eps[:], 1e-5))
    gp(lambda h, S: h.memset(s_id2[:], 0.0))
    gp(lambda h, S: h.affine_select(
        out=s_id2[:].rearrange("p (a b) -> p a b", b=128),
        in_=s_id2[:].rearrange("p (a b) -> p a b", b=128),
        compare_op=ALU.not_equal, fill=1.0, base=0,
        pattern=[[0, 2], [-1, 128]], channel_multiplier=1))
    gp(lambda h, S: h.memset(s_e64[:], 1.0))
    g_setup = gp(lambda h, S: h.affine_select(
        out=s_e64[:], in_=s_e64[:], compare_op=ALU.is_equal, fill=0.0,
        base=-64, pattern=[[0, 64]], channel_multiplier=1), inc=True)

    ld_x = None
    for c_ in range(DC):
        ld_x = p.load(vx[c_], d_xf[c_ * 128:(c_ + 1) * 128, :])
    ld_wuq = p.load(arD[:, 0:DC * QL].rearrange("p (c m) -> p c m", m=QL), d_wuq.rearrange("(c p) m -> p c m", p=128))
    ld_wuk = p.load(arD[:, DC * QL:2 * DC * QL].rearrange("p (c m) -> p c m", m=KVL),
                    d_wuk.rearrange("(c p) m -> p c m", p=128))
    ld_woT = p.load(arB[:, 0:DC * C].rearrange("p (c m) -> p c m", m=C), d_woT.rearrange("(c p) m -> p c m", p=128))
    p.load(s_bias[:], d_biasr)
    p.load(s_tdw[:].rearrange("p (c m) -> p c m", m=32), d_dw.rearrange("(c p) m -> p c m", p=128))
    ld_small = p.load(s_da[0:1, 0:1], d_da)

    R_SUM, R_SQ = slice(0, 512), slice(512, 1024)

    ln_chain = {"prev": []}

    def ln_tile(src, dst, add=None, src_wait=(), tapname=None):
        gates = []
        prevs = ln_chain["prev"]
        m1 = None
        for c_ in range(DC):
            w = list(src_wait) + fw("pH") + [g_setup] if c_ == 0 else []
            m1 = mm(pH[0:1, :], s_ones[:], src[c_], c_ == 0, c_ == DC - 1,
                    inc=(c_ == DC - 1), wait=w)
        vtt(s_lnq[:], src[0], src[0], ALU.mult,
            wait=list(src_wait) + prevs)
        for c_ in range(1, DC):
            vtt(s_lnt[:], src[c_], src[c_], ALU.mult)
            vtt(s_lnq[:], s_lnq[:], s_lnt[:], ALU.add)
        vq = p.op("v", lambda h, S: h.tensor_scalar(
            out=s_lnq[:], in0=s_lnq[:], scalar1=1.0, scalar2=None,
            op0=ALU.mult), inc=True)
        m2 = mm(pY[0][0:1, :], s_ones[:], s_lnq[:], True, True, inc=True,
                wait=[vq] + fw("pYL"))
        c1 = vcp(s_rows[0:1, R_SUM], pH[0:1, :], inc=True, wait=[m1])
        c2 = vcp(s_rows[0:1, R_SQ], pY[0][0:1, :], inc=True, wait=[m2])
        p.drain("v")
        vm = vts(s_rows[0:1, R_SUM], s_rows[0:1, R_SUM], 1.0 / C, None, ALU.mult,
                 inc=True)
        vts(s_rows[0:1, R_SQ], s_rows[0:1, R_SQ], 1.0 / C, None, ALU.mult)
        p.drain("v")
        vtt(s_lns[0:1, 0:512], s_rows[0:1, R_SUM], s_rows[0:1, R_SUM], ALU.mult)
        p.drain("v")
        vv = vtt(s_rows[0:1, R_SQ], s_rows[0:1, R_SQ], s_lns[0:1, 0:512],
                 ALU.subtract, inc=True)
        sq = sact(s_rows[0:1, R_SQ], s_rows[0:1, R_SQ], AF.Sqrt,
                  bias=s_eps[0:1, :], inc=True, wait=[vv])
        rr = vrecip(s_rows[0:1, R_SQ], s_rows[0:1, R_SQ], inc=True, wait=[sq])
        if tapname is not None and f"{tapname}_rows" in taps:
            gates.append(p.store(
                nc.dram_tensor(f"tap_{tapname}_rows", [1, 1024], F32,
                               kind="ExternalOutput").ap(), s_rows[0:1, :],
                wait=[rr]))
        b1 = mm(pH[:, :], s_onesr[0:1, 0:128], s_rows[0:1, R_SQ], True, True,
                inc=True, wait=[rr, c1])
        b2 = mm(pY[0][:, :], s_onesr[0:1, 0:128], s_rows[0:1, R_SUM], True, True,
                inc=True, wait=[vm, c2])
        e1 = vcp(s_lnt[:], pH[:, :], inc=True, wait=[b1])
        e2 = vcp(s_lns[:], pY[0][:, :], inc=True, wait=[b2])
        mark("pH", e1)
        mark("pYL", e2)
        vlast = None
        for c_ in range(DC):
            vtt(dst[c_], src[c_], s_lns[:], ALU.subtract,
                wait=[e1, e2] if c_ == 0 else [])
            if add is None:
                vlast = vtt(dst[c_], dst[c_], s_lnt[:], ALU.mult,
                            inc=(c_ == DC - 1))
            else:
                vtt(dst[c_], dst[c_], s_lnt[:], ALU.mult)
                vlast = vtt(dst[c_], dst[c_], add[c_], ALU.add,
                            inc=(c_ == DC - 1))
        ln_chain["prev"] = [vlast]
        marks = [vlast]
        if gates:
            marks += gates
        return marks

    def ln_full(srcv, dstv, addv=None, src_wait=(), tapname=None):
        last = None
        for q5 in range(NQ5):
            sl = slice(q5 * 512, (q5 + 1) * 512)
            last = ln_tile([s[:, sl] for s in srcv], [d[:, sl] for d in dstv],
                           add=None if addv is None else [a[:, sl] for a in addv],
                           src_wait=src_wait if q5 == 0 else last,
                           tapname=(tapname if q5 == 0 else
                                    (f"{tapname}s3" if (tapname and q5 == 3) else None)))
        return last

    ln_h1 = ln_full(vx, vx, src_wait=[ld_x], tapname="h1")

    pg_i = [0]

    def proj(out_views, out_w, lhs_tiles, rhs_tiles, nk, first_wait,
             m_range, n_range, lhs_sel, rhs_sel, out_sel):
        done = None
        for mi in m_range:
            for nj in n_range:
                for kk in range(nk):
                    w = []
                    if kk == 0:
                        w = list(fw(f"PQ{pg_i[0]}"))
                        if mi == m_range[0] and nj == n_range[0]:
                            w += list(first_wait)
                    mmn = mm(pG[pg_i[0]][:, 0:out_w], lhs_sel(mi, kk),
                             rhs_sel(nj, kk), kk == 0, kk == nk - 1,
                             inc=(kk == nk - 1), wait=w)
                done = vcp(out_sel(mi, nj), pG[pg_i[0]][:, 0:out_w], inc=True,
                           wait=[mmn])
                mark(f"PQ{pg_i[0]}", done)
                pg_i[0] ^= 1
        return done

    b_done = proj(None, 512, None, None, DC, [ld_wuq, ld_wuk],
                  list(range(MC)), [0],
                  lambda mi, kk: wuq_v[kk][:, mi * 128:(mi + 1) * 128],
                  lambda nj, kk: wuk_v[kk],
                  lambda mi, nj: vB[mi])
    ld_wdq = p.load(arD[:, 0:MC * C].rearrange("p (c m) -> p c m", m=C), d_wdq.rearrange("(c p) m -> p c m", p=128),
                    wait=[b_done])
    kT_done = proj(None, 512, None, None, MC, [ld_wdq],
                   list(range(LC)), list(range(2)),
                   lambda mi, kk: vB[kk][:, mi * 128:(mi + 1) * 128],
                   lambda nj, kk: wdq_v[kk][:, nj * 512:(nj + 1) * 512],
                   lambda mi, nj: vkT[mi][:, nj * 512:(nj + 1) * 512])
    ld_wuv = p.load(arD[:, MC * C:MC * C + DC * KVL].rearrange("p (c m) -> p c m", m=KVL),
                    d_wuv.rearrange("(c p) m -> p c m", p=128), wait=[kT_done])
    ve_done = proj(None, 512, None, None, DC, [ld_wuv, ld_woT],
                   list(range(LC)), list(range(2)),
                   lambda mi, kk: wuv_v[kk][:, mi * 128:(mi + 1) * 128],
                   lambda nj, kk: woT_v[kk][:, nj * 512:(nj + 1) * 512],
                   lambda mi, nj: vve[mi][:, nj * 512:(nj + 1) * 512])
    ld_wdkvT = p.load(arD[:, 0:DC * KVL].rearrange("p (c m) -> p c m", m=KVL),
                      d_wdkvT.rearrange("(c p) m -> p c m", p=128),
                      wait=[kT_done])
    ckv_done = proj(None, 512, None, None, DC, [ld_wdkvT] + ln_h1 + [ve_done],
                    list(range(LC)), list(range(4)),
                    lambda mi, kk: wdkvT_v[kk][:, mi * 128:(mi + 1) * 128],
                    lambda nj, kk: vx[kk][:, nj * 512:(nj + 1) * 512],
                    lambda mi, nj: vckv[mi][:, nj * 512:(nj + 1) * 512])
    ld_wdqT = p.load(arD[:, DC * KVL:2 * DC * KVL].rearrange("p (c m) -> p c m", m=QL),
                     d_wdqT.rearrange("(c p) m -> p c m", p=128), wait=[ve_done])
    qmid_done = proj(None, 512, None, None, DC, [ld_wdqT],
                     list(range(MC)), list(range(4)),
                     lambda mi, kk: wdqT_v[kk][:, mi * 128:(mi + 1) * 128],
                     lambda nj, kk: vx[kk][:, nj * 512:(nj + 1) * 512],
                     lambda mi, nj: vqmid[mi][:, nj * 512:(nj + 1) * 512])
    ld_wuqT = p.load(arA[:, 4 * T + NTK * 260: 4 * T + NTK * 260 + MC * C].rearrange("p (c m) -> p c m", m=C),
                     d_wuqT.rearrange("(c p) m -> p c m", p=128),
                     wait=[ckv_done, qmid_done])

    tap("h1", vx[0], 128, T, wait=ln_h1)
    tap("ckv", vckv[0], 128, T, wait=[ckv_done])
    tap("keffT", vkT[0], 128, C, wait=[kT_done])
    tap("veff", vve[0], 128, C, wait=[ve_done])
    tap("qmid", vqmid[0], 128, T, wait=[qmid_done])

    t1 = sact(s_da[0:1, 1:2], s_da[0:1, 0:1], AF.Tanh, scale=4.0, inc=True,
              wait=[ld_small] + ln_h1)
    p.drain("s")
    t2 = sact(s_rows[0:1, 0:256], s_da[0:1, 1:2].broadcast_to([1, 256]), AF.Exp,
              scale=0.5 * LN2, inc=True)
    p.drain("s")
    t3 = sact(s_rows[0:1, 256:512], s_da[0:1, 1:2].broadcast_to([1, 256]), AF.Exp,
              scale=-0.5 * LN2, inc=True)
    bd1 = mm(pH[:, 0:256], s_onesr[0:1, 0:128], s_rows[0:1, 0:256], True, True,
             inc=True, wait=[t2] + fw("pH"))
    cd1 = vcp(s_tda[:, 0:1], pH[:, 0:1], inc=True, wait=[bd1])
    bd2 = mm(pH[:, 0:256], s_onesr[0:1, 0:128], s_rows[0:1, 256:512], True, True,
             inc=True, wait=[t3, cd1])
    cd2 = vcp(s_tda[:, 1:2], pH[:, 1:2], inc=True, wait=[bd2])
    mark("pH", cd2)

    gp(lambda h, S: h.memset(s_mka[:], 0.0), wait=[qmid_done, ckv_done])
    gp(lambda h, S: h.affine_select(
        out=s_mka[:], in_=s_mka[:], compare_op=ALU.is_ge, fill=NEG,
        base=0, pattern=[[1, 256]], channel_multiplier=-1))
    gp(lambda h, S: h.memset(s_mkb[:], 0.0))
    gp(lambda h, S: h.affine_select(
        out=s_mkb[:], in_=s_mkb[:], compare_op=ALU.is_ge, fill=NEG,
        base=-128, pattern=[[1, 256]], channel_multiplier=-1))
    rrow_init = gp(lambda h, S: h.memset(s_rrow[:], 0.0), inc=True)
    pend_evac = [None]
    pt_free = [None, None, None, None]
    ps_free = [None, None]
    py_free = [None, None]
    yst_free = [None, None]
    yraw_free = [None, None]
    pb_free = [rrow_init]
    last_store = [None]

    for g_ in range(NG):
        kg_done = proj(None, 512, None, None, LC,
                       [kT_done, ckv_done] if g_ == 0 else [],
                       list(range(2)), list(range(4)),
                       lambda mi, kk, g=g_: vkT[kk][:, 256 * g + 128 * mi:
                                                    256 * g + 128 * (mi + 1)],
                       lambda nj, kk: vckv[kk][:, nj * 512:(nj + 1) * 512],
                       lambda mi, nj: vK[mi][:, nj * 512:(nj + 1) * 512])
        for lc_ in range(LC):
            vmemset(vvaug[lc_], 0.0,
                    wait=[qmid_done] if g_ == 0 and lc_ == 0 else [])
        va_done = None
        for lc_ in range(LC):
            for j in range(4):
                va_done = vcp(vvaug[lc_][:, 65 * j:65 * j + 64],
                              vve[lc_][:, 64 * (4 * g_ + j):64 * (4 * g_ + j) + 64],
                              inc=(lc_ == LC - 1 and j == 3),
                              wait=[ve_done] if (g_ == 0 and lc_ == 0 and j == 0)
                              else [])
        vg_done = proj(None, 260, None, None, LC, [va_done],
                       list(range(NTK)), [0],
                       lambda mi, kk: vckv[kk][:, mi * 128:(mi + 1) * 128],
                       lambda nj, kk: vvaug[kk],
                       lambda mi, nj: vV[mi])
        for j in range(4):
            vg_done = vmemset(vV3[:, :, 65 * j + 64:65 * j + 65], 1.0,
                              inc=(j == 3))
        qg_done = proj(None, 512, None, None, MC, [ld_wuqT, qmid_done],
                       list(range(2)), list(range(4)),
                       lambda mi, kk, g=g_: vwuqT[kk][:, 256 * g + 128 * mi:
                                                      256 * g + 128 * (mi + 1)],
                       lambda nj, kk: vqmid[kk][:, nj * 512:(nj + 1) * 512],
                       lambda mi, nj: vqg[mi][:, nj * 512:(nj + 1) * 512])
        if g_ == 0:
            tap("K0", vK[0], 128, T, wait=[kg_done])
            tap("V0", arA[:, 2 * T:2 * T + 2048], 128, 2048, wait=[vg_done])
            tap("q0", vqg[0], 128, T, wait=[qg_done])

        for pr in range(2):
            for qt in range(NQT):
                kcmax = 2 * qt + 1
                ymm = [None, None]

                def emit_S(kc, pr=pr, qt=qt):
                    out = [None, None]
                    for idx in range(2):
                        j = 2 * pr + idx
                        jc, jo = j // 2, 64 * (j % 2)
                        si = (kc % 2) * 2 + idx
                        w = [kg_done, qg_done] + fw(f"PQ{si}")
                        smm = mm(PQ[si][:, 0:256],
                                 vK[jc][jo:jo + 64, kc * 128:(kc + 1) * 128],
                                 vqg[jc][jo:jo + 64, qt * 256:(qt + 1) * 256],
                                 True, True, inc=True, wait=w)
                        out[idx] = smm
                    if kc >= 2 * qt:
                        msk = s_mka if kc == 2 * qt else s_mkb
                        for idx in range(2):
                            si = (kc % 2) * 2 + idx
                            out[idx] = vtt(PQ[si][:, 0:256], PQ[si][:, 0:256],
                                           msk, ALU.add, inc=True,
                                           wait=[out[idx], rrow_init])
                    return out

                sprods = emit_S(0)
                if pend_evac[0] is not None:
                    pend_evac[0]()
                    pend_evac[0] = None
                for kc in range(kcmax + 1):
                    exs = [None, None]
                    for idx in range(2):
                        si = (kc % 2) * 2 + idx
                        pi = (kc % 2) * 2 + idx
                        w = [sprods[idx]] + ([pt_free[pi]] if pt_free[pi] else [])
                        exs[idx] = sact(s_pt[:, pi * 256:(pi + 1) * 256],
                                        PQ[si][:, 0:256], AF.Exp, scale=0.125,
                                        inc=True, wait=w)
                        mark(f"PQ{si}", exs[idx])
                    if kc < kcmax:
                        sprods = emit_S(kc + 1)
                    for idx in range(2):
                        j = 2 * pr + idx
                        pi = (kc % 2) * 2 + idx
                        w = [exs[idx], vg_done]
                        if kc == 0 and py_free[idx] is not None:
                            w.append(py_free[idx])
                        ymm[idx] = mm(pY[idx][0:65, 0:256],
                                      vV[kc][:, 65 * j:65 * j + 65],
                                      s_pt[:, pi * 256:(pi + 1) * 256],
                                      kc == 0, kc == kcmax, inc=True, wait=w)
                        pt_free[pi] = ymm[idx]

                def evac(pr=pr, qt=qt, ymm=ymm, g_=g_):
                    for idx in range(2):
                        j = 2 * pr + idx
                        h_ = 4 * g_ + j
                        ys = idx
                        w = [ymm[idx]] + ([yraw_free[ys]] if yraw_free[ys] else [])
                        cpy = vcp(s_yraw[0:65, 256 * ys:256 * (ys + 1)],
                                  pY[idx][0:65, 0:256], inc=True, wait=w)
                        rc = vrecip(s_rrow[64:65, :], pY[idx][64:65, 0:256],
                                    inc=True,
                                    wait=[ymm[idx]] + ([pb_free[0]] if pb_free[0] else []))
                        py_free[idx] = rc
                        bc = mm(pB[0:64, :], s_e64[:, :], s_rrow[:, :], True,
                                True, inc=True, wait=[rc])
                        w = [bc, cpy] + ([yst_free[ys]] if yst_free[ys] else [])
                        yn = vtt(s_yst[0:64, 256 * ys:256 * (ys + 1)],
                                 s_yraw[0:64, 256 * ys:256 * (ys + 1)],
                                 pB[0:64, :], ALU.mult, inc=True, wait=w)
                        pb_free[0] = yn
                        yraw_free[ys] = yn
                        last_store[0] = p.store(
                            d_y[64 * h_:64 * (h_ + 1), 256 * qt:256 * (qt + 1)],
                            s_yst[0:64, 256 * ys:256 * (ys + 1)], wait=[yn])
                        yst_free[ys] = last_store[0]
                pend_evac[0] = evac
    pend_evac[0]()
    pend_evac[0] = None

    dw1 = sact(s_rdw[:], s_tdw[:], AF.Tanh, scale=4.0, inc=True,
               wait=ln_h1 + [ld_small])
    p.drain("s")
    dw2 = sact(s_tdw[:], s_rdw[:], AF.Exp, scale=0.5 * LN2, inc=True)
    p.drain("s")
    dw3 = sact(s_rdw[:], s_rdw[:], AF.Exp, scale=-0.5 * LN2, inc=True)

    def cascade(am_ap, out_ap, base_k, terms, wait0):
        n = len(terms) + 1
        vts(ck_sl[0], am_ap, 0.0, base_k, ALU.is_gt, ALU.mult, wait=wait0)
        for i, (thr, wgt) in enumerate(terms):
            vts(ck_sl[i + 1], am_ap, thr, wgt, ALU.is_ge, ALU.mult)
        p.drain("v")
        width = n
        while width > 1:
            half = width // 2
            for i in range(half):
                vtt(ck_sl[i], ck_sl[i], ck_sl[width - 1 - i], ALU.add)
            width = width - half
            p.drain("v")
        return vts(out_ap, ck_sl[0], 1.0, None, ALU.add, inc=True)

    def quant_batch4(src4, dst4, tw_ap, rw_ap, wait0, krange=(KMIN, KMAX)):
        am4 = vred_absmax(
            s_am[:].rearrange("p (t b) -> p t b", b=32).unsqueeze(3),
            src4.rearrange("p (t b e) -> p t b e", b=32, e=32),
            inc=True, wait=wait0)
        p.drain("v")
        kmin, kmax = krange
        terms = [(2.0 ** k, 2.0 ** (k - 3)) for k in range(kmin + 1, kmax + 1)]
        c1 = cascade(s_am[:], s_sc[:], 2.0 ** (kmin - 2) - 1.0, terms, [am4])
        p.drain("v")
        vrecip(s_rsc[:], s_sc[:])
        p.drain("v")
        if tw_ap.shape[1] == 1:
            vts(s_sc[:], s_sc[:], tw_ap, None, ALU.mult, wait=[cd1, cd2])
            vts(s_rsc[:], s_rsc[:], rw_ap, None, ALU.mult)
        else:
            vtt(s_sc[:], s_sc[:], tw_ap, ALU.mult, wait=[dw2, dw3])
            vtt(s_rsc[:], s_rsc[:], rw_ap, ALU.mult)
        p.drain("v")
        xsm = vtt(src4.rearrange("p (t b e) -> p t b e", b=32, e=32),
                  src4.rearrange("p (t b e) -> p t b e", b=32, e=32),
                  s_rsc[:].rearrange("p (t b) -> p t b", b=32).unsqueeze(3)
                  .broadcast_to([128, 4, 32, 32]), ALU.mult, inc=True)
        last = None
        for t_ in range(4):
            xs = src4[:, t_ * 1024:(t_ + 1) * 1024]
            xq = dst4[:, t_ * 1024:(t_ + 1) * 1024]
            w_s = [xsm] if t_ == 0 else [qsgn]
            sg1 = sact(tmp1[:], xs, AF.Sign, inc=True, wait=w_s)
            sg2 = sact(tmp2[:], xs, AF.Abs, inc=True)
            vts(tmp3[:], tmp2[:], 0.25, 0.5, ALU.is_gt, ALU.mult, wait=[sg2])
            vts(tmp4[:], tmp2[:], 0.75, 0.5, ALU.is_ge, ALU.mult)
            vtt(tmp3[:], tmp3[:], tmp4[:], ALU.add)
            vts(tmp4[:], tmp2[:], 1.25, 0.5, ALU.is_gt, ALU.mult)
            vtt(tmp3[:], tmp3[:], tmp4[:], ALU.add)
            vts(tmp4[:], tmp2[:], 1.75, 0.5, ALU.is_ge, ALU.mult)
            vtt(tmp3[:], tmp3[:], tmp4[:], ALU.add)
            vts(tmp4[:], tmp2[:], 2.5, 1.0, ALU.is_gt, ALU.mult)
            vtt(tmp3[:], tmp3[:], tmp4[:], ALU.add)
            vts(tmp4[:], tmp2[:], 3.5, 1.0, ALU.is_ge, ALU.mult)
            vtt(tmp3[:], tmp3[:], tmp4[:], ALU.add)
            vts(tmp4[:], tmp2[:], 5.0, 2.0, ALU.is_gt, ALU.mult)
            vtt(tmp3[:], tmp3[:], tmp4[:], ALU.add)
            qsgn = vtt(xq, tmp3[:], tmp1[:], ALU.mult, inc=True)
            last = vtt(xq.rearrange("p (b e) -> p b e", e=32),
                       xq.rearrange("p (b e) -> p b e", e=32),
                       s_sc[:, t_ * 32:(t_ + 1) * 32].unsqueeze(2)
                       .broadcast_to([128, 32, 32]), ALU.mult,
                       inc=(t_ == 3))
        return last

    att_end = [pb_free[0], last_store[0]]
    wq_done = None
    u_done = []
    for q5 in range(NQ5):
        sl = slice(q5 * 512, (q5 + 1) * 512)
        w = att_end if q5 == 0 else u_done
        lx = ly = None
        for c_ in range(DC):
            lx = p.load(vxs[c_], d_xf[c_ * 128:(c_ + 1) * 128, sl], wait=w)
            ly = p.load(vys[c_], d_y[c_ * 128:(c_ + 1) * 128, sl])
            w = []
        u_done = ln_tile(vys, [u[:, sl] for u in vu], add=vxs,
                         src_wait=[lx, ly])
        if q5 < 2:
            wb = q5
            lw = None
            for i in range(4):
                oc = wb * 4 + i
                wl = att_end if (wb == 0 and i == 0) else (
                    [wq_done] if (wb == 1 and i == 0) else [])
                lw = p.load(qw_src[i], d_wffn[oc * 128:(oc + 1) * 128, :],
                            wait=wl)
            qd = quant_batch4(arD[:, 0:4096], arD[:, 0:4096],
                              s_tdw[:, wb * 128:(wb + 1) * 128],
                              s_rdw[:, wb * 128:(wb + 1) * 128],
                              [lw, dw2, dw3], krange=(-11, 1))
            if wb == 0:
                tap("wq0", arD[:, 0:1024], 128, 1024, wait=[qd])
                tap("wam", s_am[:], 128, 128, wait=[qd])
                tap("wsc", s_sc[:], 128, 128, wait=[qd])
                tap("wrsc", s_rsc[:], 128, 128, wait=[qd])
                tap("wtdw", s_tdw[:, 0:128], 128, 128, wait=[qd])
            for i in range(4):
                oc = wb * 4 + i
                for dc_ in range(DC):
                    w2 = [qd] if (i == 0 and dc_ == 0) else []
                    w2 += fw(f"PQ{pg_i[0]}")
                    tm = mm(pG[pg_i[0]][:, 0:256],
                            arD[:, i * 1024 + dc_ * 128: i * 1024 + (dc_ + 1) * 128],
                            s_id2[:], True, True, inc=True, wait=w2)
                    wq_done = scp(vwqT[dc_][:, oc * 128:(oc + 1) * 128],
                                  pG[pg_i[0]][:, 0:128], inc=True, wait=[tm])
                    mark(f"PQ{pg_i[0]}", wq_done)
                    pg_i[0] ^= 1
    tap("u", vu[0], 128, T, wait=u_done)
    tap("wqT", vwqT[0], 128, C, wait=[wq_done])

    out_stores = []
    tpq = [2]
    xq_marks = {}
    st_marks = {}

    def emit_transp(src_fn, dst_fn, wait_first):
        done = None
        for t_ in range(4):
            for dc_ in range(DC):
                w = list(wait_first) if (t_ == 0 and dc_ == 0) else []
                i = tpq[0]
                w += fw(f"PQ{i}")
                tm = mm(PQ[i][:, 0:256], src_fn(t_, dc_), s_id2[:],
                        True, True, inc=True, wait=w)
                done = scp(dst_fn(t_, dc_), PQ[i][:, 0:128], inc=True, wait=[tm])
                mark(f"PQ{i}", done)
                tpq[0] = 2 + ((tpq[0] - 2 + 1) % 2)
        return done

    def emit_ffn(q5):
        xsl = slice((q5 % 2) * 512, (q5 % 2) * 512 + 512)
        vfa = vfab[q5 % 2]
        f_done = None
        for oc in range(DC):
            for dc_ in range(DC):
                w = []
                if dc_ == 0:
                    w = list(fw(f"PQ{pg_i[0]}"))
                    if oc == 0:
                        w += [xq_marks[q5], wq_done]
                        if q5 - 2 in st_marks:
                            w.append(st_marks[q5 - 2])
                mmn = mm(PQ[pg_i[0]][:, :], vwqT[dc_][:, oc * 128:(oc + 1) * 128],
                         vxq2[dc_][:, xsl], dc_ == 0, dc_ == DC - 1,
                         inc=(dc_ == DC - 1), wait=w)
            w2 = [mmn]
            if oc == 0 and q5 - 2 in st_marks:
                w2.append(st_marks[q5 - 2])
            f_done = sact(vfa[oc], PQ[pg_i[0]][:, :], AF.Identity,
                          bias=s_bias[:, oc:oc + 1], inc=True, wait=w2)
            mark(f"PQ{pg_i[0]}", f_done)
            pg_i[0] ^= 1
        return f_done

    def emit_lnf(q5, f_done):
        sl = slice(q5 * 512, (q5 + 1) * 512)
        vfa = vfab[q5 % 2]
        c4 = ln_tile(vfa, vfa, add=[u[:, sl] for u in vu], src_wait=[f_done])
        if q5 == 0:
            tap("f0", vfa[0], 128, 512, wait=c4)
        last = None
        for c_ in range(DC):
            last = p.store(d_out[c_ * 128:(c_ + 1) * 128, sl], vfa[c_],
                           wait=c4)
            out_stores.append(last)
        st_marks[q5] = last
        return c4

    def emit_quant_stage(q5, w0):
        sl = slice(q5 * 512, (q5 + 1) * 512)
        h2_done = ln_tile([u[:, sl] for u in vu], qw_h2f, src_wait=w0)
        tp_done = emit_transp(
            lambda t_, dc_: qw_h2f[dc_][:, t_ * 128:(t_ + 1) * 128],
            lambda t_, dc_: arD[:, t_ * 1024 + dc_ * 128: t_ * 1024 + (dc_ + 1) * 128],
            h2_done)
        return tp_done

    f_prev = None
    for q5 in range(NQ5):
        sl = slice(q5 * 512, (q5 + 1) * 512)
        w0 = ([wq_done] + u_done) if q5 == 0 else []
        tp_done = emit_quant_stage(q5, w0)
        if q5 >= 1:
            f_prev = emit_ffn(q5 - 1)
        qd = quant_batch4(arD[:, 0:4096], arD[:, 0:4096],
                          s_tda[:, 0:1], s_tda[:, 1:2], [tp_done],
                          krange=(-6, 4))
        xq_marks[q5] = emit_transp(
            lambda t_, dc_: arD[:, t_ * 1024 + dc_ * 128: t_ * 1024 + (dc_ + 1) * 128],
            lambda t_, dc_, q=q5: vxq2[dc_][:, (q % 2) * 512 + t_ * 128:
                                            (q % 2) * 512 + (t_ + 1) * 128],
            [qd])
        if q5 >= 1:
            emit_lnf(q5 - 1, f_prev)
    f_prev = emit_ffn(NQ5 - 1)
    emit_lnf(NQ5 - 1, f_prev)

    p.ops.append(("y", lambda h, S, n=p.dcnt["st"] * 16: h.wait_ge(S["st"], n)))

    S = {}
    S["ld"] = nc.alloc_semaphore("sem_ld")
    S["st"] = nc.alloc_semaphore("sem_st")
    for e in ("t", "v", "s", "g"):
        S[e] = nc.alloc_semaphore(f"sem_{e}")

    with nc.Block() as blk:
        @blk.tensor
        def _(eng):
            for e, fn in p.ops:
                if e == "t":
                    fn(eng, S)

        @blk.vector
        def _(eng):
            for e, fn in p.ops:
                if e == "v":
                    fn(eng, S)

        @blk.scalar
        def _(eng):
            for e, fn in p.ops:
                if e == "s":
                    fn(eng, S)

        @blk.gpsimd
        def _(eng):
            for e, fn in p.ops:
                if e == "g":
                    fn(eng, S)

        @blk.sync
        def _(eng):
            for e, fn in p.ops:
                if e == "y":
                    fn(eng, S)

    return tap_aps


def _in_map(x_b, W_dq, W_uq, W_dkv, W_uk, W_uv, W_o, Wf, bias, dw, da):
    cc = np.ascontiguousarray
    return {
        "xf": cc(x_b.T),
        "wdqT": cc(W_dq.T),
        "wuqT": cc(W_uq.T),
        "wdkvT": cc(W_dkv.T),
        "wuq": cc(W_uq),
        "wuk": cc(W_uk),
        "wuv": cc(W_uv),
        "wdq": cc(W_dq),
        "woT": cc(W_o.T),
        "wffn": cc(Wf),
        "biasr": cc(bias.reshape(DC, 128).T),
        "dw": cc(dw),
        "da": cc(da.reshape(1, 1)),
    }


def kernel(L0_attn_x, L0_attn_W_dq, L0_attn_W_uq, L0_attn_W_dkv, L0_attn_W_uk,
           L0_attn_W_uv, L0_attn_W_o, L0_ffn_weight, L0_ffn_bias,
           L0_ffn_delta_w, L0_ffn_delta_act):
    x = np.asarray(L0_attn_x, dtype=np.float32)
    args = [np.asarray(a, dtype=np.float32) for a in
            (L0_attn_W_dq, L0_attn_W_uq, L0_attn_W_dkv, L0_attn_W_uk,
             L0_attn_W_uv, L0_attn_W_o, L0_ffn_weight, L0_ffn_bias,
             L0_ffn_delta_w, L0_ffn_delta_act)]
    nc = bass.Bass("TRN2", target_bir_lowering=False, debug=False)
    build(nc, taps=tuple(DEBUG_TAPS))
    in_maps = [_in_map(x[b], *args) for b in range(8)]
    res = run_bass_kernel_spmd(nc, in_maps, core_ids=list(range(8)))
    out = np.stack([np.ascontiguousarray(r["outf"].T) for r in res.results])
    if DEBUG_TAPS:
        kernel.last_results = res
    return out


# version 13 (final)
# speedup vs baseline: 1.0495x; 1.0011x over previous
import numpy as np

import concourse.bass as bass
import concourse.mybir as mybir
from concourse.bass_utils import run_bass_kernel_spmd

F32 = mybir.dt.float32
AF = mybir.ActivationFunctionType
ALU = mybir.AluOpType
AX = mybir.AxisListType

C = 1024
T = 2048
QL = 512
KVL = 512
NH = 16
HS = 64
NG = 4
DC = C // 128
LC = KVL // 128
MC = QL // 128
NQT = T // 256
NQ5 = T // 512
NTK = T // 128
LN2 = float(np.log(2.0))
NEG = -1.0e30
KMIN, KMAX = -14, 5

DEBUG_TAPS = []


class Prog:

    ENGS = ("t", "v", "s", "g", "y")

    def __init__(self):
        self.ops = []
        self.cnt = {e: 0 for e in self.ENGS}
        self.dcnt = {"ld": 0, "st": 0}
        self._lastwait = {e: {} for e in self.ENGS}

    def _wait(self, eng, key, val):
        if val <= 0:
            return
        lw = self._lastwait[eng]
        if lw.get(key, -1) >= val:
            return
        lw[key] = val
        self.ops.append((eng, lambda h, S, k=key, v=val: h.wait_ge(S[k], v)))

    def _chk(self, key, val):
        lim = self.cnt[key] if key in self.cnt else self.dcnt[key] * 16
        assert val <= lim, f"wait on future count {key}={val} (now {lim})"

    def op(self, eng, build, inc=False, wait=()):
        for (k, v) in wait:
            self._chk(k, v)
            self._wait(eng, k, v)
        if inc:
            self.cnt[eng] += 1

            def fn(h, S, b=build, e=eng):
                b(h, S).then_inc(S[e], 1)
            self.ops.append((eng, fn))
            return (eng, self.cnt[eng])
        self.ops.append((eng, lambda h, S, b=build: b(h, S)))
        return None

    def drain(self, eng):
        self.ops.append((eng, lambda h, S: h.drain()))

    def load(self, out_ap, in_ap, wait=()):
        for (k, v) in wait:
            self._chk(k, v)
            self._wait("y", k, v)
        self.dcnt["ld"] += 1
        self.ops.append(("y", lambda h, S, o=out_ap, i=in_ap:
                         h.dma_start(out=o, in_=i).then_inc(S["ld"], 16)))
        return ("ld", self.dcnt["ld"] * 16)

    def store(self, out_ap, in_ap, wait=()):
        for (k, v) in wait:
            self._chk(k, v)
            self._wait("g", k, v)
        self.dcnt["st"] += 1
        self.ops.append(("g", lambda h, S, o=out_ap, i=in_ap:
                         h.dma_start(out=o, in_=i).then_inc(S["st"], 16)))
        return ("st", self.dcnt["st"] * 16)


def build(nc: bass.Bass, taps=()):
    d_xf = nc.dram_tensor("xf", [C, T], F32, kind="ExternalInput").ap()
    d_wdqT = nc.dram_tensor("wdqT", [C, QL], F32, kind="ExternalInput").ap()
    d_wuqT = nc.dram_tensor("wuqT", [QL, C], F32, kind="ExternalInput").ap()
    d_wdkvT = nc.dram_tensor("wdkvT", [C, KVL], F32, kind="ExternalInput").ap()
    d_wuq = nc.dram_tensor("wuq", [C, QL], F32, kind="ExternalInput").ap()
    d_wuk = nc.dram_tensor("wuk", [C, KVL], F32, kind="ExternalInput").ap()
    d_wuv = nc.dram_tensor("wuv", [C, KVL], F32, kind="ExternalInput").ap()
    d_wdq = nc.dram_tensor("wdq", [QL, C], F32, kind="ExternalInput").ap()
    d_woT = nc.dram_tensor("woT", [C, C], F32, kind="ExternalInput").ap()
    d_wffn = nc.dram_tensor("wffn", [C, C], F32, kind="ExternalInput").ap()
    d_biasr = nc.dram_tensor("biasr", [128, DC], F32, kind="ExternalInput").ap()
    d_dw = nc.dram_tensor("dw", [C, 32], F32, kind="ExternalInput").ap()
    d_da = nc.dram_tensor("da", [1, 1], F32, kind="ExternalInput").ap()
    d_out = nc.dram_tensor("outf", [C, T], F32, kind="ExternalOutput").ap()
    d_y = nc.dram_tensor("ydram", [C, T], F32, kind="Internal").ap()

    tap_aps = {}
    for name in taps:
        pass

    arA = nc.alloc_sbuf_tensor("arA", [128, 16448], F32)
    arB = nc.alloc_sbuf_tensor("arB", [128, 16384], F32)
    arC = nc.alloc_sbuf_tensor("arC", [128, 8192], F32)
    arD = nc.alloc_sbuf_tensor("arD", [128, 8192], F32)

    def views(ar, start, n, w):
        return [ar[:, start + i * w: start + (i + 1) * w] for i in range(n)]

    vx = views(arA, 0, DC, T)
    vckv = views(arB, 0, LC, T)
    vqmid = views(arB, LC * T, MC, T)
    vkT = views(arC, 0, LC, C)
    vve = views(arC, LC * C, LC, C)
    woT_v = views(arB, 0, DC, C)
    vB = views(arB, DC * C, MC, KVL)
    wuq_v = views(arD, 0, DC, QL)
    wuk_v = views(arD, DC * QL, DC, KVL)
    wdq_v = views(arD, 0, MC, C)
    wuv_v = views(arD, MC * C, DC, KVL)
    wdkvT_v = views(arD, 0, DC, KVL)
    wdqT_v = views(arD, DC * KVL, DC, QL)

    vK = views(arA, 0, 2, T)
    vV = views(arA, 2 * T, NTK, 260)
    vV3 = arA[:, 2 * T: 2 * T + NTK * 260].rearrange("p (t w) -> p t w", w=260)
    vqg = views(arA, 2 * T + NTK * 260, 2, T)
    vwuqT = views(arA, 4 * T + NTK * 260, MC, C)
    vvaug = views(arD, 0, LC, 260)
    s_pt = arD[:, 1040:2064]
    s_yraw = arD[:, 2064:2576]
    s_yst = arD[:, 2576:3088]
    s_rrow = arD[:, 3088:3344]
    s_mka = arD[:, 3344:3600]
    s_mkb = arD[:, 3600:3856]

    vu = views(arB, 0, DC, T)
    vxq2 = views(arA, 0, DC, 1024)
    vfab = [views(arA, DC * 1024, DC, 512),
            views(arA, DC * 1024 + DC * 512, DC, 512)]
    vwqT = views(arC, 0, DC, C)
    vxs = views(arA, 0, DC, 512)
    vys = views(arA, DC * 512, DC, 512)
    qw_src = views(arD, 0, 4, 1024)
    qw_h2f = views(arD, 4096, DC, 512)
    CKB = 4096
    ck_sl = [arD[:, CKB + i * 128: CKB + (i + 1) * 128] for i in range(21)]
    tmp1 = arD[:, 4096:5120]
    tmp2 = arD[:, 5120:6144]
    tmp3 = arD[:, 6144:7168]
    tmp4 = arD[:, 7168:8192]
    vf = views(arD, 0, DC, 512)
    vo = views(arD, DC * 512, DC, 512)

    s_ones = nc.alloc_sbuf_tensor("s_ones", [128, 1], F32)
    s_onesr = nc.alloc_sbuf_tensor("s_onesr", [1, 128], F32)
    s_id2 = nc.alloc_sbuf_tensor("s_id2", [128, 256], F32)
    s_e64 = nc.alloc_sbuf_tensor("s_e64", [128, 64], F32)
    s_lnt = nc.alloc_sbuf_tensor("s_lnt", [128, 512], F32)
    s_lnq = nc.alloc_sbuf_tensor("s_lnq", [128, 512], F32)
    s_lns = nc.alloc_sbuf_tensor("s_lns", [128, 512], F32)
    s_rows = nc.alloc_sbuf_tensor("s_rows", [1, 1024], F32)
    s_eps = nc.alloc_sbuf_tensor("s_eps", [128, 1], F32)
    s_am = nc.alloc_sbuf_tensor("s_am", [128, 128], F32)
    s_sc = nc.alloc_sbuf_tensor("s_sc", [128, 128], F32)
    s_rsc = nc.alloc_sbuf_tensor("s_rsc", [128, 128], F32)
    s_bias = nc.alloc_sbuf_tensor("s_bias", [128, DC], F32)
    s_tdw = nc.alloc_sbuf_tensor("s_tdw", [128, DC * 32], F32)
    s_rdw = nc.alloc_sbuf_tensor("s_rdw", [128, DC * 32], F32)
    s_da = nc.alloc_sbuf_tensor("s_da", [1, 2], F32)
    s_tda = nc.alloc_sbuf_tensor("s_tda", [128, 2], F32)

    PQ = [nc.alloc_psum_tensor(f"PQ{i}", [128, 512], F32) for i in range(4)]
    pY = [nc.alloc_psum_tensor("pY0", [128, 512], F32),
          nc.alloc_psum_tensor("pY1", [128, 512], F32)]
    pG = [PQ[0], PQ[1]]
    pB = nc.alloc_psum_tensor("pB", [64, 256], F32)
    pH = nc.alloc_psum_tensor("pH", [128, 512], F32)

    p = Prog()
    free = {}

    def fw(res):
        v = free.get(res)
        return list(v) if v else []

    def mark(res, mk):
        if mk is None:
            return
        free[res] = list(mk) if isinstance(mk, list) else [mk]

    def mm(out, lhsT, rhs, start, stop, inc=False, wait=()):
        return p.op("t", lambda h, S, o=out, l=lhsT, r=rhs, a=start, b=stop:
                    h.matmul(o, l, r, start=a, stop=b), inc=inc, wait=wait)

    def vcp(out, in_, inc=False, wait=()):
        return p.op("v", lambda h, S, o=out, i=in_: h.tensor_copy(o, i),
                    inc=inc, wait=wait)

    def vtt(out, a, b, alu, inc=False, wait=()):
        return p.op("v", lambda h, S, o=out, x=a, yy=b, u=alu:
                    h.tensor_tensor(out=o, in0=x, in1=yy, op=u), inc=inc, wait=wait)

    def vts(out, in_, s1, s2, op0, op1=None, inc=False, wait=()):
        if s2 is None:
            return p.op("v", lambda h, S, o=out, i=in_, a=s1, u0=op0:
                        h.tensor_scalar(out=o, in0=i, scalar1=a, scalar2=None,
                                        op0=u0), inc=inc, wait=wait)
        return p.op("v", lambda h, S, o=out, i=in_, a=s1, b=s2, u0=op0, u1=op1:
                    h.tensor_scalar(out=o, in0=i, scalar1=a, scalar2=b,
                                    op0=u0, op1=u1), inc=inc, wait=wait)

    def vred_absmax(out, in_, inc=False, wait=()):
        return p.op("v", lambda h, S, o=out, i=in_:
                    h.tensor_reduce(o, i, axis=AX.X, op=ALU.max,
                                    apply_absolute_value=True), inc=inc, wait=wait)

    def vrecip(out, in_, inc=False, wait=()):
        return p.op("v", lambda h, S, o=out, i=in_: h.reciprocal(o, i),
                    inc=inc, wait=wait)

    def vmemset(ap, val, inc=False, wait=()):
        return p.op("v", lambda h, S, a=ap, x=val: h.memset(a, x),
                    inc=inc, wait=wait)

    def sact(out, in_, func, bias=0.0, scale=1.0, inc=False, wait=()):
        return p.op("s", lambda h, S, o=out, i=in_, f=func, b=bias, sc=scale:
                    h.activation(out=o, in_=i, func=f, bias=b, scale=sc),
                    inc=inc, wait=wait)

    def gp(build, inc=False, wait=()):
        return p.op("g", build, inc=inc, wait=wait)

    def gtt(out, a, b, alu, inc=False, wait=()):
        return p.op("g", lambda h, S, o=out, x=a, yy=b, u=alu:
                    h.tensor_tensor(out=o, in0=x, in1=yy, op=u), inc=inc, wait=wait)

    def scp(out, in_, inc=False, wait=()):
        return p.op("s", lambda h, S, o=out, i=in_: h.copy(o, i),
                    inc=inc, wait=wait)

    def warm():
        v = p.cnt["v"]
        p.op("t", lambda h, S: h.matmul(pB[0:64, :], s_e64[:, :],
                                        s_rrow[:, :], start=True, stop=True),
             wait=[("v", v)] if v else [])

    def tap(name, src_ap, rows, cols, wait=()):
        if name not in taps:
            return
        dt = nc.dram_tensor(f"tap_{name}", [rows, cols], F32,
                            kind="ExternalOutput").ap()
        tap_aps[name] = dt
        p.store(dt, src_ap, wait=wait)

    gp(lambda h, S: h.memset(s_ones[:], 1.0))
    gp(lambda h, S: h.memset(s_onesr[:], 1.0))
    gp(lambda h, S: h.memset(s_eps[:], 1e-5))
    gp(lambda h, S: h.memset(s_id2[:], 0.0))
    gp(lambda h, S: h.affine_select(
        out=s_id2[:].rearrange("p (a b) -> p a b", b=128),
        in_=s_id2[:].rearrange("p (a b) -> p a b", b=128),
        compare_op=ALU.not_equal, fill=1.0, base=0,
        pattern=[[0, 2], [-1, 128]], channel_multiplier=1))
    gp(lambda h, S: h.memset(s_e64[:], 1.0))
    g_setup = gp(lambda h, S: h.affine_select(
        out=s_e64[:], in_=s_e64[:], compare_op=ALU.is_equal, fill=0.0,
        base=-64, pattern=[[0, 64]], channel_multiplier=1), inc=True)

    ld_x = None
    for c_ in range(DC):
        ld_x = p.load(vx[c_], d_xf[c_ * 128:(c_ + 1) * 128, :])
    ld_wuq = p.load(arD[:, 0:DC * QL].rearrange("p (c m) -> p c m", m=QL), d_wuq.rearrange("(c p) m -> p c m", p=128))
    ld_wuk = p.load(arD[:, DC * QL:2 * DC * QL].rearrange("p (c m) -> p c m", m=KVL),
                    d_wuk.rearrange("(c p) m -> p c m", p=128))
    ld_woT = p.load(arB[:, 0:DC * C].rearrange("p (c m) -> p c m", m=C), d_woT.rearrange("(c p) m -> p c m", p=128))
    p.load(s_bias[:], d_biasr)
    p.load(s_tdw[:].rearrange("p (c m) -> p c m", m=32), d_dw.rearrange("(c p) m -> p c m", p=128))
    ld_small = p.load(s_da[0:1, 0:1], d_da)

    R_SUM, R_SQ = slice(0, 512), slice(512, 1024)

    ln_chain = {"prev": []}

    def ln_tile(src, dst, add=None, src_wait=(), tapname=None):
        gates = []
        prevs = ln_chain["prev"]
        m1 = None
        for c_ in range(DC):
            w = list(src_wait) + fw("pH") + [g_setup] if c_ == 0 else []
            m1 = mm(pH[0:1, :], s_ones[:], src[c_], c_ == 0, c_ == DC - 1,
                    inc=(c_ == DC - 1), wait=w)
        vtt(s_lnq[:], src[0], src[0], ALU.mult,
            wait=list(src_wait) + prevs)
        for c_ in range(1, DC):
            vtt(s_lnt[:], src[c_], src[c_], ALU.mult)
            vtt(s_lnq[:], s_lnq[:], s_lnt[:], ALU.add)
        vq = p.op("v", lambda h, S: h.tensor_scalar(
            out=s_lnq[:], in0=s_lnq[:], scalar1=1.0, scalar2=None,
            op0=ALU.mult), inc=True)
        m2 = mm(pY[0][0:1, :], s_ones[:], s_lnq[:], True, True, inc=True,
                wait=[vq] + fw("pYL"))
        c1 = vcp(s_rows[0:1, R_SUM], pH[0:1, :], inc=True, wait=[m1])
        c2 = vcp(s_rows[0:1, R_SQ], pY[0][0:1, :], inc=True, wait=[m2])
        warm()
        p.drain("v")
        vm = vts(s_rows[0:1, R_SUM], s_rows[0:1, R_SUM], 1.0 / C, None, ALU.mult,
                 inc=True)
        vts(s_rows[0:1, R_SQ], s_rows[0:1, R_SQ], 1.0 / C, None, ALU.mult)
        p.drain("v")
        vtt(s_lns[0:1, 0:512], s_rows[0:1, R_SUM], s_rows[0:1, R_SUM], ALU.mult)
        p.drain("v")
        vv = vtt(s_rows[0:1, R_SQ], s_rows[0:1, R_SQ], s_lns[0:1, 0:512],
                 ALU.subtract, inc=True)
        sq = sact(s_rows[0:1, R_SQ], s_rows[0:1, R_SQ], AF.Sqrt,
                  bias=s_eps[0:1, :], inc=True, wait=[vv])
        rr = vrecip(s_rows[0:1, R_SQ], s_rows[0:1, R_SQ], inc=True, wait=[sq])
        if tapname is not None and f"{tapname}_rows" in taps:
            gates.append(p.store(
                nc.dram_tensor(f"tap_{tapname}_rows", [1, 1024], F32,
                               kind="ExternalOutput").ap(), s_rows[0:1, :],
                wait=[rr]))
        b1 = mm(pH[:, :], s_onesr[0:1, 0:128], s_rows[0:1, R_SQ], True, True,
                inc=True, wait=[rr, c1])
        b2 = mm(pY[0][:, :], s_onesr[0:1, 0:128], s_rows[0:1, R_SUM], True, True,
                inc=True, wait=[vm, c2])
        warm()
        e1 = vcp(s_lnt[:], pH[:, :], inc=True, wait=[b1])
        e2 = vcp(s_lns[:], pY[0][:, :], inc=True, wait=[b2])
        mark("pH", e1)
        mark("pYL", e2)
        vlast = None
        for c_ in range(DC):
            vtt(dst[c_], src[c_], s_lns[:], ALU.subtract,
                wait=[e1, e2] if c_ == 0 else [])
            if add is None:
                vlast = vtt(dst[c_], dst[c_], s_lnt[:], ALU.mult,
                            inc=(c_ == DC - 1))
            else:
                vtt(dst[c_], dst[c_], s_lnt[:], ALU.mult)
                vlast = vtt(dst[c_], dst[c_], add[c_], ALU.add,
                            inc=(c_ == DC - 1))
        ln_chain["prev"] = [vlast]
        marks = [vlast]
        if gates:
            marks += gates
        return marks

    def ln_full(srcv, dstv, addv=None, src_wait=(), tapname=None):
        last = None
        for q5 in range(NQ5):
            sl = slice(q5 * 512, (q5 + 1) * 512)
            last = ln_tile([s[:, sl] for s in srcv], [d[:, sl] for d in dstv],
                           add=None if addv is None else [a[:, sl] for a in addv],
                           src_wait=src_wait if q5 == 0 else last,
                           tapname=(tapname if q5 == 0 else
                                    (f"{tapname}s3" if (tapname and q5 == 3) else None)))
        return last

    ln_h1 = ln_full(vx, vx, src_wait=[ld_x], tapname="h1")

    pg_i = [0]

    def proj(out_views, out_w, lhs_tiles, rhs_tiles, nk, first_wait,
             m_range, n_range, lhs_sel, rhs_sel, out_sel):
        done = None
        for mi in m_range:
            for nj in n_range:
                for kk in range(nk):
                    w = []
                    if kk == 0:
                        w = list(fw(f"PQ{pg_i[0]}"))
                        if mi == m_range[0] and nj == n_range[0]:
                            w += list(first_wait)
                    mmn = mm(pG[pg_i[0]][:, 0:out_w], lhs_sel(mi, kk),
                             rhs_sel(nj, kk), kk == 0, kk == nk - 1,
                             inc=(kk == nk - 1), wait=w)
                done = vcp(out_sel(mi, nj), pG[pg_i[0]][:, 0:out_w], inc=True,
                           wait=[mmn])
                mark(f"PQ{pg_i[0]}", done)
                pg_i[0] ^= 1
        return done

    b_done = proj(None, 512, None, None, DC, [ld_wuq, ld_wuk],
                  list(range(MC)), [0],
                  lambda mi, kk: wuq_v[kk][:, mi * 128:(mi + 1) * 128],
                  lambda nj, kk: wuk_v[kk],
                  lambda mi, nj: vB[mi])
    ld_wdq = p.load(arD[:, 0:MC * C].rearrange("p (c m) -> p c m", m=C), d_wdq.rearrange("(c p) m -> p c m", p=128),
                    wait=[b_done])
    kT_done = proj(None, 512, None, None, MC, [ld_wdq],
                   list(range(LC)), list(range(2)),
                   lambda mi, kk: vB[kk][:, mi * 128:(mi + 1) * 128],
                   lambda nj, kk: wdq_v[kk][:, nj * 512:(nj + 1) * 512],
                   lambda mi, nj: vkT[mi][:, nj * 512:(nj + 1) * 512])
    ld_wuv = p.load(arD[:, MC * C:MC * C + DC * KVL].rearrange("p (c m) -> p c m", m=KVL),
                    d_wuv.rearrange("(c p) m -> p c m", p=128), wait=[kT_done])
    ve_done = proj(None, 512, None, None, DC, [ld_wuv, ld_woT],
                   list(range(LC)), list(range(2)),
                   lambda mi, kk: wuv_v[kk][:, mi * 128:(mi + 1) * 128],
                   lambda nj, kk: woT_v[kk][:, nj * 512:(nj + 1) * 512],
                   lambda mi, nj: vve[mi][:, nj * 512:(nj + 1) * 512])
    ld_wdkvT = p.load(arD[:, 0:DC * KVL].rearrange("p (c m) -> p c m", m=KVL),
                      d_wdkvT.rearrange("(c p) m -> p c m", p=128),
                      wait=[kT_done])
    ckv_done = proj(None, 512, None, None, DC, [ld_wdkvT] + ln_h1 + [ve_done],
                    list(range(LC)), list(range(4)),
                    lambda mi, kk: wdkvT_v[kk][:, mi * 128:(mi + 1) * 128],
                    lambda nj, kk: vx[kk][:, nj * 512:(nj + 1) * 512],
                    lambda mi, nj: vckv[mi][:, nj * 512:(nj + 1) * 512])
    ld_wdqT = p.load(arD[:, DC * KVL:2 * DC * KVL].rearrange("p (c m) -> p c m", m=QL),
                     d_wdqT.rearrange("(c p) m -> p c m", p=128), wait=[ve_done])
    qmid_done = proj(None, 512, None, None, DC, [ld_wdqT],
                     list(range(MC)), list(range(4)),
                     lambda mi, kk: wdqT_v[kk][:, mi * 128:(mi + 1) * 128],
                     lambda nj, kk: vx[kk][:, nj * 512:(nj + 1) * 512],
                     lambda mi, nj: vqmid[mi][:, nj * 512:(nj + 1) * 512])
    ld_wuqT = p.load(arA[:, 4 * T + NTK * 260: 4 * T + NTK * 260 + MC * C].rearrange("p (c m) -> p c m", m=C),
                     d_wuqT.rearrange("(c p) m -> p c m", p=128),
                     wait=[ckv_done, qmid_done])

    tap("h1", vx[0], 128, T, wait=ln_h1)
    tap("ckv", vckv[0], 128, T, wait=[ckv_done])
    tap("keffT", vkT[0], 128, C, wait=[kT_done])
    tap("veff", vve[0], 128, C, wait=[ve_done])
    tap("qmid", vqmid[0], 128, T, wait=[qmid_done])

    t1 = sact(s_da[0:1, 1:2], s_da[0:1, 0:1], AF.Tanh, scale=4.0, inc=True,
              wait=[ld_small] + ln_h1)
    p.drain("s")
    t2 = sact(s_rows[0:1, 0:256], s_da[0:1, 1:2].broadcast_to([1, 256]), AF.Exp,
              scale=0.5 * LN2, inc=True)
    p.drain("s")
    t3 = sact(s_rows[0:1, 256:512], s_da[0:1, 1:2].broadcast_to([1, 256]), AF.Exp,
              scale=-0.5 * LN2, inc=True)
    bd1 = mm(pH[:, 0:256], s_onesr[0:1, 0:128], s_rows[0:1, 0:256], True, True,
             inc=True, wait=[t2] + fw("pH"))
    cd1 = vcp(s_tda[:, 0:1], pH[:, 0:1], inc=True, wait=[bd1])
    bd2 = mm(pH[:, 0:256], s_onesr[0:1, 0:128], s_rows[0:1, 256:512], True, True,
             inc=True, wait=[t3, cd1])
    cd2 = vcp(s_tda[:, 1:2], pH[:, 1:2], inc=True, wait=[bd2])
    mark("pH", cd2)

    gp(lambda h, S: h.memset(s_mka[:], 0.0), wait=[qmid_done, ckv_done])
    gp(lambda h, S: h.affine_select(
        out=s_mka[:], in_=s_mka[:], compare_op=ALU.is_ge, fill=NEG,
        base=0, pattern=[[1, 256]], channel_multiplier=-1))
    gp(lambda h, S: h.memset(s_mkb[:], 0.0))
    gp(lambda h, S: h.affine_select(
        out=s_mkb[:], in_=s_mkb[:], compare_op=ALU.is_ge, fill=NEG,
        base=-128, pattern=[[1, 256]], channel_multiplier=-1))
    rrow_init = gp(lambda h, S: h.memset(s_rrow[:], 0.0), inc=True)
    pend_evac = [None]
    pt_free = [None, None, None, None]
    ps_free = [None, None]
    py_free = [None, None]
    yst_free = [None, None]
    yraw_free = [None, None]
    pb_free = [rrow_init]
    last_store = [None]

    for g_ in range(NG):
        kg_done = proj(None, 512, None, None, LC,
                       [kT_done, ckv_done] if g_ == 0 else [],
                       list(range(2)), list(range(4)),
                       lambda mi, kk, g=g_: vkT[kk][:, 256 * g + 128 * mi:
                                                    256 * g + 128 * (mi + 1)],
                       lambda nj, kk: vckv[kk][:, nj * 512:(nj + 1) * 512],
                       lambda mi, nj: vK[mi][:, nj * 512:(nj + 1) * 512])
        for lc_ in range(LC):
            vmemset(vvaug[lc_], 0.0,
                    wait=[qmid_done] if g_ == 0 and lc_ == 0 else [])
        va_done = None
        for lc_ in range(LC):
            for j in range(4):
                va_done = vcp(vvaug[lc_][:, 65 * j:65 * j + 64],
                              vve[lc_][:, 64 * (4 * g_ + j):64 * (4 * g_ + j) + 64],
                              inc=(lc_ == LC - 1 and j == 3),
                              wait=[ve_done] if (g_ == 0 and lc_ == 0 and j == 0)
                              else [])
        vg_done = proj(None, 260, None, None, LC, [va_done],
                       list(range(NTK)), [0],
                       lambda mi, kk: vckv[kk][:, mi * 128:(mi + 1) * 128],
                       lambda nj, kk: vvaug[kk],
                       lambda mi, nj: vV[mi])
        for j in range(4):
            vg_done = vmemset(vV3[:, :, 65 * j + 64:65 * j + 65], 1.0,
                              inc=(j == 3))
        qg_done = proj(None, 512, None, None, MC, [ld_wuqT, qmid_done],
                       list(range(2)), list(range(4)),
                       lambda mi, kk, g=g_: vwuqT[kk][:, 256 * g + 128 * mi:
                                                      256 * g + 128 * (mi + 1)],
                       lambda nj, kk: vqmid[kk][:, nj * 512:(nj + 1) * 512],
                       lambda mi, nj: vqg[mi][:, nj * 512:(nj + 1) * 512])
        if g_ == 0:
            tap("K0", vK[0], 128, T, wait=[kg_done])
            tap("V0", arA[:, 2 * T:2 * T + 2048], 128, 2048, wait=[vg_done])
            tap("q0", vqg[0], 128, T, wait=[qg_done])

        for pr in range(2):
            for qt in range(NQT):
                kcmax = 2 * qt + 1
                ymm = [None, None]

                def emit_S(kc, pr=pr, qt=qt):
                    out = [None, None]
                    for idx in range(2):
                        j = 2 * pr + idx
                        jc, jo = j // 2, 64 * (j % 2)
                        si = (kc % 2) * 2 + idx
                        w = [kg_done, qg_done] + fw(f"PQ{si}")
                        smm = mm(PQ[si][:, 0:256],
                                 vK[jc][jo:jo + 64, kc * 128:(kc + 1) * 128],
                                 vqg[jc][jo:jo + 64, qt * 256:(qt + 1) * 256],
                                 True, True, inc=True, wait=w)
                        out[idx] = smm
                    if kc >= 2 * qt:
                        msk = s_mka if kc == 2 * qt else s_mkb
                        for idx in range(2):
                            si = (kc % 2) * 2 + idx
                            out[idx] = vtt(PQ[si][:, 0:256], PQ[si][:, 0:256],
                                           msk, ALU.add, inc=True,
                                           wait=[out[idx], rrow_init])
                    return out

                sprods = emit_S(0)
                if pend_evac[0] is not None:
                    pend_evac[0]()
                    pend_evac[0] = None
                for kc in range(kcmax + 1):
                    exs = [None, None]
                    for idx in range(2):
                        si = (kc % 2) * 2 + idx
                        pi = (kc % 2) * 2 + idx
                        w = [sprods[idx]] + ([pt_free[pi]] if pt_free[pi] else [])
                        exs[idx] = sact(s_pt[:, pi * 256:(pi + 1) * 256],
                                        PQ[si][:, 0:256], AF.Exp, scale=0.125,
                                        inc=True, wait=w)
                        mark(f"PQ{si}", exs[idx])
                    if kc < kcmax:
                        sprods = emit_S(kc + 1)
                    for idx in range(2):
                        j = 2 * pr + idx
                        pi = (kc % 2) * 2 + idx
                        w = [exs[idx], vg_done]
                        if kc == 0 and py_free[idx] is not None:
                            w.append(py_free[idx])
                        ymm[idx] = mm(pY[idx][0:65, 0:256],
                                      vV[kc][:, 65 * j:65 * j + 65],
                                      s_pt[:, pi * 256:(pi + 1) * 256],
                                      kc == 0, kc == kcmax, inc=True, wait=w)
                        pt_free[pi] = ymm[idx]

                def evac(pr=pr, qt=qt, ymm=ymm, g_=g_):
                    for idx in range(2):
                        j = 2 * pr + idx
                        h_ = 4 * g_ + j
                        ys = idx
                        w = [ymm[idx]] + ([yraw_free[ys]] if yraw_free[ys] else [])
                        cpy = vcp(s_yraw[0:65, 256 * ys:256 * (ys + 1)],
                                  pY[idx][0:65, 0:256], inc=True, wait=w)
                        rc = vrecip(s_rrow[64:65, :], pY[idx][64:65, 0:256],
                                    inc=True,
                                    wait=[ymm[idx]] + ([pb_free[0]] if pb_free[0] else []))
                        py_free[idx] = rc
                        bc = mm(pB[0:64, :], s_e64[:, :], s_rrow[:, :], True,
                                True, inc=True, wait=[rc])
                        w = [bc, cpy] + ([yst_free[ys]] if yst_free[ys] else [])
                        yn = vtt(s_yst[0:64, 256 * ys:256 * (ys + 1)],
                                 s_yraw[0:64, 256 * ys:256 * (ys + 1)],
                                 pB[0:64, :], ALU.mult, inc=True, wait=w)
                        pb_free[0] = yn
                        yraw_free[ys] = yn
                        last_store[0] = p.store(
                            d_y[64 * h_:64 * (h_ + 1), 256 * qt:256 * (qt + 1)],
                            s_yst[0:64, 256 * ys:256 * (ys + 1)], wait=[yn])
                        yst_free[ys] = last_store[0]
                pend_evac[0] = evac
    pend_evac[0]()
    pend_evac[0] = None

    dw1 = sact(s_rdw[:], s_tdw[:], AF.Tanh, scale=4.0, inc=True,
               wait=ln_h1 + [ld_small])
    p.drain("s")
    dw2 = sact(s_tdw[:], s_rdw[:], AF.Exp, scale=0.5 * LN2, inc=True)
    p.drain("s")
    dw3 = sact(s_rdw[:], s_rdw[:], AF.Exp, scale=-0.5 * LN2, inc=True)

    def cascade(am_ap, out_ap, base_k, terms, wait0):
        n = len(terms) + 1
        vts(ck_sl[0], am_ap, 0.0, base_k, ALU.is_gt, ALU.mult, wait=wait0)
        for i, (thr, wgt) in enumerate(terms):
            vts(ck_sl[i + 1], am_ap, thr, wgt, ALU.is_ge, ALU.mult)
        p.drain("v")
        width = n
        while width > 1:
            half = width // 2
            for i in range(half):
                vtt(ck_sl[i], ck_sl[i], ck_sl[width - 1 - i], ALU.add)
            width = width - half
            p.drain("v")
        return vts(out_ap, ck_sl[0], 1.0, None, ALU.add, inc=True)

    def quant_batch4(src4, dst4, tw_ap, rw_ap, wait0, krange=(KMIN, KMAX)):
        am4 = vred_absmax(
            s_am[:].rearrange("p (t b) -> p t b", b=32).unsqueeze(3),
            src4.rearrange("p (t b e) -> p t b e", b=32, e=32),
            inc=True, wait=wait0)
        p.drain("v")
        kmin, kmax = krange
        terms = [(2.0 ** k, 2.0 ** (k - 3)) for k in range(kmin + 1, kmax + 1)]
        c1 = cascade(s_am[:], s_sc[:], 2.0 ** (kmin - 2) - 1.0, terms, [am4])
        p.drain("v")
        vrecip(s_rsc[:], s_sc[:])
        p.drain("v")
        if tw_ap.shape[1] == 1:
            vts(s_sc[:], s_sc[:], tw_ap, None, ALU.mult, wait=[cd1, cd2])
            vts(s_rsc[:], s_rsc[:], rw_ap, None, ALU.mult)
        else:
            vtt(s_sc[:], s_sc[:], tw_ap, ALU.mult, wait=[dw2, dw3])
            vtt(s_rsc[:], s_rsc[:], rw_ap, ALU.mult)
        p.drain("v")
        xsm = vtt(src4.rearrange("p (t b e) -> p t b e", b=32, e=32),
                  src4.rearrange("p (t b e) -> p t b e", b=32, e=32),
                  s_rsc[:].rearrange("p (t b) -> p t b", b=32).unsqueeze(3)
                  .broadcast_to([128, 4, 32, 32]), ALU.mult, inc=True)
        last = None
        for t_ in range(4):
            xs = src4[:, t_ * 1024:(t_ + 1) * 1024]
            xq = dst4[:, t_ * 1024:(t_ + 1) * 1024]
            w_s = [xsm] if t_ == 0 else [qsgn]
            sg1 = sact(tmp1[:], xs, AF.Sign, inc=True, wait=w_s)
            sg2 = sact(tmp2[:], xs, AF.Abs, inc=True)
            vts(tmp3[:], tmp2[:], 0.25, 0.5, ALU.is_gt, ALU.mult, wait=[sg2])
            vts(tmp4[:], tmp2[:], 0.75, 0.5, ALU.is_ge, ALU.mult)
            vtt(tmp3[:], tmp3[:], tmp4[:], ALU.add)
            vts(tmp4[:], tmp2[:], 1.25, 0.5, ALU.is_gt, ALU.mult)
            vtt(tmp3[:], tmp3[:], tmp4[:], ALU.add)
            vts(tmp4[:], tmp2[:], 1.75, 0.5, ALU.is_ge, ALU.mult)
            vtt(tmp3[:], tmp3[:], tmp4[:], ALU.add)
            vts(tmp4[:], tmp2[:], 2.5, 1.0, ALU.is_gt, ALU.mult)
            vtt(tmp3[:], tmp3[:], tmp4[:], ALU.add)
            vts(tmp4[:], tmp2[:], 3.5, 1.0, ALU.is_ge, ALU.mult)
            vtt(tmp3[:], tmp3[:], tmp4[:], ALU.add)
            vts(tmp4[:], tmp2[:], 5.0, 2.0, ALU.is_gt, ALU.mult)
            vtt(tmp3[:], tmp3[:], tmp4[:], ALU.add)
            warm()
            qsgn = vtt(xq, tmp3[:], tmp1[:], ALU.mult, inc=True)
            last = vtt(xq.rearrange("p (b e) -> p b e", e=32),
                       xq.rearrange("p (b e) -> p b e", e=32),
                       s_sc[:, t_ * 32:(t_ + 1) * 32].unsqueeze(2)
                       .broadcast_to([128, 32, 32]), ALU.mult,
                       inc=(t_ == 3))
        return last

    att_end = [pb_free[0], last_store[0]]
    wq_done = None
    u_done = []
    for q5 in range(NQ5):
        sl = slice(q5 * 512, (q5 + 1) * 512)
        w = att_end if q5 == 0 else u_done
        lx = ly = None
        for c_ in range(DC):
            lx = p.load(vxs[c_], d_xf[c_ * 128:(c_ + 1) * 128, sl], wait=w)
            ly = p.load(vys[c_], d_y[c_ * 128:(c_ + 1) * 128, sl])
            w = []
        u_done = ln_tile(vys, [u[:, sl] for u in vu], add=vxs,
                         src_wait=[lx, ly])
        if q5 < 2:
            wb = q5
            lw = None
            for i in range(4):
                oc = wb * 4 + i
                wl = att_end if (wb == 0 and i == 0) else (
                    [wq_done] if (wb == 1 and i == 0) else [])
                lw = p.load(qw_src[i], d_wffn[oc * 128:(oc + 1) * 128, :],
                            wait=wl)
            qd = quant_batch4(arD[:, 0:4096], arD[:, 0:4096],
                              s_tdw[:, wb * 128:(wb + 1) * 128],
                              s_rdw[:, wb * 128:(wb + 1) * 128],
                              [lw, dw2, dw3], krange=(-11, 1))
            if wb == 0:
                tap("wq0", arD[:, 0:1024], 128, 1024, wait=[qd])
                tap("wam", s_am[:], 128, 128, wait=[qd])
                tap("wsc", s_sc[:], 128, 128, wait=[qd])
                tap("wrsc", s_rsc[:], 128, 128, wait=[qd])
                tap("wtdw", s_tdw[:, 0:128], 128, 128, wait=[qd])
            for i in range(4):
                oc = wb * 4 + i
                for dc_ in range(DC):
                    w2 = [qd] if (i == 0 and dc_ == 0) else []
                    w2 += fw(f"PQ{pg_i[0]}")
                    tm = mm(pG[pg_i[0]][:, 0:256],
                            arD[:, i * 1024 + dc_ * 128: i * 1024 + (dc_ + 1) * 128],
                            s_id2[:], True, True, inc=True, wait=w2)
                    wq_done = scp(vwqT[dc_][:, oc * 128:(oc + 1) * 128],
                                  pG[pg_i[0]][:, 0:128], inc=True, wait=[tm])
                    mark(f"PQ{pg_i[0]}", wq_done)
                    pg_i[0] ^= 1
    tap("u", vu[0], 128, T, wait=u_done)
    tap("wqT", vwqT[0], 128, C, wait=[wq_done])

    out_stores = []
    tpq = [2]
    xq_marks = {}
    st_marks = {}

    def emit_transp(src_fn, dst_fn, wait_first):
        done = None
        for t_ in range(4):
            for dc_ in range(DC):
                w = list(wait_first) if (t_ == 0 and dc_ == 0) else []
                i = tpq[0]
                w += fw(f"PQ{i}")
                tm = mm(PQ[i][:, 0:256], src_fn(t_, dc_), s_id2[:],
                        True, True, inc=True, wait=w)
                done = scp(dst_fn(t_, dc_), PQ[i][:, 0:128], inc=True, wait=[tm])
                mark(f"PQ{i}", done)
                tpq[0] = 2 + ((tpq[0] - 2 + 1) % 2)
        return done

    def emit_ffn(q5):
        xsl = slice((q5 % 2) * 512, (q5 % 2) * 512 + 512)
        vfa = vfab[q5 % 2]
        f_done = None
        for oc in range(DC):
            for dc_ in range(DC):
                w = []
                if dc_ == 0:
                    w = list(fw(f"PQ{pg_i[0]}"))
                    if oc == 0:
                        w += [xq_marks[q5], wq_done]
                        if q5 - 2 in st_marks:
                            w.append(st_marks[q5 - 2])
                mmn = mm(PQ[pg_i[0]][:, :], vwqT[dc_][:, oc * 128:(oc + 1) * 128],
                         vxq2[dc_][:, xsl], dc_ == 0, dc_ == DC - 1,
                         inc=(dc_ == DC - 1), wait=w)
            w2 = [mmn]
            if oc == 0 and q5 - 2 in st_marks:
                w2.append(st_marks[q5 - 2])
            f_done = sact(vfa[oc], PQ[pg_i[0]][:, :], AF.Identity,
                          bias=s_bias[:, oc:oc + 1], inc=True, wait=w2)
            mark(f"PQ{pg_i[0]}", f_done)
            pg_i[0] ^= 1
        return f_done

    def emit_lnf(q5, f_done):
        sl = slice(q5 * 512, (q5 + 1) * 512)
        vfa = vfab[q5 % 2]
        c4 = ln_tile(vfa, vfa, add=[u[:, sl] for u in vu], src_wait=[f_done])
        if q5 == 0:
            tap("f0", vfa[0], 128, 512, wait=c4)
        last = None
        for c_ in range(DC):
            last = p.store(d_out[c_ * 128:(c_ + 1) * 128, sl], vfa[c_],
                           wait=c4)
            out_stores.append(last)
        st_marks[q5] = last
        return c4

    def emit_quant_stage(q5, w0):
        sl = slice(q5 * 512, (q5 + 1) * 512)
        h2_done = ln_tile([u[:, sl] for u in vu], qw_h2f, src_wait=w0)
        tp_done = emit_transp(
            lambda t_, dc_: qw_h2f[dc_][:, t_ * 128:(t_ + 1) * 128],
            lambda t_, dc_: arD[:, t_ * 1024 + dc_ * 128: t_ * 1024 + (dc_ + 1) * 128],
            h2_done)
        return tp_done

    f_prev = None
    for q5 in range(NQ5):
        sl = slice(q5 * 512, (q5 + 1) * 512)
        w0 = ([wq_done] + u_done) if q5 == 0 else []
        tp_done = emit_quant_stage(q5, w0)
        if q5 >= 1:
            f_prev = emit_ffn(q5 - 1)
        qd = quant_batch4(arD[:, 0:4096], arD[:, 0:4096],
                          s_tda[:, 0:1], s_tda[:, 1:2], [tp_done],
                          krange=(-6, 4))
        xq_marks[q5] = emit_transp(
            lambda t_, dc_: arD[:, t_ * 1024 + dc_ * 128: t_ * 1024 + (dc_ + 1) * 128],
            lambda t_, dc_, q=q5: vxq2[dc_][:, (q % 2) * 512 + t_ * 128:
                                            (q % 2) * 512 + (t_ + 1) * 128],
            [qd])
        if q5 >= 1:
            emit_lnf(q5 - 1, f_prev)
    f_prev = emit_ffn(NQ5 - 1)
    emit_lnf(NQ5 - 1, f_prev)

    p.ops.append(("y", lambda h, S, n=p.dcnt["st"] * 16: h.wait_ge(S["st"], n)))

    S = {}
    S["ld"] = nc.alloc_semaphore("sem_ld")
    S["st"] = nc.alloc_semaphore("sem_st")
    for e in ("t", "v", "s", "g"):
        S[e] = nc.alloc_semaphore(f"sem_{e}")

    with nc.Block() as blk:
        @blk.tensor
        def _(eng):
            for e, fn in p.ops:
                if e == "t":
                    fn(eng, S)

        @blk.vector
        def _(eng):
            for e, fn in p.ops:
                if e == "v":
                    fn(eng, S)

        @blk.scalar
        def _(eng):
            for e, fn in p.ops:
                if e == "s":
                    fn(eng, S)

        @blk.gpsimd
        def _(eng):
            for e, fn in p.ops:
                if e == "g":
                    fn(eng, S)

        @blk.sync
        def _(eng):
            for e, fn in p.ops:
                if e == "y":
                    fn(eng, S)

    return tap_aps


def _in_map(x_b, W_dq, W_uq, W_dkv, W_uk, W_uv, W_o, Wf, bias, dw, da):
    cc = np.ascontiguousarray
    return {
        "xf": cc(x_b.T),
        "wdqT": cc(W_dq.T),
        "wuqT": cc(W_uq.T),
        "wdkvT": cc(W_dkv.T),
        "wuq": cc(W_uq),
        "wuk": cc(W_uk),
        "wuv": cc(W_uv),
        "wdq": cc(W_dq),
        "woT": cc(W_o.T),
        "wffn": cc(Wf),
        "biasr": cc(bias.reshape(DC, 128).T),
        "dw": cc(dw),
        "da": cc(da.reshape(1, 1)),
    }


def kernel(L0_attn_x, L0_attn_W_dq, L0_attn_W_uq, L0_attn_W_dkv, L0_attn_W_uk,
           L0_attn_W_uv, L0_attn_W_o, L0_ffn_weight, L0_ffn_bias,
           L0_ffn_delta_w, L0_ffn_delta_act):
    x = np.asarray(L0_attn_x, dtype=np.float32)
    args = [np.asarray(a, dtype=np.float32) for a in
            (L0_attn_W_dq, L0_attn_W_uq, L0_attn_W_dkv, L0_attn_W_uk,
             L0_attn_W_uv, L0_attn_W_o, L0_ffn_weight, L0_ffn_bias,
             L0_ffn_delta_w, L0_ffn_delta_act)]
    nc = bass.Bass("TRN2", target_bir_lowering=False, debug=False)
    build(nc, taps=tuple(DEBUG_TAPS))
    in_maps = [_in_map(x[b], *args) for b in range(8)]
    res = run_bass_kernel_spmd(nc, in_maps, core_ids=list(range(8)))
    out = np.stack([np.ascontiguousarray(r["outf"].T) for r in res.results])
    if DEBUG_TAPS:
        kernel.last_results = res
    return out
